# version 02
# baseline (speedup 1.0000x reference)
from contextlib import ExitStack

import numpy as np
import ml_dtypes

import concourse.bass as bass
import concourse.mybir as mybir
from concourse.bass_utils import run_bass_kernel_spmd

F32 = mybir.dt.float32
BF16 = mybir.dt.bfloat16
AF = mybir.ActivationFunctionType
ALU = mybir.AluOpType
AX = mybir.AxisListType

T, D = 4096, 1024
DT = 8
T1 = 512
NT1 = T1 // 128
NCH = T // T1
E, F = 8, 2048
NFT = F // 128
LAT = 256
NH, HD = 16, 64
LN_EPS = 1e-5
RMS_EPS = 1e-6
EPS9 = 1e-9

ENGS = ("sync", "tensor", "vector", "scalar", "gpsimd")


class Prog:

    def __init__(self, nc, stack):
        self.nc = nc
        self.q = None
        self.sem = {}
        for name in ("dW", "dG", "tE", "vE", "sE"):
            h = stack.enter_context(nc.semaphore(name))
            self.sem[name] = [h, 0]

    def begin(self):
        self.q = {e: [] for e in ENGS}

    def flush(self):
        with self.nc.Block() as blk:
            for e in ENGS:
                ops = self.q[e]
                if ops:
                    def runner(eng, ops=ops):
                        for f in ops:
                            f(eng)
                    getattr(blk, e)(runner)
        self.q = None

    def emit(self, eng, fn):
        self.q[eng].append(fn)

    def tok(self, eng, sem, k=1):
        s = self.sem[sem]
        s[1] += k
        v, h = s[1], s[0]
        prev = self.q[eng][-1]
        self.q[eng][-1] = lambda eng_, p=prev, h=h, k=k: p(eng_).then_inc(h, k)
        return v

    def wait(self, eng, sem, val):
        if val is None or val <= 0:
            return
        h = self.sem[sem][0]
        self.q[eng].append(lambda eng_, h=h, v=val: eng_.wait_ge(h, v))

    def cur(self, sem):
        return self.sem[sem][1]

    OWN = {"vector": "vE", "scalar": "sE", "tensor": "tE"}

    def selfw(self, eng):
        t = self.tok(eng, self.OWN[eng])
        self.wait(eng, self.OWN[eng], t)
        return t



def emit_ffn_sublayer(P, B, src_ap, dst_ap):
    src4 = src_ap.rearrange("(c tt p) d -> c p tt d", tt=NT1, p=128)
    dst4 = dst_ap.rearrange("(c tt p) d -> c p tt d", tt=NT1, p=128)

    xst, zC, sqst, znbf, znT = B["xst"], B["zC"], B["sqst"], B["znbf"], B["znT"]
    hsT, w1sb, w2sb = B["hsT"], B["w1sb"], B["w2sb"]
    mu4, v4, rstd4, rt, cC = B["mu4"], B["v4"], B["rstd4"], B["rt"], B["cC"]
    normw_b, wmt_sb, ident = B["normw_b"], B["wmt_sb"], B["ident"]
    ph, py, ptr_, pr = B["ph"], B["py"], B["ptr"], B["pr"]
    w1b4, w2b4 = B["w1b4"], B["w2b4"]
    st = B["state"]
    x3 = xst[:].rearrange("p (tt d) -> p tt d", tt=NT1)
    z3 = zC[:].rearrange("p (tt d) -> p tt d", tt=NT1)
    zn3 = znbf[:].rearrange("p (tt d) -> p tt d", tt=NT1)
    znsc = sqst

    def pump_w2(upto):
        while st["next_w2"] <= upto:
            ei = st["next_w2"]
            st["next_w2"] += 1
            if st["first_sync_wait"] is not None:
                P.wait("sync", "dW", st["first_sync_wait"])
                st["first_sync_wait"] = None
            P.wait("sync", "tE", st["y_done"].get(ei - 2))
            e = st["expert_of"](ei)
            dstb = w2sb[ei % 2][:].rearrange("p (ft d) -> p ft d", ft=NFT)
            P.emit("sync", lambda eng, s=w2b4[e], d=dstb: eng.dma_start(d, s))
            st["w2_tok"][ei] = P.tok("sync", "dW", 16)

    def pump_w1(upto):
        while st["next_w1"] <= upto:
            g = st["next_w1"]
            st["next_w1"] += 1
            if st["first_sync_wait"] is not None:
                P.wait("sync", "dW", st["first_sync_wait"])
                st["first_sync_wait"] = None
            P.wait("sync", "tE", st["h_done"].get(g - 2))
            e = st["expert_of"](g // 2)
            half = g % 2
            src = w1b4[e][:, :, half * 1024:(half + 1) * 1024]
            dstb = w1sb[g % 2][:].rearrange("p (dt f) -> p dt f", dt=DT)
            P.emit("sync", lambda eng, s=src, d=dstb: eng.dma_start(d, s))
            st["w1_tok"][g] = P.tok("sync", "dW", 16)

    def stats(eps):
        P.wait("scalar", "vE", st["sq_free"])
        P.wait("scalar", "sE", st["zC_sc_tok"])
        P.emit("scalar", lambda eng: eng.activation(sqst[:], zC[:], AF.Square))
        t = P.tok("scalar", "sE")
        P.wait("vector", "sE", t)
        P.emit("vector", lambda eng: eng.tensor_reduce(
            v4[:], sqst[:].rearrange("p (tt d) -> p tt d", tt=NT1), AX.X, ALU.add))
        P.selfw("vector")
        P.emit("vector", lambda eng, eps=eps: eng.tensor_scalar(
            v4[:], v4[:], 1.0 / D, eps, ALU.mult, ALU.add))
        st["sq_free"] = P.tok("vector", "vE")
        P.wait("scalar", "vE", st["sq_free"])
        P.emit("scalar", lambda eng: eng.activation(rstd4[:], v4[:], AF.Sqrt))
        t = P.tok("scalar", "sE")
        P.wait("vector", "sE", t)
        P.emit("vector", lambda eng: eng.reciprocal(rstd4[:], rstd4[:]))
        return P.tok("vector", "vE")

    for C in range(NCH):
        P.begin()
        pump_w2(st["e_seen"] + 1)
        pump_w1(2 * st["e_seen"] + 1)

        P.wait("gpsimd", "dG", st["xst_store_tok"])
        P.wait("gpsimd", "dW", st["xst_guard_dW"])
        st["xst_guard_dW"] = None
        P.wait("gpsimd", "vE", st["xst_free_tok"])
        xv = xst[:].rearrange("p (tt d) -> p tt d", tt=NT1)
        P.emit("gpsimd", lambda eng, s=src4[C], d=xv: eng.dma_start(d, s))
        x_tok = P.tok("gpsimd", "dG", 16)

        P.wait("vector", "dG", x_tok)
        P.emit("vector", lambda eng: eng.tensor_reduce(mu4[:], x3, AX.X, ALU.add))
        P.selfw("vector")
        P.emit("vector", lambda eng: eng.tensor_scalar(
            mu4[:], mu4[:], 1.0 / D, None, ALU.mult))
        P.selfw("vector")
        P.wait("vector", "sE", st["zC_sc_tok"])
        for tt in range(NT1):
            P.emit("vector", lambda eng, tt=tt: eng.tensor_scalar(
                z3[:, tt], x3[:, tt], mu4[:, tt:tt + 1], None, ALU.subtract))
        xc_tok = P.tok("vector", "vE")
        P.wait("scalar", "vE", xc_tok)
        rs_tok = stats(LN_EPS)
        P.wait("scalar", "vE", rs_tok)
        for tt in range(NT1):
            P.emit("scalar", lambda eng, tt=tt: eng.activation(
                z3[:, tt], z3[:, tt], AF.Copy, scale=rstd4[:, tt:tt + 1]))
        st["zC_sc_tok"] = P.tok("scalar", "sE")
        if C == 0 and getattr(P, "debug", False):
            P.wait("gpsimd", "sE", st["zC_sc_tok"])
            P.dbg_dump("zC_ln", zC[:], (128, NT1 * D), F32)

        for L in range(2):
            if L == 1:
                P.wait("scalar", "vE", st["z_upd_tok"])
            rs_tok = stats(RMS_EPS)
            P.wait("scalar", "vE", rs_tok)
            for tt in range(NT1):
                P.wait("scalar", "vE", st["sq_free"])
                P.emit("scalar", lambda eng, tt=tt: eng.activation(
                    znsc[:, 0:D], z3[:, tt], AF.Copy, scale=rstd4[:, tt:tt + 1]))
                sc_tok = P.tok("scalar", "sE")
                st["zC_sc_tok"] = sc_tok
                P.wait("vector", "sE", sc_tok)
                if tt == 0:
                    P.wait("vector", "tE", st["znbf_rd_tok"])
                P.emit("vector", lambda eng, tt=tt: eng.tensor_tensor(
                    zn3[:, tt], znsc[:, 0:D], normw_b[:], ALU.mult))
                st["sq_free"] = P.tok("vector", "vE")
            znbf_tok = st["sq_free"]

            P.wait("tensor", "vE", znbf_tok)
            for dt in range(DT):
                P.wait("tensor", "vE", st["ptr_ev_tok"])
                for tt in range(NT1):
                    P.emit("tensor", lambda eng, dt=dt, tt=tt: eng.transpose(
                        ptr_[:, tt * 128:(tt + 1) * 128],
                        znbf[:, tt * D + dt * 128: tt * D + dt * 128 + 128],
                        ident[:]))
                tr_tok = P.tok("tensor", "tE")
                if dt == DT - 1:
                    st["znbf_rd_tok"] = tr_tok
                P.wait("vector", "tE", tr_tok)
                if dt == 0:
                    P.wait("vector", "tE", st["znT_rd_tok"])
                P.emit("vector", lambda eng, dt=dt: eng.tensor_copy(
                    znT[:, dt * T1:(dt + 1) * T1], ptr_[:]))
                st["ptr_ev_tok"] = P.tok("vector", "vE")
            znT_tok = st["ptr_ev_tok"]
            if C == 0 and L == 0 and getattr(P, "debug", False):
                P.wait("gpsimd", "vE", znT_tok)
                P.dbg_dump("znT0", znT[:], (128, DT * T1), BF16)

            P.wait("tensor", "vE", znT_tok)
            P.wait("tensor", "vE", st["pr_ev_tok"])
            for tt in range(NT1):
                for dt in range(DT):
                    P.emit("tensor", lambda eng, tt=tt, dt=dt: eng.matmul(
                        pr[:, tt * 12:(tt + 1) * 12],
                        znT[:, dt * T1 + tt * 128: dt * T1 + (tt + 1) * 128],
                        wmt_sb[:, dt * 12:(dt + 1) * 12],
                        start=(dt == 0), stop=(dt == DT - 1),
                        skip_group_check=True))
            route_mm_tok = P.tok("tensor", "tE")
            P.wait("vector", "tE", route_mm_tok)
            P.emit("vector", lambda eng: eng.tensor_copy(rt[:], pr[:]))
            st["pr_ev_tok"] = P.tok("vector", "vE")
            route_tok = _emit_routing(P, B)
            st["route_tok"] = route_tok
            if C == 0 and L == 0 and getattr(P, "debug", False):
                P.wait("gpsimd", "vE", route_tok)
                P.dbg_dump("rt0", rt[:], (128, NT1 * 12), F32)
                P.dbg_dump("cC0", cC[:], (128, NT1 * 8), F32)

            for e in range(E):
                ei = st["e_seen"]
                st["e_seen"] += 1
                st["expert_map"][ei] = e
                hb = ei % 2
                pump_w2(ei + 1)
                for ft in range(NFT):
                    g = 2 * ei + ft // 8
                    pb = st["ph_i"] % 2
                    st["ph_i"] += 1
                    P.wait("tensor", "sE", st["silu_tok"].get(st["ph_i"] - 1 - 2))
                    if ft == 0 or ft == 8:
                        P.wait("tensor", "dW", st["w1_tok"][g])
                    for dt in range(DT):
                        P.emit("tensor",
                               lambda eng, pb=pb, gb=g % 2, ft=ft, dt=dt: eng.matmul(
                                   ph[pb][:],
                                   w1sb[gb][:, dt * 1024 + (ft % 8) * 128:
                                            dt * 1024 + (ft % 8) * 128 + 128],
                                   znT[:, dt * T1:(dt + 1) * T1],
                                   start=(dt == 0), stop=(dt == DT - 1),
                                   skip_group_check=True))
                    h_tok = P.tok("tensor", "tE")
                    if ft == 7 or ft == 15:
                        st["h_done"][g] = h_tok
                    if ft == NFT - 1:
                        st["znT_rd_tok"] = h_tok
                    P.wait("scalar", "tE", h_tok)
                    if ft == 0:
                        P.wait("scalar", "tE", st["y_done"].get(ei - 2))
                    P.emit("scalar", lambda eng, hb=hb, ft=ft, pb=pb: eng.activation(
                        hsT[hb][:, ft * T1:(ft + 1) * T1], ph[pb][:], AF.Silu))
                    st["silu_tok"][st["ph_i"] - 1] = P.tok("scalar", "sE")
                silu_done = st["silu_tok"][st["ph_i"] - 1]
                pump_w1(2 * (ei + 1) + 1)

                P.wait("tensor", "sE", silu_done)
                P.wait("tensor", "dW", st["w2_tok"][ei])
                for tt in range(NT1):
                    pb = st["py_i"] % 2
                    st["py_i"] += 1
                    P.wait("tensor", "vE", st["stt_tok"].get(st["py_i"] - 1 - 2))
                    for ft in range(NFT):
                        for h2 in range(2):
                            P.emit("tensor",
                                   lambda eng, pb=pb, hb=hb, tt=tt, ft=ft, h2=h2,
                                   eb=ei % 2: eng.matmul(
                                       py[pb][:, h2 * 512:(h2 + 1) * 512],
                                       hsT[hb][:, ft * T1 + tt * 128:
                                               ft * T1 + (tt + 1) * 128],
                                       w2sb[eb][:, ft * 1024 + h2 * 512:
                                                ft * 1024 + h2 * 512 + 512],
                                       start=(ft == 0), stop=(ft == NFT - 1),
                                       skip_group_check=True))
                    y_tok = P.tok("tensor", "tE")
                    if tt == NT1 - 1:
                        st["y_done"][ei] = y_tok
                    P.wait("vector", "tE", y_tok)
                    P.wait("vector", "vE",
                           st["stt_tok"].get(st["py_i"] - 1 - NT1))
                    if e == 0 and tt == 0:
                        P.wait("vector", "vE", st["route_tok"])
                    P.emit("vector",
                           lambda eng, tt=tt, pb=pb, e=e: eng.scalar_tensor_tensor(
                               z3[:, tt], py[pb][:],
                               cC[:, tt * 8 + e: tt * 8 + e + 1],
                               z3[:, tt], ALU.mult, ALU.add))
                    st["stt_tok"][st["py_i"] - 1] = P.tok("vector", "vE")
            st["z_upd_tok"] = st["stt_tok"][st["py_i"] - 1]
            if C == 0 and L == 0 and getattr(P, "debug", False):
                P.wait("gpsimd", "vE", st["z_upd_tok"])
                P.dbg_dump("zC_L0", zC[:], (128, NT1 * D), F32)
                P.dbg_dump("hsT_e7", hsT[(st["e_seen"] - 1) % 2][:],
                           (128, NFT * T1), BF16)

        P.wait("vector", "vE", st["z_upd_tok"])
        for tt in range(NT1):
            P.emit("vector", lambda eng, tt=tt: eng.scalar_tensor_tensor(
                x3[:, tt], z3[:, tt], 0.5, x3[:, tt], ALU.mult, ALU.add))
        out_tok = P.tok("vector", "vE")
        st["xst_free_tok"] = out_tok
        P.wait("gpsimd", "vE", out_tok)
        P.emit("gpsimd", lambda eng, d=dst4[C], s=xv: eng.dma_start(d, s))
        st["xst_store_tok"] = P.tok("gpsimd", "dG", 16)
        P.flush()


def _emit_routing(P, B):
    rt, cC = B["rt"], B["cC"]
    mm_, me, sc1 = B["r_max"], B["r_me"], B["r_sc1"]
    cnt, tmp, ev, od, lv = B["r_cnt"], B["r_tmp"], B["r_ev"], B["r_od"], B["r_lv"]
    xa, xb = B["r_xa"], B["r_xb"]
    rt3 = rt[:].rearrange("p (t c) -> p t c", c=12)
    me3 = me[:].rearrange("p (t c) -> p t c", c=4)
    ev3 = ev[:].rearrange("p (t c) -> p t c", c=4)
    od3 = od[:].rearrange("p (t c) -> p t c", c=4)
    cnt3 = cnt[:].rearrange("p (t c) -> p t c", c=4)
    tmp3 = tmp[:].rearrange("p (t c) -> p t c", c=4)
    lv3 = lv[:].rearrange("p (t c) -> p t c", c=4)
    cC3 = cC[:].rearrange("p (t c) -> p t c", c=8)

    def VS(fn):
        P.emit("vector", fn)
        P.selfw("vector")

    VS(lambda eng: eng.tensor_reduce(mm_[:], rt3[:, :, 0:4], AX.X, ALU.max))
    for tt in range(NT1):
        VS(lambda eng, tt=tt: eng.tensor_scalar(
            me3[:, tt], rt3[:, tt, 0:4], mm_[:, tt:tt + 1], None, ALU.subtract))
    t = P.cur("vE")
    P.wait("scalar", "vE", t)
    P.emit("scalar", lambda eng: eng.activation(xa[:], me[:], AF.Exp))
    t = P.tok("scalar", "sE")
    P.wait("vector", "sE", t)
    VS(lambda eng: eng.tensor_copy(me[:], xa[:]))
    VS(lambda eng: eng.tensor_reduce(sc1[:], me3, AX.X, ALU.add))
    VS(lambda eng: eng.reciprocal(sc1[:], sc1[:]))
    for tt in range(NT1):
        VS(lambda eng, tt=tt: eng.tensor_scalar(
            me3[:, tt], me3[:, tt], sc1[:, tt:tt + 1], None, ALU.mult))
    VS(lambda eng: eng.memset(cnt[:], 0.0))
    for g in range(4):
        for gp in range(4):
            if gp == g:
                continue
            VS(lambda eng, g=g, gp=gp: eng.tensor_tensor(
                tmp3[:, :, 0:1], me3[:, :, gp:gp + 1], me3[:, :, g:g + 1],
                ALU.is_gt))
            VS(lambda eng, g=g: eng.tensor_tensor(
                cnt3[:, :, g:g + 1], cnt3[:, :, g:g + 1], tmp3[:, :, 0:1],
                ALU.add))
            if gp < g:
                VS(lambda eng, g=g, gp=gp: eng.tensor_tensor(
                    tmp3[:, :, 0:1], me3[:, :, gp:gp + 1], me3[:, :, g:g + 1],
                    ALU.is_equal))
                VS(lambda eng, g=g: eng.tensor_tensor(
                    cnt3[:, :, g:g + 1], cnt3[:, :, g:g + 1], tmp3[:, :, 0:1],
                    ALU.add))
    VS(lambda eng: eng.tensor_scalar(cnt[:], cnt[:], 2.0, None, ALU.is_lt))
    VS(lambda eng: eng.tensor_tensor(me[:], me[:], cnt[:], ALU.mult))
    VS(lambda eng: eng.tensor_reduce(sc1[:], me3, AX.X, ALU.add))
    VS(lambda eng: eng.tensor_scalar(sc1[:], sc1[:], EPS9, None, ALU.add))
    VS(lambda eng: eng.reciprocal(sc1[:], sc1[:]))
    for tt in range(NT1):
        VS(lambda eng, tt=tt: eng.tensor_scalar(
            me3[:, tt], me3[:, tt], sc1[:, tt:tt + 1], None, ALU.mult))
    if getattr(P, "debug", False) and not P.dbg_tensors.get("r_me_post"):
        P.dbg_dump("r_me_post", me[:], (128, NT1 * 4), F32)
        P.dbg_dump("r_cnt_post", cnt[:], (128, NT1 * 4), F32)
        P.dbg_dump("r_sc1_post", sc1[:], (128, NT1), F32)
    VS(lambda eng: eng.tensor_tensor(ev3, rt3[:, :, 4:12:2], rt3[:, :, 5:12:2],
                                     ALU.max))
    VS(lambda eng: eng.tensor_tensor(od3, rt3[:, :, 5:12:2], ev3, ALU.subtract))
    VS(lambda eng: eng.tensor_tensor(ev3, rt3[:, :, 4:12:2], ev3, ALU.subtract))
    t = P.cur("vE")
    P.wait("scalar", "vE", t)
    P.emit("scalar", lambda eng: eng.activation(xa[:], ev[:], AF.Exp))
    P.selfw("scalar")
    P.emit("scalar", lambda eng: eng.activation(xb[:], od[:], AF.Exp))
    t = P.tok("scalar", "sE")
    P.wait("vector", "sE", t)
    VS(lambda eng: eng.tensor_copy(ev[:], xa[:]))
    VS(lambda eng: eng.tensor_copy(od[:], xb[:]))
    VS(lambda eng: eng.tensor_tensor(tmp[:], ev[:], od[:], ALU.add))
    VS(lambda eng: eng.reciprocal(tmp[:], tmp[:]))
    VS(lambda eng: eng.tensor_tensor(ev[:], ev[:], tmp[:], ALU.mult))
    VS(lambda eng: eng.tensor_tensor(od[:], od[:], tmp[:], ALU.mult))
    VS(lambda eng: eng.tensor_tensor(lv[:], ev[:], od[:], ALU.max))
    VS(lambda eng: eng.tensor_tensor(tmp[:], ev[:], od[:], ALU.is_ge))
    VS(lambda eng: eng.tensor_scalar(ev[:], lv[:], EPS9, None, ALU.add))
    VS(lambda eng: eng.reciprocal(ev[:], ev[:]))
    VS(lambda eng: eng.tensor_tensor(lv[:], lv[:], ev[:], ALU.mult))
    if getattr(P, "debug", False) and not P.dbg_tensors.get("r_lv_post"):
        P.dbg_dump("r_lv_post", lv[:], (128, NT1 * 4), F32)
        P.dbg_dump("r_ind_post", tmp[:], (128, NT1 * 4), F32)
    VS(lambda eng: eng.tensor_tensor(lv[:], lv[:], me[:], ALU.mult))
    VS(lambda eng: eng.tensor_tensor(cC3[:, :, 0:8:2], lv3, tmp3, ALU.mult))
    VS(lambda eng: eng.tensor_scalar(tmp[:], tmp[:], -1.0, 1.0, ALU.mult, ALU.add))
    VS(lambda eng: eng.tensor_tensor(cC3[:, :, 1:8:2], lv3, tmp3, ALU.mult))
    return P.cur("vE")




TCA = 256
NTA = TCA // 128
CHA = T // TCA


def emit_attention(P, nc, A):
    x1v = A["x1s"].rearrange("(c tt p) d -> c p tt d", tt=NTA, p=128)
    x1t = A["x1s"].rearrange("(tt p) d -> tt p d", p=128)
    x2t = A["x2s"].rearrange("(tt p) d -> tt p d", p=128)
    qtr, ktr = A["qtr"], A["ktr"]
    ident, ptm, e2sb = A["ident"], A["ptm"], A["e2sb"]

    def barrier():
        for e_ in ENGS:
            for s_ in ("vE", "sE", "tE", "dG", "dW"):
                P.wait(e_, s_, P.cur(s_))

    with ExitStack() as SO:
        EO = SO.enter_context
        v65 = EO(nc.sbuf_tensor("v65", [128, 32 * 1040], BF16))

        with ExitStack() as SA:
            EA = SA.enter_context
            qn2 = EA(nc.sbuf_tensor("qn2", [16, T], F32))
            Bsb = EA(nc.sbuf_tensor("Bsb", [16, T], F32))
            ones16 = EA(nc.sbuf_tensor("ones16", [16, T], BF16))
            km2 = EA(nc.sbuf_tensor("km2", [2, 8], F32))
            kmx = EA(nc.sbuf_tensor("kmx", [16, 1], F32))
            kmx2 = EA(nc.sbuf_tensor("kmx2", [16, 1], F32))
            st2 = [EA(nc.sbuf_tensor("stg2_%d" % i, [2, 512], F32))
                   for i in range(2)]
            ast = EA(nc.sbuf_tensor("ast", [128, NTA * D], F32))
            asq = EA(nc.sbuf_tensor("asq", [128, NTA * D], F32))
            ybf = EA(nc.sbuf_tensor("ybf", [128, NTA * D], BF16))
            yT = EA(nc.sbuf_tensor("yT", [128, DT * TCA], BF16))
            amu = EA(nc.sbuf_tensor("amu", [128, NTA], F32))
            av4 = EA(nc.sbuf_tensor("av4", [128, NTA], F32))
            ars = EA(nc.sbuf_tensor("ars", [128, NTA], F32))
            lst = EA(nc.sbuf_tensor("lst", [128, LAT], F32))
            lsc = EA(nc.sbuf_tensor("lsc", [128, LAT], F32))
            lnbf = EA(nc.sbuf_tensor("lnbf", [128, LAT], BF16))
            lnT = EA(nc.sbuf_tensor("lnT", [128, 2 * TCA], BF16))
            lm1 = EA(nc.sbuf_tensor("lm1", [128, 1], F32))
            lv1 = EA(nc.sbuf_tensor("lv1", [128, 1], F32))
            lr1 = EA(nc.sbuf_tensor("lr1", [128, 1], F32))
            wq_sb = EA(nc.sbuf_tensor("wq_sb", [128, DT * D], BF16))
            wdkv_sb = EA(nc.sbuf_tensor("wdkv_sb", [128, DT * LAT], BF16))
            wuk_sb = EA(nc.sbuf_tensor("wuk_sb", [128, 2 * D], BF16))
            wuv_sb = EA(nc.sbuf_tensor("wuv_sb", [128, 2 * D], BF16))
            lng_b = EA(nc.sbuf_tensor("lng_b", [128, LAT], F32))
            lnb_b = EA(nc.sbuf_tensor("lnb_b", [128, LAT], F32))
            csq = EA(nc.sbuf_tensor("csq", [128, TCA], F32))
            snq = EA(nc.sbuf_tensor("snq", [128, TCA], F32))
            csk = EA(nc.sbuf_tensor("csk", [128, TCA], F32))
            snk = EA(nc.sbuf_tensor("snk", [128, TCA], F32))
            qtf = EA(nc.sbuf_tensor("qtf", [128, TCA], F32))
            qtb = EA(nc.sbuf_tensor("qtb", [128, TCA], BF16))
            rt1 = EA(nc.sbuf_tensor("rt1", [128, TCA], F32))
            rt2 = EA(nc.sbuf_tensor("rt2", [128, TCA], F32))
            qrf = EA(nc.sbuf_tensor("qrf", [128, TCA], F32))
            qrb = EA(nc.sbuf_tensor("qrb", [128, TCA], BF16))
            sqb = EA(nc.sbuf_tensor("sqb", [128, TCA], BF16))
            pT = EA(nc.psum_tensor("pTa", [128, 512], BF16))
            pL = EA(nc.psum_tensor("pLa", [128, LAT], F32))
            pQ = EA(nc.psum_tensor("pQa", [128, TCA], F32))
            pR = EA(nc.psum_tensor("pRa", [128, TCA], F32))
            pN = EA(nc.psum_tensor("pNa", [2, TCA], F32))
            pV = EA(nc.psum_tensor("pVa", [128, 1024], F32))

            P.begin()
            barrier()
            for dst_, srcm, nt_ in ((wq_sb, A["wq"], DT), (wdkv_sb, A["wdkv"], DT),
                                    (wuk_sb, A["wuk"], 2), (wuv_sb, A["wuv"], 2)):
                cc = dst_.shape[1] // nt_
                P.emit("gpsimd", lambda eng, d=dst_, s=srcm, nt=nt_, cc=cc:
                       eng.dma_start(
                           d[:].rearrange("p (t c) -> p t c", t=nt),
                           s.rearrange("(t p) c -> p t c", p=128)))
                P.tok("gpsimd", "dG", 16)
            P.emit("gpsimd", lambda eng: eng.dma_start(lng_b[:], A["lng"]))
            P.tok("gpsimd", "dG", 16)
            P.emit("gpsimd", lambda eng: eng.dma_start(lnb_b[:], A["lnb"]))
            wload_tok = P.tok("gpsimd", "dG", 16)
            P.emit("vector", lambda eng: eng.memset(v65[:], 1.0))
            v65_tok = P.tok("vector", "vE")
            P.emit("vector", lambda eng: eng.memset(km2[:], 0.0))
            P.selfw("vector")
            P.flush()

            tk = dict(ast_free=None, cs_rd=None, ln_done=None, ybf_rd=None,
                      yT_rd=None, lnT_rd=None, pT_ev=None, lat_ev=None,
                      lsc_rd=None, pQ_free=None, rot_rd=None, qtf_rd=None,
                      pN_ev=None, stg_ev=[None, None],
                      stg_free=[None, None], vout=v65_tok, sqb_rd=None)
            sti = [0]

            for C in range(CHA):
                P.begin()
                cix = slice(C * TCA, (C + 1) * TCA)
                P.wait("gpsimd", "sE", tk["ast_free"])
                P.emit("gpsimd", lambda eng, s=x1v[C],
                       d=ast[:].rearrange("p (tt d) -> p tt d", tt=NTA):
                       eng.dma_start(d, s))
                P.tok("gpsimd", "dG", 16)
                P.wait("gpsimd", "vE", tk["cs_rd"])
                for dst_, src_ in ((csq, A["cosq"]), (snq, A["sinq"]),
                                   (csk, A["cosk"]), (snk, A["sink"])):
                    P.emit("gpsimd", lambda eng, d=dst_, s=src_[:, cix]:
                           eng.dma_start(d[:], s))
                    P.tok("gpsimd", "dG", 16)
                ld_tok = P.cur("dG")

                x3 = ast[:].rearrange("p (tt d) -> p tt d", tt=NTA)
                a3 = asq[:].rearrange("p (tt d) -> p tt d", tt=NTA)
                P.wait("vector", "dG", ld_tok)
                P.emit("vector", lambda eng: eng.tensor_reduce(
                    amu[:], x3, AX.X, ALU.add))
                P.selfw("vector")
                P.emit("vector", lambda eng: eng.tensor_scalar(
                    amu[:], amu[:], 1.0 / D, None, ALU.mult))
                P.selfw("vector")
                P.wait("vector", "sE", tk["ln_done"])
                for tt in range(NTA):
                    P.emit("vector", lambda eng, tt=tt: eng.tensor_scalar(
                        a3[:, tt], x3[:, tt], amu[:, tt:tt + 1], None,
                        ALU.subtract))
                xc_tok = P.tok("vector", "vE")
                P.wait("scalar", "vE", xc_tok)
                P.emit("scalar", lambda eng: eng.activation(ast[:], asq[:],
                                                            AF.Square))
                sq_tok = P.tok("scalar", "sE")
                tk["ast_free"] = sq_tok
                P.wait("vector", "sE", sq_tok)
                P.emit("vector", lambda eng: eng.tensor_reduce(
                    av4[:], x3, AX.X, ALU.add))
                P.selfw("vector")
                P.emit("vector", lambda eng: eng.tensor_scalar(
                    av4[:], av4[:], 1.0 / D, LN_EPS, ALU.mult, ALU.add))
                t = P.tok("vector", "vE")
                P.wait("scalar", "vE", t)
                P.emit("scalar", lambda eng: eng.activation(ars[:], av4[:],
                                                            AF.Sqrt))
                t = P.tok("scalar", "sE")
                P.wait("vector", "sE", t)
                P.emit("vector", lambda eng: eng.reciprocal(ars[:], ars[:]))
                rs_tok = P.tok("vector", "vE")
                P.wait("scalar", "vE", rs_tok)
                P.wait("scalar", "tE", tk["ybf_rd"])
                for tt in range(NTA):
                    P.emit("scalar", lambda eng, tt=tt: eng.activation(
                        ybf[:, tt * D:(tt + 1) * D], a3[:, tt], AF.Copy,
                        scale=ars[:, tt:tt + 1]))
                ybf_tok = P.tok("scalar", "sE")
                tk["ln_done"] = ybf_tok

                P.wait("tensor", "sE", ybf_tok)
                for dt in range(DT):
                    P.wait("tensor", "vE", tk["pT_ev"])
                    for tt in range(NTA):
                        P.emit("tensor", lambda eng, dt=dt, tt=tt: eng.transpose(
                            pT[:, tt * 128:(tt + 1) * 128],
                            ybf[:, tt * D + dt * 128: tt * D + dt * 128 + 128],
                            ident[:]))
                    tr_tok = P.tok("tensor", "tE")
                    if dt == DT - 1:
                        tk["ybf_rd"] = tr_tok
                    P.wait("vector", "tE", tr_tok)
                    if dt == 0:
                        P.wait("vector", "tE", tk["yT_rd"])
                    P.emit("vector", lambda eng, dt=dt: eng.tensor_copy(
                        yT[:, dt * TCA:(dt + 1) * TCA], pT[:, 0:NTA * 128]))
                    tk["pT_ev"] = P.tok("vector", "vE")
                yT_tok = tk["pT_ev"]

                P.wait("tensor", "vE", yT_tok)
                P.wait("tensor", "dG", wload_tok)
                for tt in range(NTA):
                    P.wait("tensor", "vE", tk["lat_ev"])
                    for dt in range(DT):
                        P.emit("tensor", lambda eng, tt=tt, dt=dt: eng.matmul(
                            pL[:],
                            yT[:, dt * TCA + tt * 128: dt * TCA + (tt + 1) * 128],
                            wdkv_sb[:, dt * LAT:(dt + 1) * LAT],
                            start=(dt == 0), stop=(dt == DT - 1),
                            skip_group_check=True))
                    lm_tok = P.tok("tensor", "tE")
                    P.wait("vector", "tE", lm_tok)
                    P.wait("vector", "sE", tk["lsc_rd"])
                    P.emit("vector", lambda eng: eng.tensor_copy(lst[:], pL[:]))
                    tk["lat_ev"] = P.tok("vector", "vE")
                    P.wait("vector", "vE", tk["lat_ev"])
                    P.emit("vector", lambda eng: eng.tensor_reduce(
                        lm1[:], lst[:].rearrange("p (o l) -> p o l", o=1),
                        AX.X, ALU.add))
                    P.selfw("vector")
                    P.emit("vector", lambda eng: eng.tensor_scalar(
                        lm1[:], lm1[:], 1.0 / LAT, None, ALU.mult))
                    P.selfw("vector")
                    P.emit("vector", lambda eng: eng.tensor_scalar(
                        lsc[:], lst[:], lm1[:, 0:1], None, ALU.subtract))
                    t = P.tok("vector", "vE")
                    P.wait("scalar", "vE", t)
                    P.emit("scalar", lambda eng: eng.activation(lst[:], lsc[:],
                                                                AF.Square))
                    t = P.tok("scalar", "sE")
                    P.wait("vector", "sE", t)
                    P.emit("vector", lambda eng: eng.tensor_reduce(
                        lv1[:], lst[:].rearrange("p (o l) -> p o l", o=1),
                        AX.X, ALU.add))
                    P.selfw("vector")
                    P.emit("vector", lambda eng: eng.tensor_scalar(
                        lv1[:], lv1[:], 1.0 / LAT, LN_EPS, ALU.mult, ALU.add))
                    t = P.tok("vector", "vE")
                    P.wait("scalar", "vE", t)
                    P.emit("scalar", lambda eng: eng.activation(lr1[:], lv1[:],
                                                                AF.Sqrt))
                    t = P.tok("scalar", "sE")
                    P.wait("vector", "sE", t)
                    P.emit("vector", lambda eng: eng.reciprocal(lr1[:], lr1[:]))
                    t = P.tok("vector", "vE")
                    P.wait("scalar", "vE", t)
                    P.emit("scalar", lambda eng: eng.activation(
                        lst[:], lsc[:], AF.Copy, scale=lr1[:, 0:1]))
                    tk["lsc_rd"] = P.tok("scalar", "sE")
                    P.wait("vector", "sE", tk["lsc_rd"])
                    P.emit("vector", lambda eng: eng.tensor_tensor(
                        lsc[:], lst[:], lng_b[:], ALU.mult))
                    P.selfw("vector")
                    if tt == 0:
                        P.wait("vector", "tE", tk["lnT_rd"])
                    P.emit("vector", lambda eng: eng.tensor_tensor(
                        lnbf[:], lsc[:], lnb_b[:], ALU.add))
                    lnbf_tok = P.tok("vector", "vE")
                    P.wait("tensor", "vE", lnbf_tok)
                    P.wait("tensor", "vE", tk["pT_ev"])
                    for lt in range(2):
                        P.emit("tensor", lambda eng, lt=lt: eng.transpose(
                            pT[:, lt * 128:(lt + 1) * 128],
                            lnbf[:, lt * 128:(lt + 1) * 128], ident[:]))
                    t = P.tok("tensor", "tE")
                    P.wait("vector", "tE", t)
                    for lt in range(2):
                        P.emit("vector", lambda eng, lt=lt, tt=tt: eng.tensor_copy(
                            lnT[:, lt * TCA + tt * 128: lt * TCA + (tt + 1) * 128],
                            pT[:, lt * 128:(lt + 1) * 128]))
                    tk["pT_ev"] = P.tok("vector", "vE")
                lnT_tok = tk["pT_ev"]
                P.wait("tensor", "vE", lnT_tok)

                def qk_side(wsb, nacc, rhsT, cosb, sinb, scr, is_q, C=C):
                    for ot in range(8):
                        P.wait("tensor", "sE", tk["pQ_free"])
                        for at in range(nacc):
                            P.emit("tensor", lambda eng, ot=ot, at=at, w=wsb,
                                   r=rhsT: eng.matmul(
                                       pQ[:],
                                       w[:, at * D + ot * 128:
                                         at * D + ot * 128 + 128],
                                       r[:, at * TCA:(at + 1) * TCA],
                                       start=(at == 0), stop=(at == nacc - 1),
                                       skip_group_check=True))
                        mm_tok = P.tok("tensor", "tE")
                        P.wait("scalar", "tE", mm_tok)
                        P.wait("scalar", "vE", tk["qtf_rd"])
                        P.emit("scalar", lambda eng: eng.copy(qtf[:], pQ[:]))
                        ev_tok = P.tok("scalar", "sE")
                        tk["pQ_free"] = ev_tok
                        P.wait("vector", "sE", ev_tok)
                        P.emit("vector", lambda eng: eng.tensor_copy(qtb[:],
                                                                     qtf[:]))
                        cb_tok = P.tok("vector", "vE")
                        P.wait("tensor", "vE", cb_tok)
                        P.wait("tensor", "vE", tk["rot_rd"])
                        P.emit("tensor", lambda eng: eng.matmul(
                            pR[:], ptm[:], qtb[:],
                            start=True, stop=True, skip_group_check=True))
                        rot_tok = P.tok("tensor", "tE")
                        P.wait("vector", "tE", rot_tok)
                        P.emit("vector", lambda eng, cb=cosb: eng.tensor_tensor(
                            rt1[:], qtf[:], cb[:], ALU.mult))
                        P.selfw("vector")
                        P.emit("vector", lambda eng, sb=sinb: eng.tensor_tensor(
                            rt2[:], pR[:], sb[:], ALU.mult))
                        tk["rot_rd"] = P.tok("vector", "vE")
                        tk["qtf_rd"] = tk["rot_rd"]
                        P.wait("vector", "vE", tk["rot_rd"])
                        P.wait("vector", "sE", tk["sqb_rd"])
                        P.emit("vector", lambda eng: eng.tensor_tensor(
                            qrf[:], rt1[:], rt2[:], ALU.add))
                        P.selfw("vector")
                        P.emit("vector", lambda eng: eng.tensor_copy(qrb[:],
                                                                     qrf[:]))
                        qr_tok = P.tok("vector", "vE")
                        P.wait("scalar", "vE", qr_tok)
                        P.wait("scalar", "tE", tk.get("pN_mm"))
                        P.emit("scalar", lambda eng: eng.activation(
                            sqb[:], qrf[:], AF.Square))
                        sq2_tok = P.tok("scalar", "sE")
                        tk["sqb_rd"] = sq2_tok
                        P.wait("tensor", "sE", sq2_tok)
                        P.wait("tensor", "vE", tk["pN_ev"])
                        P.emit("tensor", lambda eng: eng.matmul(
                            pN[:], e2sb[:], sqb[:],
                            start=True, stop=True, skip_group_check=True))
                        n_tok = P.tok("tensor", "tE")
                        tk["pN_mm"] = n_tok
                        sb_ = sti[0] % 2
                        sti[0] += 1
                        P.wait("vector", "tE", n_tok)
                        if tk["stg_free"][sb_] is not None:
                            P.wait("vector", *tk["stg_free"][sb_])
                        P.emit("vector", lambda eng, sb=sb_: eng.tensor_copy(
                            st2[sb][:, 0:TCA], pN[:]))
                        tk["stg_ev"][sb_] = P.tok("vector", "vE")
                        tk["pN_ev"] = tk["stg_ev"][sb_]
                        P.wait("sync", "vE", qr_tok)
                        for hh in range(2):
                            P.emit("sync", lambda eng, ot=ot, s=scr, hh=hh:
                                   eng.dma_start(
                                       s[2 * ot + hh, 0:64,
                                         C * TCA:(C + 1) * TCA],
                                       qrb[hh * 64:(hh + 1) * 64, :]))
                            P.tok("sync", "dW", 16)
                        if is_q:
                            P.wait("gpsimd", "vE", tk["stg_ev"][sb_])
                            P.emit("gpsimd", lambda eng, sb=sb_, ot=ot:
                                   eng.dma_start(
                                       qn2[2 * ot:2 * ot + 2,
                                           C * TCA:(C + 1) * TCA],
                                       st2[sb][:, 0:TCA]))
                            tk["stg_free"][sb_] = ("dG",
                                                   P.tok("gpsimd", "dG", 16))
                        else:
                            P.wait("vector", "vE", tk["stg_ev"][sb_])
                            P.emit("vector", lambda eng, sb=sb_:
                                   eng.tensor_reduce(
                                       st2[sb][:, 300:301], st2[sb][:, 0:TCA],
                                       AX.X, ALU.max))
                            P.selfw("vector")
                            P.emit("vector", lambda eng, sb=sb_, ot=ot:
                                   eng.tensor_tensor(
                                       km2[:, ot:ot + 1], km2[:, ot:ot + 1],
                                       st2[sb][:, 300:301], ALU.max))
                            P.selfw("vector")
                            tk["stg_free"][sb_] = ("vE", P.cur("vE"))

                qk_side(wq_sb, DT, yT, csq, snq, qtr, True)
                qk_side(wuk_sb, 2, lnT, csk, snk, ktr, False)

                for tt in range(NTA):
                    P.wait("tensor", "vE", tk["vout"])
                    for h2 in range(2):
                        for lt in range(2):
                            P.emit("tensor", lambda eng, tt=tt, h2=h2, lt=lt:
                                   eng.matmul(
                                       pV[:, h2 * 512:(h2 + 1) * 512],
                                       lnT[:, lt * TCA + tt * 128:
                                           lt * TCA + (tt + 1) * 128],
                                       wuv_sb[:, lt * D + h2 * 512:
                                              lt * D + h2 * 512 + 512],
                                       start=(lt == 0), stop=(lt == 1),
                                       skip_group_check=True))
                    v_tok = P.tok("tensor", "tE")
                    P.wait("vector", "tE", v_tok)
                    tg = C * NTA + tt
                    for h in range(NH):
                        P.emit("vector", lambda eng, h=h, tg=tg: eng.tensor_copy(
                            v65[:, tg * 1040 + 65 * h: tg * 1040 + 65 * h + 64],
                            pV[:, h * 64:(h + 1) * 64]))
                    tk["vout"] = P.tok("vector", "vE")
                tk["yT_rd"] = P.cur("tE")
                tk["lnT_rd"] = P.cur("tE")
                tk["cs_rd"] = P.cur("vE")
                P.flush()

            P.begin()
            barrier()
            P.emit("gpsimd", lambda eng: eng.dma_start(A["kmxs"], km2[:]))
            t = P.tok("gpsimd", "dG", 16)
            P.wait("gpsimd", "dG", t)
            P.emit("gpsimd", lambda eng: eng.dma_start(
                kmx[:, 0:1], A["kmxs"].rearrange("c p -> (c p)")[:, None]))
            t = P.tok("gpsimd", "dG", 16)
            P.wait("scalar", "dG", t)
            P.emit("scalar", lambda eng: eng.activation(kmx2[:], kmx[:], AF.Sqrt))
            P.selfw("scalar")
            P.emit("scalar", lambda eng: eng.activation(Bsb[:], qn2[:], AF.Sqrt))
            b_tok = P.tok("scalar", "sE")
            P.wait("vector", "sE", b_tok)
            P.emit("vector", lambda eng: eng.tensor_scalar(
                Bsb[:], Bsb[:], kmx2[:, 0:1], -1.0, ALU.mult, ALU.mult))
            t = P.tok("vector", "vE")
            P.wait("gpsimd", "vE", t)
            P.emit("gpsimd", lambda eng: eng.dma_start(qtr[:, 64, :], Bsb[:]))
            P.tok("gpsimd", "dG", 16)
            P.emit("vector", lambda eng: eng.memset(ones16[:], 1.0))
            t = P.tok("vector", "vE")
            P.wait("sync", "vE", t)
            P.emit("sync", lambda eng: eng.dma_start(ktr[:, 64, :], ones16[:]))
            P.tok("sync", "dW", 16)
            P.flush()

        avT = EO(nc.sbuf_tensor("avT", [128, DT * T], BF16))
        with ExitStack() as SB:
            EB = SB.enter_context
            ktb = [EB(nc.sbuf_tensor("ktb%d" % i, [65, T], BF16))
                   for i in range(2)]
            qhb = [EB(nc.sbuf_tensor("qhb%d" % i, [65, T], BF16))
                   for i in range(2)]
            expt = [EB(nc.sbuf_tensor("expt%d" % i, [128, 512], BF16))
                    for i in range(4)]
            tms = EB(nc.sbuf_tensor("tms", [128, 4 * 512], F32))
            recb = EB(nc.sbuf_tensor("recb", [64, 512], F32))
            rec1 = EB(nc.sbuf_tensor("rec1", [1, 512], F32))
            on64 = EB(nc.sbuf_tensor("on64", [1, 64], F32))
            pS = [EB(nc.psum_tensor("pS%d" % i, [128, 512], F32))
                  for i in range(2)]
            pAV = [EB(nc.psum_tensor("pAV%d" % i, [65, 512], F32))
                   for i in range(2)]
            pBC = EB(nc.psum_tensor("pBC", [64, 512], F32))

            P.begin()
            barrier()
            P.emit("gpsimd", lambda eng: eng.dma_start(
                tms[:].rearrange("p (m c) -> p m c", m=4),
                A["trimask"].rearrange("m p c -> p m c")))
            P.tok("gpsimd", "dG", 16)
            P.emit("gpsimd", lambda eng: eng.dma_start(on64[:], A["ones64"]))
            msk_tok = P.tok("gpsimd", "dG", 16)
            P.wait("vector", "dG", msk_tok)
            P.wait("tensor", "dG", msk_tok)
            P.flush()

            bt = dict(hload=[None, None], hdone=[None, None],
                      exp={}, mm2={}, pav_ev=[None, None], bc_done=None,
                      recb_ev=None)
            jg = [0]
            for h in range(NH):
                P.begin()
                hb = h % 2
                P.wait("sync", "tE", bt["hdone"][hb])
                P.emit("sync", lambda eng, h=h, hb=hb: eng.dma_start(
                    ktb[hb][:], ktr[h]))
                P.tok("sync", "dW", 16)
                P.emit("sync", lambda eng, h=h, hb=hb: eng.dma_start(
                    qhb[hb][:], qtr[h]))
                bt["hload"][hb] = P.tok("sync", "dW", 16)
                P.wait("tensor", "dW", bt["hload"][hb])
                for tqc in range(8):
                    pv = tqc % 2
                    P.wait("tensor", "vE", bt["pav_ev"][pv])
                    njt = 4 * tqc + 4
                    for j in range(njt):
                        g = jg[0]
                        jg[0] += 1
                        ps = g % 2
                        eb = g % 4
                        P.wait("tensor", "sE", bt["exp"].get(g - 2))
                        P.emit("tensor", lambda eng, hb=hb, j=j, tqc=tqc, ps=ps:
                               eng.matmul(
                                   pS[ps][:],
                                   ktb[hb][:, j * 128:(j + 1) * 128],
                                   qhb[hb][:, tqc * 512:(tqc + 1) * 512],
                                   start=True, stop=True,
                                   skip_group_check=True))
                        s_tok = P.tok("tensor", "tE")
                        m = j - 4 * tqc
                        if m >= 0:
                            P.wait("vector", "tE", s_tok)
                            P.emit("vector", lambda eng, ps=ps, m=m:
                                   eng.tensor_tensor(
                                       pS[ps][:], pS[ps][:],
                                       tms[:, m * 512:(m + 1) * 512], ALU.add))
                            mk_tok = P.tok("vector", "vE")
                            P.wait("scalar", "vE", mk_tok)
                        else:
                            P.wait("scalar", "tE", s_tok)
                        P.wait("scalar", "tE", bt["mm2"].get(g - 4))
                        P.emit("scalar", lambda eng, ps=ps, eb=eb:
                               eng.activation(expt[eb][:], pS[ps][:], AF.Exp))
                        bt["exp"][g] = P.tok("scalar", "sE")
                        P.wait("tensor", "sE", bt["exp"][g])
                        P.emit("tensor", lambda eng, j=j, eb=eb, pv=pv, h=h,
                               njt=njt: eng.matmul(
                                   pAV[pv][0:65, :],
                                   v65[:, j * 1040 + 65 * h:
                                       j * 1040 + 65 * h + 65],
                                   expt[eb][:],
                                   start=(j == 0), stop=(j == njt - 1),
                                   skip_group_check=True))
                        bt["mm2"][g] = P.tok("tensor", "tE")
                    av_tok = bt["mm2"][jg[0] - 1]
                    P.wait("vector", "tE", av_tok)
                    P.wait("vector", "tE", bt["bc_done"])
                    P.emit("vector", lambda eng, pv=pv: eng.reciprocal(
                        rec1[:], pAV[pv][64:65, :]))
                    r_tok = P.tok("vector", "vE")
                    P.wait("tensor", "vE", r_tok)
                    P.emit("tensor", lambda eng: eng.matmul(
                        pBC[:], on64[:], rec1[:],
                        start=True, stop=True, skip_group_check=True))
                    bc_tok = P.tok("tensor", "tE")
                    bt["bc_done"] = bc_tok
                    P.wait("vector", "tE", bc_tok)
                    P.wait("vector", "vE", bt["recb_ev"])
                    P.emit("vector", lambda eng: eng.tensor_copy(recb[:], pBC[:]))
                    P.selfw("vector")
                    P.emit("vector", lambda eng, h=h, tqc=tqc, pv=pv:
                           eng.tensor_tensor(
                               avT[(h % 2) * 64:(h % 2) * 64 + 64,
                                   (h // 2) * T + tqc * 512:
                                   (h // 2) * T + (tqc + 1) * 512],
                               pAV[pv][0:64, :], recb[:], ALU.mult))
                    bt["pav_ev"][pv] = P.tok("vector", "vE")
                    bt["recb_ev"] = bt["pav_ev"][pv]
                bt["hdone"][hb] = P.cur("tE")
                P.flush()

        with ExitStack() as SC:
            ECp = SC.enter_context
            wo_sb = ECp(nc.sbuf_tensor("wo_sb", [128, DT * D], BF16))
            x1b = [ECp(nc.sbuf_tensor("x1b%d" % i, [128, D], F32))
                   for i in range(2)]
            x2st = [ECp(nc.sbuf_tensor("x2st%d" % i, [128, D], F32))
                    for i in range(2)]
            pF = [ECp(nc.psum_tensor("pF%d" % i, [128, 1024], F32))
                  for i in range(2)]
            P.begin()
            barrier()
            P.emit("gpsimd", lambda eng: eng.dma_start(
                wo_sb[:].rearrange("p (dt o) -> p dt o", dt=DT),
                A["wo"].rearrange("(dt p) o -> p dt o", p=128)))
            wo_tok = P.tok("gpsimd", "dG", 16)
            P.wait("tensor", "dG", wo_tok)
            ct = dict(out=[None, None], store=[None, None], xload=[None, None])
            for tt in range(32):
                b = tt % 2
                P.wait("gpsimd", "vE", ct["out"][b])
                P.emit("gpsimd", lambda eng, tt=tt, b=b: eng.dma_start(
                    x1b[b][:], x1t[tt]))
                ct["xload"][b] = P.tok("gpsimd", "dG", 16)
                P.wait("tensor", "vE", ct["out"][b])
                for dt in range(DT):
                    for h2 in range(2):
                        P.emit("tensor", lambda eng, tt=tt, dt=dt, h2=h2, b=b:
                               eng.matmul(
                                   pF[b][:, h2 * 512:(h2 + 1) * 512],
                                   avT[:, dt * T + tt * 128:
                                       dt * T + (tt + 1) * 128],
                                   wo_sb[:, dt * D + h2 * 512:
                                         dt * D + h2 * 512 + 512],
                                   start=(dt == 0), stop=(dt == DT - 1),
                                   skip_group_check=True))
                f_tok = P.tok("tensor", "tE")
                P.wait("vector", "tE", f_tok)
                P.wait("vector", "dG", ct["xload"][b])
                P.wait("vector", "dG", ct["store"][b])
                P.emit("vector", lambda eng, b=b: eng.tensor_tensor(
                    x2st[b][:], x1b[b][:], pF[b][:], ALU.add))
                ct["out"][b] = P.tok("vector", "vE")
                P.wait("gpsimd", "vE", ct["out"][b])
                P.emit("gpsimd", lambda eng, tt=tt, b=b: eng.dma_start(
                    x2t[tt], x2st[b][:]))
                ct["store"][b] = P.tok("gpsimd", "dG", 16)
            P.flush()



def build_program(upto="full", debug=False):
    nc = bass.Bass("TRN2", target_bir_lowering=False)
    dt_ = nc.dram_tensor
    x = dt_("x", [T, D], F32, kind="ExternalInput").ap()
    w1t = dt_("w1t", [E, D, F], F32, kind="ExternalInput").ap()
    w2t = dt_("w2t", [E, F, D], F32, kind="ExternalInput").ap()
    wq = dt_("wq", [D, D], F32, kind="ExternalInput").ap()
    wdkv = dt_("wdkv", [D, LAT], F32, kind="ExternalInput").ap()
    wuk = dt_("wuk", [LAT, D], F32, kind="ExternalInput").ap()
    wuv = dt_("wuv", [LAT, D], F32, kind="ExternalInput").ap()
    wo = dt_("wo", [D, D], F32, kind="ExternalInput").ap()
    lng = dt_("lng", [128, LAT], F32, kind="ExternalInput").ap()
    lnb = dt_("lnb", [128, LAT], F32, kind="ExternalInput").ap()
    normw = dt_("normw", [128, D], F32, kind="ExternalInput").ap()
    wmt = dt_("wmt", [D, 12], F32, kind="ExternalInput").ap()
    identity = dt_("identity", [128, 128], BF16, kind="ExternalInput").ap()
    ptmat = dt_("ptmat", [128, 128], BF16, kind="ExternalInput").ap()
    cosq = dt_("cosq", [128, T], F32, kind="ExternalInput").ap()
    sinq = dt_("sinq", [128, T], F32, kind="ExternalInput").ap()
    cosk = dt_("cosk", [128, T], F32, kind="ExternalInput").ap()
    sink = dt_("sink", [128, T], F32, kind="ExternalInput").ap()
    trimask = dt_("trimask", [4, 128, 512], F32, kind="ExternalInput").ap()
    e2mat = dt_("e2mat", [128, 2], BF16, kind="ExternalInput").ap()
    ones64 = dt_("ones64", [1, 64], F32, kind="ExternalInput").ap()
    y = dt_("y", [T, D], F32, kind="ExternalOutput").ap()

    kmxs = dt_("kmxs", [8, 2], F32).ap()
    w1b = dt_("w1b", [E, D, F], BF16).ap()
    w2b = dt_("w2b", [E, F, D], BF16).ap()
    x1s = dt_("x1s", [T, D], F32).ap()
    x2s = dt_("x2s", [T, D], F32).ap()
    qtr = dt_("qtr", [NH, 65, T], BF16).ap()
    ktr = dt_("ktr", [NH, 65, T], BF16).ap()
    _ = (wq, wdkv, wuk, wuv, wo, lng, lnb, cosq, sinq, cosk, sink, trimask,
         ptmat, x2s, qtr, ktr)

    with ExitStack() as stack:
        EC = stack.enter_context
        P = Prog(nc, stack)
        P.debug = debug
        P.dbg_tensors = {}

        def dbg_dump(name, sb_ap, shape, dtype):
            if not P.debug or name in P.dbg_tensors:
                return
            t = nc.dram_tensor("dbg_" + name, list(shape), dtype,
                               kind="ExternalOutput").ap()
            P.dbg_tensors[name] = t
            for s_ in ("vE", "sE", "tE"):
                P.wait("gpsimd", s_, P.cur(s_))
            P.emit("gpsimd", lambda eng, d=t, s=sb_ap: eng.dma_start(d, s))
            tk = P.tok("gpsimd", "dG", 16)
            for e_ in ("vector", "scalar", "tensor", "sync"):
                P.wait(e_, "dG", tk)
        P.dbg_dump = dbg_dump

        ident = EC(nc.sbuf_tensor("identsb", [128, 128], BF16))
        ptm = EC(nc.sbuf_tensor("ptmsb", [128, 128], BF16))
        e2sb = EC(nc.sbuf_tensor("e2sb", [128, 2], BF16))
        normw_b = EC(nc.sbuf_tensor("normw_b", [128, D], F32))
        wmt_sb = EC(nc.sbuf_tensor("wmt_sb", [128, DT * 12], BF16))

        P.begin()
        P.emit("gpsimd", lambda eng: eng.dma_start(ident[:], identity))
        P.tok("gpsimd", "dG", 16)
        P.emit("gpsimd", lambda eng: eng.dma_start(ptm[:], ptmat))
        P.tok("gpsimd", "dG", 16)
        P.emit("gpsimd", lambda eng: eng.dma_start(e2sb[:], e2mat))
        P.tok("gpsimd", "dG", 16)
        P.emit("gpsimd", lambda eng: eng.dma_start(normw_b[:], normw))
        P.tok("gpsimd", "dG", 16)
        P.emit("gpsimd", lambda eng: eng.dma_start(
            wmt_sb[:].rearrange("p (dt c) -> p dt c", c=12),
            wmt.rearrange("(dt p) c -> p dt c", p=128)))
        P.tok("gpsimd", "dG", 16)
        P.flush()

        def alloc_ffn(EC, sfx=""):
            nm = lambda s: s + sfx
            return dict(
            xst=EC(nc.sbuf_tensor(nm("xst"), [128, NT1 * D], F32)),
            zC=EC(nc.sbuf_tensor(nm("zC"), [128, NT1 * D], F32)),
            sqst=EC(nc.sbuf_tensor(nm("sqst"), [128, NT1 * D], F32)),
            znbf=EC(nc.sbuf_tensor(nm("znbf"), [128, NT1 * D], BF16)),
            znT=EC(nc.sbuf_tensor(nm("znT"), [128, DT * T1], BF16)),
            hsT=[EC(nc.sbuf_tensor(nm("hsT0"), [128, NFT * T1], BF16)),
                 EC(nc.sbuf_tensor(nm("hsT1"), [128, NFT * T1], BF16))],
            w1sb=[EC(nc.sbuf_tensor(nm("w1sb0"), [128, DT * 1024], BF16)),
                  EC(nc.sbuf_tensor(nm("w1sb1"), [128, DT * 1024], BF16))],
            w2sb=[EC(nc.sbuf_tensor(nm("w2sb0"), [128, NFT * 1024], BF16)),
                  EC(nc.sbuf_tensor(nm("w2sb1"), [128, NFT * 1024], BF16))],
            mu4=EC(nc.sbuf_tensor(nm("mu4"), [128, NT1], F32)),
            v4=EC(nc.sbuf_tensor(nm("v4"), [128, NT1], F32)),
            rstd4=EC(nc.sbuf_tensor(nm("rstd4"), [128, NT1], F32)),
            rt=EC(nc.sbuf_tensor(nm("rt"), [128, NT1 * 12], F32)),
            cC=EC(nc.sbuf_tensor(nm("cCb"), [128, NT1 * 8], F32)),
            r_max=EC(nc.sbuf_tensor(nm("r_max"), [128, NT1], F32)),
            r_me=EC(nc.sbuf_tensor(nm("r_me"), [128, NT1 * 4], F32)),
            r_sc1=EC(nc.sbuf_tensor(nm("r_sc1"), [128, NT1], F32)),
            r_cnt=EC(nc.sbuf_tensor(nm("r_cnt"), [128, NT1 * 4], F32)),
            r_tmp=EC(nc.sbuf_tensor(nm("r_tmp"), [128, NT1 * 4], F32)),
            r_ev=EC(nc.sbuf_tensor(nm("r_ev"), [128, NT1 * 4], F32)),
            r_od=EC(nc.sbuf_tensor(nm("r_od"), [128, NT1 * 4], F32)),
            r_lv=EC(nc.sbuf_tensor(nm("r_lv"), [128, NT1 * 4], F32)),
            r_xa=EC(nc.sbuf_tensor(nm("r_xa"), [128, NT1 * 4], F32)),
            r_xb=EC(nc.sbuf_tensor(nm("r_xb"), [128, NT1 * 4], F32)),
            ph=[EC(nc.psum_tensor(nm("ph0"), [128, T1], F32)),
                EC(nc.psum_tensor(nm("ph1"), [128, T1], F32))],
            py=[EC(nc.psum_tensor(nm("py0"), [128, 1024], F32)),
                EC(nc.psum_tensor(nm("py1"), [128, 1024], F32))],
            ptr=EC(nc.psum_tensor(nm("ptrb"), [128, NT1 * 128], BF16)),
            pr=EC(nc.psum_tensor(nm("prb"), [128, NT1 * 12], F32)),
            normw_b=normw_b, wmt_sb=wmt_sb, ident=ident,
            w1b4=[w1b[e].rearrange("(dt p) f -> p dt f", p=128) for e in range(E)],
            w2b4=[w2b[e].rearrange("(ft p) d -> p ft d", p=128) for e in range(E)],
            state=dict(
                g_w1=0, g_w2=0, e_seen=0, ph_i=0, py_i=0,
                next_w1=0, next_w2=0,
                w1_tok={}, w2_tok={}, h_done={}, y_done={},
                silu_tok={}, stt_tok={}, expert_map={},
                znbf_rd_tok=None, znT_rd_tok=None,
                ptr_ev_tok=None, pr_ev_tok=None,
                zC_sc_tok=None, z_upd_tok=None, sq_free=None, route_tok=None,
                xst_store_tok=None, xst_free_tok=None,
                first_sync_wait=None, xst_guard_dW=None,
            ),
        )

        ffn1_stack = ExitStack()
        ffnbufs = alloc_ffn(ffn1_stack.enter_context)
        st = ffnbufs["state"]
        st["expert_of"] = lambda ei: ei % E

        P.begin()
        cbuf = ffnbufs["w2sb"]
        i = 0
        store_tok = {}
        for (srcm, dstm, R, Cc) in ((w1t, w1b, D, F), (w2t, w2b, F, D)):
            s4 = srcm.rearrange("e (rt p) c -> e p rt c", p=128)
            d4 = dstm.rearrange("e (rt p) c -> e p rt c", p=128)
            nrt = R // 128
            for e in range(E):
                b = i % 2
                P.wait("gpsimd", "dW", store_tok.get(i - 2))
                sb3 = cbuf[b][:, 0:nrt * Cc].rearrange("p (rt c) -> p rt c", c=Cc)
                P.emit("gpsimd", lambda eng, s=s4[e], d=sb3: eng.dma_start(d, s))
                ld = P.tok("gpsimd", "dG", 16)
                P.wait("sync", "dG", ld)
                P.emit("sync", lambda eng, s=sb3, d=d4[e]: eng.dma_start(d, s))
                store_tok[i] = P.tok("sync", "dW", 16)
                i += 1
        cast_done_dW = P.cur("dW")
        P.flush()
        st["first_sync_wait"] = cast_done_dW
        st["xst_guard_dW"] = cast_done_dW


        dst1 = y if upto == "ffn1" else x1s
        emit_ffn_sublayer(P, ffnbufs, x, dst1)
        ffn1_stack.close()
        if upto != "ffn1":
            x2dst = y if upto == "attn" else x2s
            A = dict(x1s=x1s, x2s=x2dst, qtr=qtr, ktr=ktr, kmxs=kmxs,
                     wq=wq, wdkv=wdkv, wuk=wuk, wuv=wuv, wo=wo,
                     lng=lng, lnb=lnb, cosq=cosq, sinq=sinq, cosk=cosk,
                     sink=sink, trimask=trimask, ones64=ones64,
                     ident=ident, ptm=ptm, e2sb=e2sb)
            emit_attention(P, nc, A)
            if upto == "attn":
                P.begin()
                P.wait("gpsimd", "dG", P.cur("dG"))
                P.wait("sync", "dW", P.cur("dW"))
                P.flush()
                return nc
            P.begin()
            for e_ in ENGS:
                for s_ in ("vE", "sE", "tE", "dG", "dW"):
                    P.wait(e_, s_, P.cur(s_))
            P.flush()
            ffn2_stack = ExitStack()
            fb2 = alloc_ffn(ffn2_stack.enter_context, sfx="_f2")
            stf2 = fb2["state"]
            stf2["expert_of"] = lambda ei: ei % E
            emit_ffn_sublayer(P, fb2, x2s, y)
            ffn2_stack.close()

        P.begin()
        P.wait("gpsimd", "dG", P.cur("dG"))
        P.wait("sync", "dW", P.cur("dW"))
        P.flush()
    return nc



def _host_constants():
    c = {}
    c["identity"] = np.eye(128, dtype=ml_dtypes.bfloat16)
    Pm = np.zeros((128, 128), dtype=np.float32)
    for b in (0, 64):
        for i in range(32):
            Pm[b + i, b + 32 + i] = -1.0
            Pm[b + 32 + i, b + i] = 1.0
    c["ptmat"] = Pm.T.astype(ml_dtypes.bfloat16)
    pos = np.arange(T, dtype=np.float64)
    inv = 1.0 / (10000.0 ** (np.arange(32, dtype=np.float64) / 32.0))
    ang = pos[None, :] * inv[:, None]
    cos128 = np.tile(np.cos(ang), (4, 1)).astype(np.float32)
    sin128 = np.tile(np.sin(ang), (4, 1)).astype(np.float32)
    c["cosk"], c["sink"] = cos128, sin128
    c["cosq"] = (cos128 * 0.125).astype(np.float32)
    c["sinq"] = (sin128 * 0.125).astype(np.float32)
    e2 = np.zeros((128, 2), dtype=np.float32)
    e2[0:64, 0] = 1.0
    e2[64:128, 1] = 1.0
    c["e2mat"] = e2.astype(ml_dtypes.bfloat16)
    c["ones64"] = np.ones((1, 64), dtype=np.float32)
    tm = np.zeros((4, 128, 512), dtype=np.float32)
    r = np.arange(128)[:, None]
    cidx = np.arange(512)[None, :]
    for m in range(4):
        tm[m] = np.where(128 * m + r <= cidx, 0.0, -1e30)
    c["trimask"] = tm
    return c


def kernel(L0_attn_x, L0_attn_W_q, L0_attn_W_dkv, L0_attn_W_uk, L0_attn_W_uv,
           L0_attn_W_o, L0_attn_ln_gamma, L0_attn_ln_beta, L0_ffn_norm_w,
           L0_ffn_w_macro, L0_ffn_w_micro, L0_ffn_w1, L0_ffn_w2, _upto="full",
           _debug=False):
    xs = np.ascontiguousarray(np.asarray(L0_attn_x, dtype=np.float32))
    w1t = np.ascontiguousarray(np.asarray(L0_ffn_w1, np.float32).transpose(0, 2, 1))
    w2t = np.ascontiguousarray(np.asarray(L0_ffn_w2, np.float32).transpose(0, 2, 1))
    wmt = np.ascontiguousarray(np.concatenate(
        [np.asarray(L0_ffn_w_macro, np.float32).T,
         np.asarray(L0_ffn_w_micro, np.float32).reshape(8, D).T], axis=1))
    consts = _host_constants()
    shared = dict(
        w1t=w1t, w2t=w2t,
        wq=np.ascontiguousarray(np.asarray(L0_attn_W_q, np.float32)),
        wdkv=np.ascontiguousarray(np.asarray(L0_attn_W_dkv, np.float32)),
        wuk=np.ascontiguousarray(np.asarray(L0_attn_W_uk, np.float32)),
        wuv=np.ascontiguousarray(np.asarray(L0_attn_W_uv, np.float32)),
        wo=np.ascontiguousarray(np.asarray(L0_attn_W_o, np.float32)),
        lng=np.tile(np.asarray(L0_attn_ln_gamma, np.float32).reshape(1, LAT),
                    (128, 1)),
        lnb=np.tile(np.asarray(L0_attn_ln_beta, np.float32).reshape(1, LAT),
                    (128, 1)),
        normw=np.tile(np.asarray(L0_ffn_norm_w, np.float32).reshape(1, D),
                      (128, 1)),
        wmt=wmt, **consts)
    nc = build_program(upto=_upto, debug=_debug)
    in_maps = [dict(shared, x=np.ascontiguousarray(xs[i])) for i in range(8)]
    res = run_bass_kernel_spmd(nc, in_maps, core_ids=list(range(8)))
    out = np.stack([np.asarray(r["y"], np.float32) for r in res.results], axis=0)
    if _debug:
        return out, res.results
    return out


# version 03
# speedup vs baseline: 1.2646x; 1.2646x over previous
from contextlib import ExitStack

import numpy as np
import ml_dtypes

import concourse.bass as bass
import concourse.mybir as mybir
from concourse.bass_utils import run_bass_kernel_spmd

F32 = mybir.dt.float32
BF16 = mybir.dt.bfloat16
AF = mybir.ActivationFunctionType
ALU = mybir.AluOpType
AX = mybir.AxisListType

T, D = 4096, 1024
DT = 8
T1 = 512
NT1 = T1 // 128
NCH = T // T1
E, F = 8, 2048
NFT = F // 128
LAT = 256
NH, HD = 16, 64
LN_EPS = 1e-5
RMS_EPS = 1e-6
EPS9 = 1e-9

ENGS = ("sync", "tensor", "vector", "scalar", "gpsimd")


class Prog:

    def __init__(self, nc, stack):
        self.nc = nc
        self.q = None
        self.sem = {}
        for name in ("dW", "dG", "tE", "vE", "sE"):
            h = stack.enter_context(nc.semaphore(name))
            self.sem[name] = [h, 0]

    def begin(self):
        self.q = {e: [] for e in ENGS}

    def flush(self):
        with self.nc.Block() as blk:
            for e in ENGS:
                ops = self.q[e]
                if ops:
                    def runner(eng, ops=ops):
                        for f in ops:
                            f(eng)
                    getattr(blk, e)(runner)
        self.q = None

    def emit(self, eng, fn):
        self.q[eng].append(fn)

    def tok(self, eng, sem, k=1):
        s = self.sem[sem]
        s[1] += k
        v, h = s[1], s[0]
        prev = self.q[eng][-1]
        self.q[eng][-1] = lambda eng_, p=prev, h=h, k=k: p(eng_).then_inc(h, k)
        return v

    def wait(self, eng, sem, val):
        if val is None or val <= 0:
            return
        h = self.sem[sem][0]
        self.q[eng].append(lambda eng_, h=h, v=val: eng_.wait_ge(h, v))

    def cur(self, sem):
        return self.sem[sem][1]

    OWN = {"vector": "vE", "scalar": "sE", "tensor": "tE"}

    def selfw(self, eng):
        t = self.tok(eng, self.OWN[eng])
        self.wait(eng, self.OWN[eng], t)
        return t



def emit_ffn_sublayer(P, B, src_ap, dst_ap):
    src4 = src_ap.rearrange("(c tt p) d -> c p tt d", tt=NT1, p=128)
    dst4 = dst_ap.rearrange("(c tt p) d -> c p tt d", tt=NT1, p=128)

    xst, zC, sqst, znbf, znT = B["xst"], B["zC"], B["sqst"], B["znbf"], B["znT"]
    hsT, w1sb, w2sb = B["hsT"], B["w1sb"], B["w2sb"]
    mu4, v4, rstd4, rt, cC = B["mu4"], B["v4"], B["rstd4"], B["rt"], B["cC"]
    normw_b, wmt_sb, ident = B["normw_b"], B["wmt_sb"], B["ident"]
    ph, py, ptr_, pr = B["ph"], B["py"], B["ptr"], B["pr"]
    w1b4, w2b4 = B["w1b4"], B["w2b4"]
    st = B["state"]
    x3 = xst[:].rearrange("p (tt d) -> p tt d", tt=NT1)
    z3 = zC[:].rearrange("p (tt d) -> p tt d", tt=NT1)
    zn3 = znbf[:].rearrange("p (tt d) -> p tt d", tt=NT1)
    znsc = sqst

    def pump_w2(upto):
        while st["next_w2"] <= upto:
            ei = st["next_w2"]
            st["next_w2"] += 1
            if st["first_sync_wait"] is not None:
                P.wait("sync", "dW", st["first_sync_wait"])
                st["first_sync_wait"] = None
            P.wait("sync", "tE", st["y_done"].get(ei - 2))
            e = st["expert_of"](ei)
            dstb = w2sb[ei % 2][:].rearrange("p (ft d) -> p ft d", ft=NFT)
            P.emit("sync", lambda eng, s=w2b4[e], d=dstb: eng.dma_start(d, s))
            st["w2_tok"][ei] = P.tok("sync", "dW", 16)

    def pump_w1(upto):
        while st["next_w1"] <= upto:
            g = st["next_w1"]
            st["next_w1"] += 1
            if st["first_sync_wait"] is not None:
                P.wait("sync", "dW", st["first_sync_wait"])
                st["first_sync_wait"] = None
            P.wait("sync", "tE", st["h_done"].get(g - 2))
            e = st["expert_of"](g // 2)
            half = g % 2
            src = w1b4[e][:, :, half * 1024:(half + 1) * 1024]
            dstb = w1sb[g % 2][:].rearrange("p (dt f) -> p dt f", dt=DT)
            P.emit("sync", lambda eng, s=src, d=dstb: eng.dma_start(d, s))
            st["w1_tok"][g] = P.tok("sync", "dW", 16)

    def stats(eps):
        P.wait("scalar", "vE", st["sq_free"])
        P.wait("scalar", "sE", st["zC_sc_tok"])
        P.emit("scalar", lambda eng: eng.activation(sqst[:], zC[:], AF.Square))
        t = P.tok("scalar", "sE")
        P.wait("vector", "sE", t)
        P.emit("vector", lambda eng: eng.tensor_reduce(
            v4[:], sqst[:].rearrange("p (tt d) -> p tt d", tt=NT1), AX.X, ALU.add))
        P.selfw("vector")
        P.emit("vector", lambda eng, eps=eps: eng.tensor_scalar(
            v4[:], v4[:], 1.0 / D, eps, ALU.mult, ALU.add))
        st["sq_free"] = P.tok("vector", "vE")
        P.wait("scalar", "vE", st["sq_free"])
        P.emit("scalar", lambda eng: eng.activation(rstd4[:], v4[:], AF.Sqrt))
        t = P.tok("scalar", "sE")
        P.wait("vector", "sE", t)
        P.emit("vector", lambda eng: eng.reciprocal(rstd4[:], rstd4[:]))
        return P.tok("vector", "vE")

    for C in range(NCH):
        P.begin()
        pump_w2(st["e_seen"] + 1)
        pump_w1(2 * st["e_seen"] + 1)

        P.wait("gpsimd", "dG", st["xst_store_tok"])
        P.wait("gpsimd", "dW", st["xst_guard_dW"])
        st["xst_guard_dW"] = None
        P.wait("gpsimd", "vE", st["xst_free_tok"])
        xv = xst[:].rearrange("p (tt d) -> p tt d", tt=NT1)
        P.emit("gpsimd", lambda eng, s=src4[C], d=xv: eng.dma_start(d, s))
        x_tok = P.tok("gpsimd", "dG", 16)

        P.wait("vector", "dG", x_tok)
        P.emit("vector", lambda eng: eng.tensor_reduce(mu4[:], x3, AX.X, ALU.add))
        P.selfw("vector")
        P.emit("vector", lambda eng: eng.tensor_scalar(
            mu4[:], mu4[:], 1.0 / D, None, ALU.mult))
        P.selfw("vector")
        P.wait("vector", "sE", st["zC_sc_tok"])
        for tt in range(NT1):
            P.emit("vector", lambda eng, tt=tt: eng.tensor_scalar(
                z3[:, tt], x3[:, tt], mu4[:, tt:tt + 1], None, ALU.subtract))
        xc_tok = P.tok("vector", "vE")
        P.wait("scalar", "vE", xc_tok)
        rs_tok = stats(LN_EPS)
        P.wait("scalar", "vE", rs_tok)
        for tt in range(NT1):
            P.emit("scalar", lambda eng, tt=tt: eng.activation(
                z3[:, tt], z3[:, tt], AF.Copy, scale=rstd4[:, tt:tt + 1]))
        st["zC_sc_tok"] = P.tok("scalar", "sE")
        if C == 0 and getattr(P, "debug", False):
            P.wait("gpsimd", "sE", st["zC_sc_tok"])
            P.dbg_dump("zC_ln", zC[:], (128, NT1 * D), F32)

        for L in range(2):
            if L == 1:
                P.wait("scalar", "vE", st["z_upd_tok"])
            rs_tok = stats(RMS_EPS)
            P.wait("scalar", "vE", rs_tok)
            for tt in range(NT1):
                P.wait("scalar", "vE", st["sq_free"])
                P.emit("scalar", lambda eng, tt=tt: eng.activation(
                    znsc[:, 0:D], z3[:, tt], AF.Copy, scale=rstd4[:, tt:tt + 1]))
                sc_tok = P.tok("scalar", "sE")
                st["zC_sc_tok"] = sc_tok
                P.wait("vector", "sE", sc_tok)
                if tt == 0:
                    P.wait("vector", "tE", st["znbf_rd_tok"])
                P.emit("vector", lambda eng, tt=tt: eng.tensor_tensor(
                    zn3[:, tt], znsc[:, 0:D], normw_b[:], ALU.mult))
                st["sq_free"] = P.tok("vector", "vE")
            znbf_tok = st["sq_free"]

            P.wait("tensor", "vE", znbf_tok)
            for dt in range(DT):
                P.wait("tensor", "vE", st["ptr_ev_tok"])
                for tt in range(NT1):
                    P.emit("tensor", lambda eng, dt=dt, tt=tt: eng.transpose(
                        ptr_[:, tt * 128:(tt + 1) * 128],
                        znbf[:, tt * D + dt * 128: tt * D + dt * 128 + 128],
                        ident[:]))
                tr_tok = P.tok("tensor", "tE")
                if dt == DT - 1:
                    st["znbf_rd_tok"] = tr_tok
                P.wait("vector", "tE", tr_tok)
                if dt == 0:
                    P.wait("vector", "tE", st["znT_rd_tok"])
                P.emit("vector", lambda eng, dt=dt: eng.tensor_copy(
                    znT[:, dt * T1:(dt + 1) * T1], ptr_[:]))
                st["ptr_ev_tok"] = P.tok("vector", "vE")
            znT_tok = st["ptr_ev_tok"]
            if C == 0 and L == 0 and getattr(P, "debug", False):
                P.wait("gpsimd", "vE", znT_tok)
                P.dbg_dump("znT0", znT[:], (128, DT * T1), BF16)

            P.wait("tensor", "vE", znT_tok)
            P.wait("tensor", "vE", st["pr_ev_tok"])
            for tt in range(NT1):
                for dt in range(DT):
                    P.emit("tensor", lambda eng, tt=tt, dt=dt: eng.matmul(
                        pr[:, tt * 12:(tt + 1) * 12],
                        znT[:, dt * T1 + tt * 128: dt * T1 + (tt + 1) * 128],
                        wmt_sb[:, dt * 12:(dt + 1) * 12],
                        start=(dt == 0), stop=(dt == DT - 1),
                        skip_group_check=True))
            route_mm_tok = P.tok("tensor", "tE")
            P.wait("vector", "tE", route_mm_tok)
            P.emit("vector", lambda eng: eng.tensor_copy(rt[:], pr[:]))
            st["pr_ev_tok"] = P.tok("vector", "vE")
            route_tok = _emit_routing(P, B)
            st["route_tok"] = route_tok
            if C == 0 and L == 0 and getattr(P, "debug", False):
                P.wait("gpsimd", "vE", route_tok)
                P.dbg_dump("rt0", rt[:], (128, NT1 * 12), F32)
                P.dbg_dump("cC0", cC[:], (128, NT1 * 8), F32)

            for e in range(E):
                ei = st["e_seen"]
                st["e_seen"] += 1
                st["expert_map"][ei] = e
                hb = ei % 2
                pump_w2(ei + 1)
                for ft in range(NFT):
                    g = 2 * ei + ft // 8
                    pb = st["ph_i"] % 2
                    st["ph_i"] += 1
                    P.wait("tensor", "sE", st["silu_tok"].get(st["ph_i"] - 1 - 2))
                    if ft == 0 or ft == 8:
                        P.wait("tensor", "dW", st["w1_tok"][g])
                    for dt in range(DT):
                        P.emit("tensor",
                               lambda eng, pb=pb, gb=g % 2, ft=ft, dt=dt: eng.matmul(
                                   ph[pb][:],
                                   w1sb[gb][:, dt * 1024 + (ft % 8) * 128:
                                            dt * 1024 + (ft % 8) * 128 + 128],
                                   znT[:, dt * T1:(dt + 1) * T1],
                                   start=(dt == 0), stop=(dt == DT - 1),
                                   skip_group_check=True))
                    h_tok = P.tok("tensor", "tE")
                    if ft == 7 or ft == 15:
                        st["h_done"][g] = h_tok
                    if ft == NFT - 1:
                        st["znT_rd_tok"] = h_tok
                    P.wait("scalar", "tE", h_tok)
                    if ft == 0:
                        P.wait("scalar", "tE", st["y_done"].get(ei - 2))
                    P.emit("scalar", lambda eng, hb=hb, ft=ft, pb=pb: eng.activation(
                        hsT[hb][:, ft * T1:(ft + 1) * T1], ph[pb][:], AF.Silu))
                    st["silu_tok"][st["ph_i"] - 1] = P.tok("scalar", "sE")
                silu_done = st["silu_tok"][st["ph_i"] - 1]
                pump_w1(2 * (ei + 1) + 1)

                P.wait("tensor", "sE", silu_done)
                P.wait("tensor", "dW", st["w2_tok"][ei])
                for tt in range(NT1):
                    pb = st["py_i"] % 2
                    st["py_i"] += 1
                    P.wait("tensor", "vE", st["stt_tok"].get(st["py_i"] - 1 - 2))
                    for ft in range(NFT):
                        for h2 in range(2):
                            P.emit("tensor",
                                   lambda eng, pb=pb, hb=hb, tt=tt, ft=ft, h2=h2,
                                   eb=ei % 2: eng.matmul(
                                       py[pb][:, h2 * 512:(h2 + 1) * 512],
                                       hsT[hb][:, ft * T1 + tt * 128:
                                               ft * T1 + (tt + 1) * 128],
                                       w2sb[eb][:, ft * 1024 + h2 * 512:
                                                ft * 1024 + h2 * 512 + 512],
                                       start=(ft == 0), stop=(ft == NFT - 1),
                                       skip_group_check=True))
                    y_tok = P.tok("tensor", "tE")
                    if tt == NT1 - 1:
                        st["y_done"][ei] = y_tok
                    P.wait("vector", "tE", y_tok)
                    P.wait("vector", "vE",
                           st["stt_tok"].get(st["py_i"] - 1 - NT1))
                    if e == 0 and tt == 0:
                        P.wait("vector", "vE", st["route_tok"])
                    P.emit("vector",
                           lambda eng, tt=tt, pb=pb, e=e: eng.scalar_tensor_tensor(
                               z3[:, tt], py[pb][:],
                               cC[:, tt * 8 + e: tt * 8 + e + 1],
                               z3[:, tt], ALU.mult, ALU.add))
                    st["stt_tok"][st["py_i"] - 1] = P.tok("vector", "vE")
            st["z_upd_tok"] = st["stt_tok"][st["py_i"] - 1]
            if C == 0 and L == 0 and getattr(P, "debug", False):
                P.wait("gpsimd", "vE", st["z_upd_tok"])
                P.dbg_dump("zC_L0", zC[:], (128, NT1 * D), F32)
                P.dbg_dump("hsT_e7", hsT[(st["e_seen"] - 1) % 2][:],
                           (128, NFT * T1), BF16)

        P.wait("vector", "vE", st["z_upd_tok"])
        for tt in range(NT1):
            P.emit("vector", lambda eng, tt=tt: eng.scalar_tensor_tensor(
                x3[:, tt], z3[:, tt], 0.5, x3[:, tt], ALU.mult, ALU.add))
        out_tok = P.tok("vector", "vE")
        st["xst_free_tok"] = out_tok
        P.wait("gpsimd", "vE", out_tok)
        P.emit("gpsimd", lambda eng, d=dst4[C], s=xv: eng.dma_start(d, s))
        st["xst_store_tok"] = P.tok("gpsimd", "dG", 16)
        P.flush()


def _emit_routing(P, B):
    rt, cC = B["rt"], B["cC"]
    mm_, me, sc1 = B["r_max"], B["r_me"], B["r_sc1"]
    cnt, tmp, ev, od, lv = B["r_cnt"], B["r_tmp"], B["r_ev"], B["r_od"], B["r_lv"]
    xa, xb = B["r_xa"], B["r_xb"]
    rt3 = rt[:].rearrange("p (t c) -> p t c", c=12)
    me3 = me[:].rearrange("p (t c) -> p t c", c=4)
    ev3 = ev[:].rearrange("p (t c) -> p t c", c=4)
    od3 = od[:].rearrange("p (t c) -> p t c", c=4)
    cnt3 = cnt[:].rearrange("p (t c) -> p t c", c=4)
    tmp3 = tmp[:].rearrange("p (t c) -> p t c", c=4)
    lv3 = lv[:].rearrange("p (t c) -> p t c", c=4)
    cC3 = cC[:].rearrange("p (t c) -> p t c", c=8)

    def VS(fn):
        P.emit("vector", fn)
        P.selfw("vector")

    VS(lambda eng: eng.tensor_reduce(mm_[:], rt3[:, :, 0:4], AX.X, ALU.max))
    for tt in range(NT1):
        VS(lambda eng, tt=tt: eng.tensor_scalar(
            me3[:, tt], rt3[:, tt, 0:4], mm_[:, tt:tt + 1], None, ALU.subtract))
    t = P.cur("vE")
    P.wait("scalar", "vE", t)
    P.emit("scalar", lambda eng: eng.activation(xa[:], me[:], AF.Exp))
    t = P.tok("scalar", "sE")
    P.wait("vector", "sE", t)
    VS(lambda eng: eng.tensor_copy(me[:], xa[:]))
    VS(lambda eng: eng.tensor_reduce(sc1[:], me3, AX.X, ALU.add))
    VS(lambda eng: eng.reciprocal(sc1[:], sc1[:]))
    for tt in range(NT1):
        VS(lambda eng, tt=tt: eng.tensor_scalar(
            me3[:, tt], me3[:, tt], sc1[:, tt:tt + 1], None, ALU.mult))
    VS(lambda eng: eng.memset(cnt[:], 0.0))
    for g in range(4):
        for gp in range(4):
            if gp == g:
                continue
            VS(lambda eng, g=g, gp=gp: eng.tensor_tensor(
                tmp3[:, :, 0:1], me3[:, :, gp:gp + 1], me3[:, :, g:g + 1],
                ALU.is_gt))
            VS(lambda eng, g=g: eng.tensor_tensor(
                cnt3[:, :, g:g + 1], cnt3[:, :, g:g + 1], tmp3[:, :, 0:1],
                ALU.add))
            if gp < g:
                VS(lambda eng, g=g, gp=gp: eng.tensor_tensor(
                    tmp3[:, :, 0:1], me3[:, :, gp:gp + 1], me3[:, :, g:g + 1],
                    ALU.is_equal))
                VS(lambda eng, g=g: eng.tensor_tensor(
                    cnt3[:, :, g:g + 1], cnt3[:, :, g:g + 1], tmp3[:, :, 0:1],
                    ALU.add))
    VS(lambda eng: eng.tensor_scalar(cnt[:], cnt[:], 2.0, None, ALU.is_lt))
    VS(lambda eng: eng.tensor_tensor(me[:], me[:], cnt[:], ALU.mult))
    VS(lambda eng: eng.tensor_reduce(sc1[:], me3, AX.X, ALU.add))
    VS(lambda eng: eng.tensor_scalar(sc1[:], sc1[:], EPS9, None, ALU.add))
    VS(lambda eng: eng.reciprocal(sc1[:], sc1[:]))
    for tt in range(NT1):
        VS(lambda eng, tt=tt: eng.tensor_scalar(
            me3[:, tt], me3[:, tt], sc1[:, tt:tt + 1], None, ALU.mult))
    if getattr(P, "debug", False) and not P.dbg_tensors.get("r_me_post"):
        P.dbg_dump("r_me_post", me[:], (128, NT1 * 4), F32)
        P.dbg_dump("r_cnt_post", cnt[:], (128, NT1 * 4), F32)
        P.dbg_dump("r_sc1_post", sc1[:], (128, NT1), F32)
    VS(lambda eng: eng.tensor_tensor(ev3, rt3[:, :, 4:12:2], rt3[:, :, 5:12:2],
                                     ALU.max))
    VS(lambda eng: eng.tensor_tensor(od3, rt3[:, :, 5:12:2], ev3, ALU.subtract))
    VS(lambda eng: eng.tensor_tensor(ev3, rt3[:, :, 4:12:2], ev3, ALU.subtract))
    t = P.cur("vE")
    P.wait("scalar", "vE", t)
    P.emit("scalar", lambda eng: eng.activation(xa[:], ev[:], AF.Exp))
    P.selfw("scalar")
    P.emit("scalar", lambda eng: eng.activation(xb[:], od[:], AF.Exp))
    t = P.tok("scalar", "sE")
    P.wait("vector", "sE", t)
    VS(lambda eng: eng.tensor_copy(ev[:], xa[:]))
    VS(lambda eng: eng.tensor_copy(od[:], xb[:]))
    VS(lambda eng: eng.tensor_tensor(tmp[:], ev[:], od[:], ALU.add))
    VS(lambda eng: eng.reciprocal(tmp[:], tmp[:]))
    VS(lambda eng: eng.tensor_tensor(ev[:], ev[:], tmp[:], ALU.mult))
    VS(lambda eng: eng.tensor_tensor(od[:], od[:], tmp[:], ALU.mult))
    VS(lambda eng: eng.tensor_tensor(lv[:], ev[:], od[:], ALU.max))
    VS(lambda eng: eng.tensor_tensor(tmp[:], ev[:], od[:], ALU.is_ge))
    VS(lambda eng: eng.tensor_scalar(ev[:], lv[:], EPS9, None, ALU.add))
    VS(lambda eng: eng.reciprocal(ev[:], ev[:]))
    VS(lambda eng: eng.tensor_tensor(lv[:], lv[:], ev[:], ALU.mult))
    if getattr(P, "debug", False) and not P.dbg_tensors.get("r_lv_post"):
        P.dbg_dump("r_lv_post", lv[:], (128, NT1 * 4), F32)
        P.dbg_dump("r_ind_post", tmp[:], (128, NT1 * 4), F32)
    VS(lambda eng: eng.tensor_tensor(lv[:], lv[:], me[:], ALU.mult))
    VS(lambda eng: eng.tensor_tensor(cC3[:, :, 0:8:2], lv3, tmp3, ALU.mult))
    VS(lambda eng: eng.tensor_scalar(tmp[:], tmp[:], -1.0, 1.0, ALU.mult, ALU.add))
    VS(lambda eng: eng.tensor_tensor(cC3[:, :, 1:8:2], lv3, tmp3, ALU.mult))
    return P.cur("vE")




TCA = 256
NTA = TCA // 128
CHA = T // TCA


def emit_attention(P, nc, A):
    x1v = A["x1s"].rearrange("(c tt p) d -> c p tt d", tt=NTA, p=128)
    x1t = A["x1s"].rearrange("(tt p) d -> tt p d", p=128)
    x2t = A["x2s"].rearrange("(tt p) d -> tt p d", p=128)
    qtr, ktr = A["qtr"], A["ktr"]
    ident, ptm, e2sb = A["ident"], A["ptm"], A["e2sb"]

    def barrier():
        for e_ in ENGS:
            for s_ in ("vE", "sE", "tE", "dG", "dW"):
                P.wait(e_, s_, P.cur(s_))

    with ExitStack() as SO:
        EO = SO.enter_context
        v65 = EO(nc.sbuf_tensor("v65", [128, 32 * 1040], BF16))

        with ExitStack() as SA:
            EA = SA.enter_context
            qn2 = EA(nc.sbuf_tensor("qn2", [16, T], F32))
            Bsb = EA(nc.sbuf_tensor("Bsb", [16, T], F32))
            ones16 = EA(nc.sbuf_tensor("ones16", [16, T], BF16))
            km2 = EA(nc.sbuf_tensor("km2", [2, 8], F32))
            kmx = EA(nc.sbuf_tensor("kmx", [16, 1], F32))
            kmx2 = EA(nc.sbuf_tensor("kmx2", [16, 1], F32))
            st2 = [EA(nc.sbuf_tensor("stg2_%d" % i, [2, 512], F32))
                   for i in range(2)]
            ast = EA(nc.sbuf_tensor("ast", [128, NTA * D], F32))
            asq = EA(nc.sbuf_tensor("asq", [128, NTA * D], F32))
            ybf = EA(nc.sbuf_tensor("ybf", [128, NTA * D], BF16))
            yT = EA(nc.sbuf_tensor("yT", [128, DT * TCA], BF16))
            amu = EA(nc.sbuf_tensor("amu", [128, NTA], F32))
            av4 = EA(nc.sbuf_tensor("av4", [128, NTA], F32))
            ars = EA(nc.sbuf_tensor("ars", [128, NTA], F32))
            lst = EA(nc.sbuf_tensor("lst", [128, LAT], F32))
            lsc = EA(nc.sbuf_tensor("lsc", [128, LAT], F32))
            lnbf = EA(nc.sbuf_tensor("lnbf", [128, LAT], BF16))
            lnT = EA(nc.sbuf_tensor("lnT", [128, 2 * TCA], BF16))
            lm1 = EA(nc.sbuf_tensor("lm1", [128, 1], F32))
            lv1 = EA(nc.sbuf_tensor("lv1", [128, 1], F32))
            lr1 = EA(nc.sbuf_tensor("lr1", [128, 1], F32))
            wq_sb = EA(nc.sbuf_tensor("wq_sb", [128, DT * D], BF16))
            wdkv_sb = EA(nc.sbuf_tensor("wdkv_sb", [128, DT * LAT], BF16))
            wuk_sb = EA(nc.sbuf_tensor("wuk_sb", [128, 2 * D], BF16))
            wuv_sb = EA(nc.sbuf_tensor("wuv_sb", [128, 2 * D], BF16))
            lng_b = EA(nc.sbuf_tensor("lng_b", [128, LAT], F32))
            lnb_b = EA(nc.sbuf_tensor("lnb_b", [128, LAT], F32))
            csq = EA(nc.sbuf_tensor("csq", [128, TCA], F32))
            snq = EA(nc.sbuf_tensor("snq", [128, TCA], F32))
            csk = EA(nc.sbuf_tensor("csk", [128, TCA], F32))
            snk = EA(nc.sbuf_tensor("snk", [128, TCA], F32))
            qtf = EA(nc.sbuf_tensor("qtf", [128, TCA], F32))
            qtb = EA(nc.sbuf_tensor("qtb", [128, TCA], BF16))
            rt1 = EA(nc.sbuf_tensor("rt1", [128, TCA], F32))
            rt2 = EA(nc.sbuf_tensor("rt2", [128, TCA], F32))
            qrf = EA(nc.sbuf_tensor("qrf", [128, TCA], F32))
            qrb = EA(nc.sbuf_tensor("qrb", [128, TCA], BF16))
            sqb = EA(nc.sbuf_tensor("sqb", [128, TCA], BF16))
            pT = EA(nc.psum_tensor("pTa", [128, 512], BF16))
            pL = EA(nc.psum_tensor("pLa", [128, LAT], F32))
            pQ = EA(nc.psum_tensor("pQa", [128, TCA], F32))
            pR = EA(nc.psum_tensor("pRa", [128, TCA], F32))
            pN = EA(nc.psum_tensor("pNa", [2, TCA], F32))
            pV = EA(nc.psum_tensor("pVa", [128, 1024], F32))

            P.begin()
            barrier()
            for dst_, srcm, nt_ in ((wq_sb, A["wq"], DT), (wdkv_sb, A["wdkv"], DT),
                                    (wuk_sb, A["wuk"], 2), (wuv_sb, A["wuv"], 2)):
                cc = dst_.shape[1] // nt_
                P.emit("gpsimd", lambda eng, d=dst_, s=srcm, nt=nt_, cc=cc:
                       eng.dma_start(
                           d[:].rearrange("p (t c) -> p t c", t=nt),
                           s.rearrange("(t p) c -> p t c", p=128)))
                P.tok("gpsimd", "dG", 16)
            P.emit("gpsimd", lambda eng: eng.dma_start(lng_b[:], A["lng"]))
            P.tok("gpsimd", "dG", 16)
            P.emit("gpsimd", lambda eng: eng.dma_start(lnb_b[:], A["lnb"]))
            wload_tok = P.tok("gpsimd", "dG", 16)
            P.emit("vector", lambda eng: eng.memset(v65[:], 1.0))
            v65_tok = P.tok("vector", "vE")
            P.emit("vector", lambda eng: eng.memset(km2[:], 0.0))
            P.selfw("vector")
            P.flush()

            tk = dict(ast_free=None, cs_rd=None, ln_done=None, ybf_rd=None,
                      yT_rd=None, lnT_rd=None, pT_ev=None, lat_ev=None,
                      lsc_rd=None, pQ_free=None, rot_rd=None, qtf_rd=None,
                      pN_ev=None, stg_ev=[None, None],
                      stg_free=[None, None], vout=v65_tok, sqb_rd=None)
            sti = [0]

            for C in range(CHA):
                P.begin()
                cix = slice(C * TCA, (C + 1) * TCA)
                P.wait("gpsimd", "sE", tk["ast_free"])
                P.emit("gpsimd", lambda eng, s=x1v[C],
                       d=ast[:].rearrange("p (tt d) -> p tt d", tt=NTA):
                       eng.dma_start(d, s))
                P.tok("gpsimd", "dG", 16)
                P.wait("gpsimd", "vE", tk["cs_rd"])
                for dst_, src_ in ((csq, A["cosq"]), (snq, A["sinq"]),
                                   (csk, A["cosk"]), (snk, A["sink"])):
                    P.emit("gpsimd", lambda eng, d=dst_, s=src_[:, cix]:
                           eng.dma_start(d[:], s))
                    P.tok("gpsimd", "dG", 16)
                ld_tok = P.cur("dG")

                x3 = ast[:].rearrange("p (tt d) -> p tt d", tt=NTA)
                a3 = asq[:].rearrange("p (tt d) -> p tt d", tt=NTA)
                P.wait("vector", "dG", ld_tok)
                P.emit("vector", lambda eng: eng.tensor_reduce(
                    amu[:], x3, AX.X, ALU.add))
                P.selfw("vector")
                P.emit("vector", lambda eng: eng.tensor_scalar(
                    amu[:], amu[:], 1.0 / D, None, ALU.mult))
                P.selfw("vector")
                P.wait("vector", "sE", tk["ln_done"])
                for tt in range(NTA):
                    P.emit("vector", lambda eng, tt=tt: eng.tensor_scalar(
                        a3[:, tt], x3[:, tt], amu[:, tt:tt + 1], None,
                        ALU.subtract))
                xc_tok = P.tok("vector", "vE")
                P.wait("scalar", "vE", xc_tok)
                P.emit("scalar", lambda eng: eng.activation(ast[:], asq[:],
                                                            AF.Square))
                sq_tok = P.tok("scalar", "sE")
                tk["ast_free"] = sq_tok
                P.wait("vector", "sE", sq_tok)
                P.emit("vector", lambda eng: eng.tensor_reduce(
                    av4[:], x3, AX.X, ALU.add))
                P.selfw("vector")
                P.emit("vector", lambda eng: eng.tensor_scalar(
                    av4[:], av4[:], 1.0 / D, LN_EPS, ALU.mult, ALU.add))
                t = P.tok("vector", "vE")
                P.wait("scalar", "vE", t)
                P.emit("scalar", lambda eng: eng.activation(ars[:], av4[:],
                                                            AF.Sqrt))
                t = P.tok("scalar", "sE")
                P.wait("vector", "sE", t)
                P.emit("vector", lambda eng: eng.reciprocal(ars[:], ars[:]))
                rs_tok = P.tok("vector", "vE")
                P.wait("scalar", "vE", rs_tok)
                P.wait("scalar", "tE", tk["ybf_rd"])
                for tt in range(NTA):
                    P.emit("scalar", lambda eng, tt=tt: eng.activation(
                        ybf[:, tt * D:(tt + 1) * D], a3[:, tt], AF.Copy,
                        scale=ars[:, tt:tt + 1]))
                ybf_tok = P.tok("scalar", "sE")
                tk["ln_done"] = ybf_tok

                P.wait("tensor", "sE", ybf_tok)
                for dt in range(DT):
                    P.wait("tensor", "vE", tk["pT_ev"])
                    for tt in range(NTA):
                        P.emit("tensor", lambda eng, dt=dt, tt=tt: eng.transpose(
                            pT[:, tt * 128:(tt + 1) * 128],
                            ybf[:, tt * D + dt * 128: tt * D + dt * 128 + 128],
                            ident[:]))
                    tr_tok = P.tok("tensor", "tE")
                    if dt == DT - 1:
                        tk["ybf_rd"] = tr_tok
                    P.wait("vector", "tE", tr_tok)
                    if dt == 0:
                        P.wait("vector", "tE", tk["yT_rd"])
                    P.emit("vector", lambda eng, dt=dt: eng.tensor_copy(
                        yT[:, dt * TCA:(dt + 1) * TCA], pT[:, 0:NTA * 128]))
                    tk["pT_ev"] = P.tok("vector", "vE")
                yT_tok = tk["pT_ev"]

                P.wait("tensor", "vE", yT_tok)
                P.wait("tensor", "dG", wload_tok)
                for tt in range(NTA):
                    P.wait("tensor", "vE", tk["lat_ev"])
                    for dt in range(DT):
                        P.emit("tensor", lambda eng, tt=tt, dt=dt: eng.matmul(
                            pL[:],
                            yT[:, dt * TCA + tt * 128: dt * TCA + (tt + 1) * 128],
                            wdkv_sb[:, dt * LAT:(dt + 1) * LAT],
                            start=(dt == 0), stop=(dt == DT - 1),
                            skip_group_check=True))
                    lm_tok = P.tok("tensor", "tE")
                    P.wait("vector", "tE", lm_tok)
                    P.wait("vector", "sE", tk["lsc_rd"])
                    P.emit("vector", lambda eng: eng.tensor_copy(lst[:], pL[:]))
                    tk["lat_ev"] = P.tok("vector", "vE")
                    P.wait("vector", "vE", tk["lat_ev"])
                    P.emit("vector", lambda eng: eng.tensor_reduce(
                        lm1[:], lst[:].rearrange("p (o l) -> p o l", o=1),
                        AX.X, ALU.add))
                    P.selfw("vector")
                    P.emit("vector", lambda eng: eng.tensor_scalar(
                        lm1[:], lm1[:], 1.0 / LAT, None, ALU.mult))
                    P.selfw("vector")
                    P.emit("vector", lambda eng: eng.tensor_scalar(
                        lsc[:], lst[:], lm1[:, 0:1], None, ALU.subtract))
                    t = P.tok("vector", "vE")
                    P.wait("scalar", "vE", t)
                    P.emit("scalar", lambda eng: eng.activation(lst[:], lsc[:],
                                                                AF.Square))
                    t = P.tok("scalar", "sE")
                    P.wait("vector", "sE", t)
                    P.emit("vector", lambda eng: eng.tensor_reduce(
                        lv1[:], lst[:].rearrange("p (o l) -> p o l", o=1),
                        AX.X, ALU.add))
                    P.selfw("vector")
                    P.emit("vector", lambda eng: eng.tensor_scalar(
                        lv1[:], lv1[:], 1.0 / LAT, LN_EPS, ALU.mult, ALU.add))
                    t = P.tok("vector", "vE")
                    P.wait("scalar", "vE", t)
                    P.emit("scalar", lambda eng: eng.activation(lr1[:], lv1[:],
                                                                AF.Sqrt))
                    t = P.tok("scalar", "sE")
                    P.wait("vector", "sE", t)
                    P.emit("vector", lambda eng: eng.reciprocal(lr1[:], lr1[:]))
                    t = P.tok("vector", "vE")
                    P.wait("scalar", "vE", t)
                    P.emit("scalar", lambda eng: eng.activation(
                        lst[:], lsc[:], AF.Copy, scale=lr1[:, 0:1]))
                    tk["lsc_rd"] = P.tok("scalar", "sE")
                    P.wait("vector", "sE", tk["lsc_rd"])
                    P.emit("vector", lambda eng: eng.tensor_tensor(
                        lsc[:], lst[:], lng_b[:], ALU.mult))
                    P.selfw("vector")
                    if tt == 0:
                        P.wait("vector", "tE", tk["lnT_rd"])
                    P.emit("vector", lambda eng: eng.tensor_tensor(
                        lnbf[:], lsc[:], lnb_b[:], ALU.add))
                    lnbf_tok = P.tok("vector", "vE")
                    P.wait("tensor", "vE", lnbf_tok)
                    P.wait("tensor", "vE", tk["pT_ev"])
                    for lt in range(2):
                        P.emit("tensor", lambda eng, lt=lt: eng.transpose(
                            pT[:, lt * 128:(lt + 1) * 128],
                            lnbf[:, lt * 128:(lt + 1) * 128], ident[:]))
                    t = P.tok("tensor", "tE")
                    P.wait("vector", "tE", t)
                    for lt in range(2):
                        P.emit("vector", lambda eng, lt=lt, tt=tt: eng.tensor_copy(
                            lnT[:, lt * TCA + tt * 128: lt * TCA + (tt + 1) * 128],
                            pT[:, lt * 128:(lt + 1) * 128]))
                    tk["pT_ev"] = P.tok("vector", "vE")
                lnT_tok = tk["pT_ev"]
                P.wait("tensor", "vE", lnT_tok)

                def qk_side(wsb, nacc, rhsT, cosb, sinb, scr, is_q, C=C):
                    for ot in range(8):
                        P.wait("tensor", "sE", tk["pQ_free"])
                        for at in range(nacc):
                            P.emit("tensor", lambda eng, ot=ot, at=at, w=wsb,
                                   r=rhsT: eng.matmul(
                                       pQ[:],
                                       w[:, at * D + ot * 128:
                                         at * D + ot * 128 + 128],
                                       r[:, at * TCA:(at + 1) * TCA],
                                       start=(at == 0), stop=(at == nacc - 1),
                                       skip_group_check=True))
                        mm_tok = P.tok("tensor", "tE")
                        P.wait("scalar", "tE", mm_tok)
                        P.wait("scalar", "vE", tk["qtf_rd"])
                        P.emit("scalar", lambda eng: eng.copy(qtf[:], pQ[:]))
                        ev_tok = P.tok("scalar", "sE")
                        tk["pQ_free"] = ev_tok
                        P.wait("vector", "sE", ev_tok)
                        P.emit("vector", lambda eng: eng.tensor_copy(qtb[:],
                                                                     qtf[:]))
                        cb_tok = P.tok("vector", "vE")
                        P.wait("tensor", "vE", cb_tok)
                        P.wait("tensor", "vE", tk["rot_rd"])
                        P.emit("tensor", lambda eng: eng.matmul(
                            pR[:], ptm[:], qtb[:],
                            start=True, stop=True, skip_group_check=True))
                        rot_tok = P.tok("tensor", "tE")
                        P.wait("vector", "tE", rot_tok)
                        P.emit("vector", lambda eng, cb=cosb: eng.tensor_tensor(
                            rt1[:], qtf[:], cb[:], ALU.mult))
                        P.selfw("vector")
                        P.emit("vector", lambda eng, sb=sinb: eng.tensor_tensor(
                            rt2[:], pR[:], sb[:], ALU.mult))
                        tk["rot_rd"] = P.tok("vector", "vE")
                        tk["qtf_rd"] = tk["rot_rd"]
                        P.wait("vector", "vE", tk["rot_rd"])
                        P.wait("vector", "sE", tk["sqb_rd"])
                        P.emit("vector", lambda eng: eng.tensor_tensor(
                            qrf[:], rt1[:], rt2[:], ALU.add))
                        P.selfw("vector")
                        P.emit("vector", lambda eng: eng.tensor_copy(qrb[:],
                                                                     qrf[:]))
                        qr_tok = P.tok("vector", "vE")
                        P.wait("scalar", "vE", qr_tok)
                        P.wait("scalar", "tE", tk.get("pN_mm"))
                        P.emit("scalar", lambda eng: eng.activation(
                            sqb[:], qrf[:], AF.Square))
                        sq2_tok = P.tok("scalar", "sE")
                        tk["sqb_rd"] = sq2_tok
                        P.wait("tensor", "sE", sq2_tok)
                        P.wait("tensor", "vE", tk["pN_ev"])
                        P.emit("tensor", lambda eng: eng.matmul(
                            pN[:], e2sb[:], sqb[:],
                            start=True, stop=True, skip_group_check=True))
                        n_tok = P.tok("tensor", "tE")
                        tk["pN_mm"] = n_tok
                        sb_ = sti[0] % 2
                        sti[0] += 1
                        P.wait("vector", "tE", n_tok)
                        if tk["stg_free"][sb_] is not None:
                            P.wait("vector", *tk["stg_free"][sb_])
                        P.emit("vector", lambda eng, sb=sb_: eng.tensor_copy(
                            st2[sb][:, 0:TCA], pN[:]))
                        tk["stg_ev"][sb_] = P.tok("vector", "vE")
                        tk["pN_ev"] = tk["stg_ev"][sb_]
                        P.wait("sync", "vE", qr_tok)
                        for hh in range(2):
                            P.emit("sync", lambda eng, ot=ot, s=scr, hh=hh:
                                   eng.dma_start(
                                       s[2 * ot + hh, 0:64,
                                         C * TCA:(C + 1) * TCA],
                                       qrb[hh * 64:(hh + 1) * 64, :]))
                            P.tok("sync", "dW", 16)
                        if is_q:
                            P.wait("gpsimd", "vE", tk["stg_ev"][sb_])
                            P.emit("gpsimd", lambda eng, sb=sb_, ot=ot:
                                   eng.dma_start(
                                       qn2[2 * ot:2 * ot + 2,
                                           C * TCA:(C + 1) * TCA],
                                       st2[sb][:, 0:TCA]))
                            tk["stg_free"][sb_] = ("dG",
                                                   P.tok("gpsimd", "dG", 16))
                        else:
                            P.wait("vector", "vE", tk["stg_ev"][sb_])
                            P.emit("vector", lambda eng, sb=sb_:
                                   eng.tensor_reduce(
                                       st2[sb][:, 300:301], st2[sb][:, 0:TCA],
                                       AX.X, ALU.max))
                            P.selfw("vector")
                            P.emit("vector", lambda eng, sb=sb_, ot=ot:
                                   eng.tensor_tensor(
                                       km2[:, ot:ot + 1], km2[:, ot:ot + 1],
                                       st2[sb][:, 300:301], ALU.max))
                            P.selfw("vector")
                            tk["stg_free"][sb_] = ("vE", P.cur("vE"))

                qk_side(wq_sb, DT, yT, csq, snq, qtr, True)
                qk_side(wuk_sb, 2, lnT, csk, snk, ktr, False)

                for tt in range(NTA):
                    P.wait("tensor", "vE", tk["vout"])
                    for h2 in range(2):
                        for lt in range(2):
                            P.emit("tensor", lambda eng, tt=tt, h2=h2, lt=lt:
                                   eng.matmul(
                                       pV[:, h2 * 512:(h2 + 1) * 512],
                                       lnT[:, lt * TCA + tt * 128:
                                           lt * TCA + (tt + 1) * 128],
                                       wuv_sb[:, lt * D + h2 * 512:
                                              lt * D + h2 * 512 + 512],
                                       start=(lt == 0), stop=(lt == 1),
                                       skip_group_check=True))
                    v_tok = P.tok("tensor", "tE")
                    P.wait("vector", "tE", v_tok)
                    tg = C * NTA + tt
                    for h in range(NH):
                        P.emit("vector", lambda eng, h=h, tg=tg: eng.tensor_copy(
                            v65[:, tg * 1040 + 65 * h: tg * 1040 + 65 * h + 64],
                            pV[:, h * 64:(h + 1) * 64]))
                    tk["vout"] = P.tok("vector", "vE")
                tk["yT_rd"] = P.cur("tE")
                tk["lnT_rd"] = P.cur("tE")
                tk["cs_rd"] = P.cur("vE")
                P.flush()

            P.begin()
            barrier()
            P.emit("gpsimd", lambda eng: eng.dma_start(A["kmxs"], km2[:]))
            t = P.tok("gpsimd", "dG", 16)
            P.wait("gpsimd", "dG", t)
            P.emit("gpsimd", lambda eng: eng.dma_start(
                kmx[:, 0:1], A["kmxs"].rearrange("c p -> (c p)")[:, None]))
            t = P.tok("gpsimd", "dG", 16)
            P.wait("scalar", "dG", t)
            P.emit("scalar", lambda eng: eng.activation(kmx2[:], kmx[:], AF.Sqrt))
            P.selfw("scalar")
            P.emit("scalar", lambda eng: eng.activation(Bsb[:], qn2[:], AF.Sqrt))
            b_tok = P.tok("scalar", "sE")
            P.wait("vector", "sE", b_tok)
            P.emit("vector", lambda eng: eng.tensor_scalar(
                Bsb[:], Bsb[:], kmx2[:, 0:1], -1.0, ALU.mult, ALU.mult))
            t = P.tok("vector", "vE")
            P.wait("gpsimd", "vE", t)
            P.emit("gpsimd", lambda eng: eng.dma_start(qtr[:, 64, :], Bsb[:]))
            P.tok("gpsimd", "dG", 16)
            P.emit("vector", lambda eng: eng.memset(ones16[:], 1.0))
            t = P.tok("vector", "vE")
            P.wait("sync", "vE", t)
            P.emit("sync", lambda eng: eng.dma_start(ktr[:, 64, :], ones16[:]))
            P.tok("sync", "dW", 16)
            P.flush()

        avT = EO(nc.sbuf_tensor("avT", [128, DT * T], BF16))
        with ExitStack() as SB:
            EB = SB.enter_context
            ktb = [EB(nc.sbuf_tensor("ktb%d" % i, [65, T], BF16))
                   for i in range(2)]
            qhb = [EB(nc.sbuf_tensor("qhb%d" % i, [65, T], BF16))
                   for i in range(2)]
            expt = [EB(nc.sbuf_tensor("expt%d" % i, [128, 512], BF16))
                    for i in range(4)]
            tms = EB(nc.sbuf_tensor("tms", [128, 4 * 512], BF16))
            recb = EB(nc.sbuf_tensor("recb", [64, 512], F32))
            rec1 = EB(nc.sbuf_tensor("rec1", [1, 512], F32))
            on64 = EB(nc.sbuf_tensor("on64", [1, 64], F32))
            pS = [EB(nc.psum_tensor("pS%d" % i, [128, 512], F32))
                  for i in range(3)]
            pAV = [EB(nc.psum_tensor("pAV%d" % i, [65, 512], F32))
                   for i in range(2)]
            pBC = EB(nc.psum_tensor("pBC", [64, 512], F32))

            P.begin()
            barrier()
            P.emit("gpsimd", lambda eng: eng.dma_start(
                tms[:].rearrange("p (m c) -> p m c", m=4),
                A["trimask"].rearrange("m p c -> p m c")))
            P.tok("gpsimd", "dG", 16)
            P.emit("gpsimd", lambda eng: eng.dma_start(on64[:], A["ones64"]))
            msk_tok = P.tok("gpsimd", "dG", 16)
            P.wait("vector", "dG", msk_tok)
            P.wait("tensor", "dG", msk_tok)
            P.flush()

            bt = dict(hload=[None, None], hdone=[None, None],
                      exp={}, mm2={}, expdone={}, pav_ev=[None, None],
                      bc_done=None, recb_ev=None)
            jg = [0]
            for h in range(NH):
                P.begin()
                hb = h % 2
                P.wait("sync", "tE", bt["hdone"][hb])
                P.emit("sync", lambda eng, h=h, hb=hb: eng.dma_start(
                    ktb[hb][:], ktr[h]))
                P.tok("sync", "dW", 16)
                P.emit("sync", lambda eng, h=h, hb=hb: eng.dma_start(
                    qhb[hb][:], qtr[h]))
                bt["hload"][hb] = P.tok("sync", "dW", 16)
                P.wait("tensor", "dW", bt["hload"][hb])
                for tqc in range(8):
                    pv = tqc % 2
                    P.wait("tensor", "vE", bt["pav_ev"][pv])
                    njt = 4 * tqc + 4

                    def mm2_emit(j, g, eb, pv=pv, h=h, njt=njt):
                        sem, val = bt["expdone"][g]
                        P.wait("tensor", sem, val)
                        P.emit("tensor", lambda eng, j=j, eb=eb, pv=pv, h=h,
                               njt=njt: eng.matmul(
                                   pAV[pv][0:65, :],
                                   v65[:, j * 1040 + 65 * h:
                                       j * 1040 + 65 * h + 65],
                                   expt[eb][:],
                                   start=(j == 0), stop=(j == njt - 1),
                                   skip_group_check=True))
                        bt["mm2"][g] = P.tok("tensor", "tE")

                    pend = None
                    for j in range(njt):
                        g = jg[0]
                        jg[0] += 1
                        ps = g % 3
                        eb = g % 4
                        ed = bt["expdone"].get(g - 3)
                        if ed is not None:
                            P.wait("tensor", *ed)
                        P.emit("tensor", lambda eng, hb=hb, j=j, tqc=tqc, ps=ps:
                               eng.matmul(
                                   pS[ps][:],
                                   ktb[hb][:, j * 128:(j + 1) * 128],
                                   qhb[hb][:, tqc * 512:(tqc + 1) * 512],
                                   start=True, stop=True,
                                   skip_group_check=True))
                        s_tok = P.tok("tensor", "tE")
                        P.wait("scalar", "tE", s_tok)
                        P.wait("scalar", "tE", bt["mm2"].get(g - 4))
                        P.emit("scalar", lambda eng, ps=ps, eb=eb:
                               eng.activation(expt[eb][:], pS[ps][:], AF.Exp))
                        e_tok = P.tok("scalar", "sE")
                        m = j - 4 * tqc
                        if m >= 0:
                            P.wait("vector", "sE", e_tok)
                            P.emit("vector", lambda eng, eb=eb, m=m:
                                   eng.tensor_tensor(
                                       expt[eb][:], expt[eb][:],
                                       tms[:, m * 512:(m + 1) * 512], ALU.mult))
                            bt["expdone"][g] = ("vE", P.tok("vector", "vE"))
                        else:
                            bt["expdone"][g] = ("sE", e_tok)
                        if pend is not None:
                            mm2_emit(*pend)
                        pend = (j, g, eb)
                    mm2_emit(*pend)
                    av_tok = bt["mm2"][jg[0] - 1]
                    P.wait("vector", "tE", av_tok)
                    P.wait("vector", "tE", bt["bc_done"])
                    P.emit("vector", lambda eng, pv=pv: eng.reciprocal(
                        rec1[:], pAV[pv][64:65, :]))
                    r_tok = P.tok("vector", "vE")
                    P.wait("tensor", "vE", r_tok)
                    P.emit("tensor", lambda eng: eng.matmul(
                        pBC[:], on64[:], rec1[:],
                        start=True, stop=True, skip_group_check=True))
                    bc_tok = P.tok("tensor", "tE")
                    bt["bc_done"] = bc_tok
                    P.wait("vector", "tE", bc_tok)
                    P.wait("vector", "vE", bt["recb_ev"])
                    P.emit("vector", lambda eng: eng.tensor_copy(recb[:], pBC[:]))
                    P.selfw("vector")
                    P.emit("vector", lambda eng, h=h, tqc=tqc, pv=pv:
                           eng.tensor_tensor(
                               avT[(h % 2) * 64:(h % 2) * 64 + 64,
                                   (h // 2) * T + tqc * 512:
                                   (h // 2) * T + (tqc + 1) * 512],
                               pAV[pv][0:64, :], recb[:], ALU.mult))
                    bt["pav_ev"][pv] = P.tok("vector", "vE")
                    bt["recb_ev"] = bt["pav_ev"][pv]
                bt["hdone"][hb] = P.cur("tE")
                P.flush()

        with ExitStack() as SC:
            ECp = SC.enter_context
            wo_sb = ECp(nc.sbuf_tensor("wo_sb", [128, DT * D], BF16))
            x1b = [ECp(nc.sbuf_tensor("x1b%d" % i, [128, D], F32))
                   for i in range(2)]
            x2st = [ECp(nc.sbuf_tensor("x2st%d" % i, [128, D], F32))
                    for i in range(2)]
            pF = [ECp(nc.psum_tensor("pF%d" % i, [128, 1024], F32))
                  for i in range(2)]
            P.begin()
            barrier()
            P.emit("gpsimd", lambda eng: eng.dma_start(
                wo_sb[:].rearrange("p (dt o) -> p dt o", dt=DT),
                A["wo"].rearrange("(dt p) o -> p dt o", p=128)))
            wo_tok = P.tok("gpsimd", "dG", 16)
            P.wait("tensor", "dG", wo_tok)
            ct = dict(out=[None, None], store=[None, None], xload=[None, None])
            for tt in range(32):
                b = tt % 2
                P.wait("gpsimd", "vE", ct["out"][b])
                P.emit("gpsimd", lambda eng, tt=tt, b=b: eng.dma_start(
                    x1b[b][:], x1t[tt]))
                ct["xload"][b] = P.tok("gpsimd", "dG", 16)
                P.wait("tensor", "vE", ct["out"][b])
                for dt in range(DT):
                    for h2 in range(2):
                        P.emit("tensor", lambda eng, tt=tt, dt=dt, h2=h2, b=b:
                               eng.matmul(
                                   pF[b][:, h2 * 512:(h2 + 1) * 512],
                                   avT[:, dt * T + tt * 128:
                                       dt * T + (tt + 1) * 128],
                                   wo_sb[:, dt * D + h2 * 512:
                                         dt * D + h2 * 512 + 512],
                                   start=(dt == 0), stop=(dt == DT - 1),
                                   skip_group_check=True))
                f_tok = P.tok("tensor", "tE")
                P.wait("vector", "tE", f_tok)
                P.wait("vector", "dG", ct["xload"][b])
                P.wait("vector", "dG", ct["store"][b])
                P.emit("vector", lambda eng, b=b: eng.tensor_tensor(
                    x2st[b][:], x1b[b][:], pF[b][:], ALU.add))
                ct["out"][b] = P.tok("vector", "vE")
                P.wait("gpsimd", "vE", ct["out"][b])
                P.emit("gpsimd", lambda eng, tt=tt, b=b: eng.dma_start(
                    x2t[tt], x2st[b][:]))
                ct["store"][b] = P.tok("gpsimd", "dG", 16)
            P.flush()



def build_program(upto="full", debug=False):
    nc = bass.Bass("TRN2", target_bir_lowering=False)
    dt_ = nc.dram_tensor
    x = dt_("x", [T, D], F32, kind="ExternalInput").ap()
    w1t = dt_("w1t", [E, D, F], F32, kind="ExternalInput").ap()
    w2t = dt_("w2t", [E, F, D], F32, kind="ExternalInput").ap()
    wq = dt_("wq", [D, D], F32, kind="ExternalInput").ap()
    wdkv = dt_("wdkv", [D, LAT], F32, kind="ExternalInput").ap()
    wuk = dt_("wuk", [LAT, D], F32, kind="ExternalInput").ap()
    wuv = dt_("wuv", [LAT, D], F32, kind="ExternalInput").ap()
    wo = dt_("wo", [D, D], F32, kind="ExternalInput").ap()
    lng = dt_("lng", [128, LAT], F32, kind="ExternalInput").ap()
    lnb = dt_("lnb", [128, LAT], F32, kind="ExternalInput").ap()
    normw = dt_("normw", [128, D], F32, kind="ExternalInput").ap()
    wmt = dt_("wmt", [D, 12], F32, kind="ExternalInput").ap()
    identity = dt_("identity", [128, 128], BF16, kind="ExternalInput").ap()
    ptmat = dt_("ptmat", [128, 128], BF16, kind="ExternalInput").ap()
    cosq = dt_("cosq", [128, T], F32, kind="ExternalInput").ap()
    sinq = dt_("sinq", [128, T], F32, kind="ExternalInput").ap()
    cosk = dt_("cosk", [128, T], F32, kind="ExternalInput").ap()
    sink = dt_("sink", [128, T], F32, kind="ExternalInput").ap()
    trimask = dt_("trimask", [4, 128, 512], BF16, kind="ExternalInput").ap()
    e2mat = dt_("e2mat", [128, 2], BF16, kind="ExternalInput").ap()
    ones64 = dt_("ones64", [1, 64], F32, kind="ExternalInput").ap()
    y = dt_("y", [T, D], F32, kind="ExternalOutput").ap()

    kmxs = dt_("kmxs", [8, 2], F32).ap()
    w1b = dt_("w1b", [E, D, F], BF16).ap()
    w2b = dt_("w2b", [E, F, D], BF16).ap()
    x1s = dt_("x1s", [T, D], F32).ap()
    x2s = dt_("x2s", [T, D], F32).ap()
    qtr = dt_("qtr", [NH, 65, T], BF16).ap()
    ktr = dt_("ktr", [NH, 65, T], BF16).ap()
    _ = (wq, wdkv, wuk, wuv, wo, lng, lnb, cosq, sinq, cosk, sink, trimask,
         ptmat, x2s, qtr, ktr)

    with ExitStack() as stack:
        EC = stack.enter_context
        P = Prog(nc, stack)
        P.debug = debug
        P.dbg_tensors = {}

        def dbg_dump(name, sb_ap, shape, dtype):
            if not P.debug or name in P.dbg_tensors:
                return
            t = nc.dram_tensor("dbg_" + name, list(shape), dtype,
                               kind="ExternalOutput").ap()
            P.dbg_tensors[name] = t
            for s_ in ("vE", "sE", "tE"):
                P.wait("gpsimd", s_, P.cur(s_))
            P.emit("gpsimd", lambda eng, d=t, s=sb_ap: eng.dma_start(d, s))
            tk = P.tok("gpsimd", "dG", 16)
            for e_ in ("vector", "scalar", "tensor", "sync"):
                P.wait(e_, "dG", tk)
        P.dbg_dump = dbg_dump

        ident = EC(nc.sbuf_tensor("identsb", [128, 128], BF16))
        ptm = EC(nc.sbuf_tensor("ptmsb", [128, 128], BF16))
        e2sb = EC(nc.sbuf_tensor("e2sb", [128, 2], BF16))
        normw_b = EC(nc.sbuf_tensor("normw_b", [128, D], F32))
        wmt_sb = EC(nc.sbuf_tensor("wmt_sb", [128, DT * 12], BF16))

        P.begin()
        P.emit("gpsimd", lambda eng: eng.dma_start(ident[:], identity))
        P.tok("gpsimd", "dG", 16)
        P.emit("gpsimd", lambda eng: eng.dma_start(ptm[:], ptmat))
        P.tok("gpsimd", "dG", 16)
        P.emit("gpsimd", lambda eng: eng.dma_start(e2sb[:], e2mat))
        P.tok("gpsimd", "dG", 16)
        P.emit("gpsimd", lambda eng: eng.dma_start(normw_b[:], normw))
        P.tok("gpsimd", "dG", 16)
        P.emit("gpsimd", lambda eng: eng.dma_start(
            wmt_sb[:].rearrange("p (dt c) -> p dt c", c=12),
            wmt.rearrange("(dt p) c -> p dt c", p=128)))
        P.tok("gpsimd", "dG", 16)
        P.flush()

        def alloc_ffn(EC, sfx=""):
            nm = lambda s: s + sfx
            return dict(
            xst=EC(nc.sbuf_tensor(nm("xst"), [128, NT1 * D], F32)),
            zC=EC(nc.sbuf_tensor(nm("zC"), [128, NT1 * D], F32)),
            sqst=EC(nc.sbuf_tensor(nm("sqst"), [128, NT1 * D], F32)),
            znbf=EC(nc.sbuf_tensor(nm("znbf"), [128, NT1 * D], BF16)),
            znT=EC(nc.sbuf_tensor(nm("znT"), [128, DT * T1], BF16)),
            hsT=[EC(nc.sbuf_tensor(nm("hsT0"), [128, NFT * T1], BF16)),
                 EC(nc.sbuf_tensor(nm("hsT1"), [128, NFT * T1], BF16))],
            w1sb=[EC(nc.sbuf_tensor(nm("w1sb0"), [128, DT * 1024], BF16)),
                  EC(nc.sbuf_tensor(nm("w1sb1"), [128, DT * 1024], BF16))],
            w2sb=[EC(nc.sbuf_tensor(nm("w2sb0"), [128, NFT * 1024], BF16)),
                  EC(nc.sbuf_tensor(nm("w2sb1"), [128, NFT * 1024], BF16))],
            mu4=EC(nc.sbuf_tensor(nm("mu4"), [128, NT1], F32)),
            v4=EC(nc.sbuf_tensor(nm("v4"), [128, NT1], F32)),
            rstd4=EC(nc.sbuf_tensor(nm("rstd4"), [128, NT1], F32)),
            rt=EC(nc.sbuf_tensor(nm("rt"), [128, NT1 * 12], F32)),
            cC=EC(nc.sbuf_tensor(nm("cCb"), [128, NT1 * 8], F32)),
            r_max=EC(nc.sbuf_tensor(nm("r_max"), [128, NT1], F32)),
            r_me=EC(nc.sbuf_tensor(nm("r_me"), [128, NT1 * 4], F32)),
            r_sc1=EC(nc.sbuf_tensor(nm("r_sc1"), [128, NT1], F32)),
            r_cnt=EC(nc.sbuf_tensor(nm("r_cnt"), [128, NT1 * 4], F32)),
            r_tmp=EC(nc.sbuf_tensor(nm("r_tmp"), [128, NT1 * 4], F32)),
            r_ev=EC(nc.sbuf_tensor(nm("r_ev"), [128, NT1 * 4], F32)),
            r_od=EC(nc.sbuf_tensor(nm("r_od"), [128, NT1 * 4], F32)),
            r_lv=EC(nc.sbuf_tensor(nm("r_lv"), [128, NT1 * 4], F32)),
            r_xa=EC(nc.sbuf_tensor(nm("r_xa"), [128, NT1 * 4], F32)),
            r_xb=EC(nc.sbuf_tensor(nm("r_xb"), [128, NT1 * 4], F32)),
            ph=[EC(nc.psum_tensor(nm("ph0"), [128, T1], F32)),
                EC(nc.psum_tensor(nm("ph1"), [128, T1], F32))],
            py=[EC(nc.psum_tensor(nm("py0"), [128, 1024], F32)),
                EC(nc.psum_tensor(nm("py1"), [128, 1024], F32))],
            ptr=EC(nc.psum_tensor(nm("ptrb"), [128, NT1 * 128], BF16)),
            pr=EC(nc.psum_tensor(nm("prb"), [128, NT1 * 12], F32)),
            normw_b=normw_b, wmt_sb=wmt_sb, ident=ident,
            w1b4=[w1b[e].rearrange("(dt p) f -> p dt f", p=128) for e in range(E)],
            w2b4=[w2b[e].rearrange("(ft p) d -> p ft d", p=128) for e in range(E)],
            state=dict(
                g_w1=0, g_w2=0, e_seen=0, ph_i=0, py_i=0,
                next_w1=0, next_w2=0,
                w1_tok={}, w2_tok={}, h_done={}, y_done={},
                silu_tok={}, stt_tok={}, expert_map={},
                znbf_rd_tok=None, znT_rd_tok=None,
                ptr_ev_tok=None, pr_ev_tok=None,
                zC_sc_tok=None, z_upd_tok=None, sq_free=None, route_tok=None,
                xst_store_tok=None, xst_free_tok=None,
                first_sync_wait=None, xst_guard_dW=None,
            ),
        )

        ffn1_stack = ExitStack()
        ffnbufs = alloc_ffn(ffn1_stack.enter_context)
        st = ffnbufs["state"]
        st["expert_of"] = lambda ei: ei % E

        P.begin()
        cbuf = ffnbufs["w2sb"]
        i = 0
        store_tok = {}
        for (srcm, dstm, R, Cc) in ((w1t, w1b, D, F), (w2t, w2b, F, D)):
            s4 = srcm.rearrange("e (rt p) c -> e p rt c", p=128)
            d4 = dstm.rearrange("e (rt p) c -> e p rt c", p=128)
            nrt = R // 128
            for e in range(E):
                b = i % 2
                P.wait("gpsimd", "dW", store_tok.get(i - 2))
                sb3 = cbuf[b][:, 0:nrt * Cc].rearrange("p (rt c) -> p rt c", c=Cc)
                P.emit("gpsimd", lambda eng, s=s4[e], d=sb3: eng.dma_start(d, s))
                ld = P.tok("gpsimd", "dG", 16)
                P.wait("sync", "dG", ld)
                P.emit("sync", lambda eng, s=sb3, d=d4[e]: eng.dma_start(d, s))
                store_tok[i] = P.tok("sync", "dW", 16)
                i += 1
        cast_done_dW = P.cur("dW")
        P.flush()
        st["first_sync_wait"] = cast_done_dW
        st["xst_guard_dW"] = cast_done_dW


        dst1 = y if upto == "ffn1" else x1s
        emit_ffn_sublayer(P, ffnbufs, x, dst1)
        ffn1_stack.close()
        if upto != "ffn1":
            x2dst = y if upto == "attn" else x2s
            A = dict(x1s=x1s, x2s=x2dst, qtr=qtr, ktr=ktr, kmxs=kmxs,
                     wq=wq, wdkv=wdkv, wuk=wuk, wuv=wuv, wo=wo,
                     lng=lng, lnb=lnb, cosq=cosq, sinq=sinq, cosk=cosk,
                     sink=sink, trimask=trimask, ones64=ones64,
                     ident=ident, ptm=ptm, e2sb=e2sb)
            emit_attention(P, nc, A)
            if upto == "attn":
                P.begin()
                P.wait("gpsimd", "dG", P.cur("dG"))
                P.wait("sync", "dW", P.cur("dW"))
                P.flush()
                return nc
            P.begin()
            for e_ in ENGS:
                for s_ in ("vE", "sE", "tE", "dG", "dW"):
                    P.wait(e_, s_, P.cur(s_))
            P.flush()
            ffn2_stack = ExitStack()
            fb2 = alloc_ffn(ffn2_stack.enter_context, sfx="_f2")
            stf2 = fb2["state"]
            stf2["expert_of"] = lambda ei: ei % E
            emit_ffn_sublayer(P, fb2, x2s, y)
            ffn2_stack.close()

        P.begin()
        P.wait("gpsimd", "dG", P.cur("dG"))
        P.wait("sync", "dW", P.cur("dW"))
        P.flush()
    return nc



def _host_constants():
    c = {}
    c["identity"] = np.eye(128, dtype=ml_dtypes.bfloat16)
    Pm = np.zeros((128, 128), dtype=np.float32)
    for b in (0, 64):
        for i in range(32):
            Pm[b + i, b + 32 + i] = -1.0
            Pm[b + 32 + i, b + i] = 1.0
    c["ptmat"] = Pm.T.astype(ml_dtypes.bfloat16)
    pos = np.arange(T, dtype=np.float64)
    inv = 1.0 / (10000.0 ** (np.arange(32, dtype=np.float64) / 32.0))
    ang = pos[None, :] * inv[:, None]
    cos128 = np.tile(np.cos(ang), (4, 1)).astype(np.float32)
    sin128 = np.tile(np.sin(ang), (4, 1)).astype(np.float32)
    c["cosk"], c["sink"] = cos128, sin128
    c["cosq"] = (cos128 * 0.125).astype(np.float32)
    c["sinq"] = (sin128 * 0.125).astype(np.float32)
    e2 = np.zeros((128, 2), dtype=np.float32)
    e2[0:64, 0] = 1.0
    e2[64:128, 1] = 1.0
    c["e2mat"] = e2.astype(ml_dtypes.bfloat16)
    c["ones64"] = np.ones((1, 64), dtype=np.float32)
    tm = np.zeros((4, 128, 512), dtype=np.float32)
    r = np.arange(128)[:, None]
    cidx = np.arange(512)[None, :]
    for m in range(4):
        tm[m] = np.where(128 * m + r <= cidx, 1.0, 0.0)
    c["trimask"] = tm.astype(ml_dtypes.bfloat16)
    return c


def kernel(L0_attn_x, L0_attn_W_q, L0_attn_W_dkv, L0_attn_W_uk, L0_attn_W_uv,
           L0_attn_W_o, L0_attn_ln_gamma, L0_attn_ln_beta, L0_ffn_norm_w,
           L0_ffn_w_macro, L0_ffn_w_micro, L0_ffn_w1, L0_ffn_w2, _upto="full",
           _debug=False):
    xs = np.ascontiguousarray(np.asarray(L0_attn_x, dtype=np.float32))
    w1t = np.ascontiguousarray(np.asarray(L0_ffn_w1, np.float32).transpose(0, 2, 1))
    w2t = np.ascontiguousarray(np.asarray(L0_ffn_w2, np.float32).transpose(0, 2, 1))
    wmt = np.ascontiguousarray(np.concatenate(
        [np.asarray(L0_ffn_w_macro, np.float32).T,
         np.asarray(L0_ffn_w_micro, np.float32).reshape(8, D).T], axis=1))
    consts = _host_constants()
    shared = dict(
        w1t=w1t, w2t=w2t,
        wq=np.ascontiguousarray(np.asarray(L0_attn_W_q, np.float32)),
        wdkv=np.ascontiguousarray(np.asarray(L0_attn_W_dkv, np.float32)),
        wuk=np.ascontiguousarray(np.asarray(L0_attn_W_uk, np.float32)),
        wuv=np.ascontiguousarray(np.asarray(L0_attn_W_uv, np.float32)),
        wo=np.ascontiguousarray(np.asarray(L0_attn_W_o, np.float32)),
        lng=np.tile(np.asarray(L0_attn_ln_gamma, np.float32).reshape(1, LAT),
                    (128, 1)),
        lnb=np.tile(np.asarray(L0_attn_ln_beta, np.float32).reshape(1, LAT),
                    (128, 1)),
        normw=np.tile(np.asarray(L0_ffn_norm_w, np.float32).reshape(1, D),
                      (128, 1)),
        wmt=wmt, **consts)
    nc = build_program(upto=_upto, debug=_debug)
    in_maps = [dict(shared, x=np.ascontiguousarray(xs[i])) for i in range(8)]
    res = run_bass_kernel_spmd(nc, in_maps, core_ids=list(range(8)))
    out = np.stack([np.asarray(r["y"], np.float32) for r in res.results], axis=0)
    if _debug:
        return out, res.results
    return out


# version 07 (final)
# speedup vs baseline: 1.2782x; 1.0108x over previous
from contextlib import ExitStack

import numpy as np
import ml_dtypes

import concourse.bass as bass
import concourse.mybir as mybir
from concourse.bass_utils import run_bass_kernel_spmd

F32 = mybir.dt.float32
BF16 = mybir.dt.bfloat16
AF = mybir.ActivationFunctionType
ALU = mybir.AluOpType
AX = mybir.AxisListType

T, D = 4096, 1024
DT = 8
T1 = 512
NT1 = T1 // 128
NCH = T // T1
E, F = 8, 2048
NFT = F // 128
LAT = 256
NH, HD = 16, 64
LN_EPS = 1e-5
RMS_EPS = 1e-6
EPS9 = 1e-9

ENGS = ("sync", "tensor", "vector", "scalar", "gpsimd")


class Prog:

    def __init__(self, nc, stack):
        self.nc = nc
        self.q = None
        self.sem = {}
        for name in ("dW", "dG", "tE", "vE", "sE"):
            h = stack.enter_context(nc.semaphore(name))
            self.sem[name] = [h, 0]

    def begin(self):
        self.q = {e: [] for e in ENGS}

    def flush(self):
        with self.nc.Block() as blk:
            for e in ENGS:
                ops = self.q[e]
                if ops:
                    def runner(eng, ops=ops):
                        for f in ops:
                            f(eng)
                    getattr(blk, e)(runner)
        self.q = None

    def emit(self, eng, fn):
        self.q[eng].append(fn)

    def tok(self, eng, sem, k=1):
        s = self.sem[sem]
        s[1] += k
        v, h = s[1], s[0]
        prev = self.q[eng][-1]
        self.q[eng][-1] = lambda eng_, p=prev, h=h, k=k: p(eng_).then_inc(h, k)
        return v

    def wait(self, eng, sem, val):
        if val is None or val <= 0:
            return
        h = self.sem[sem][0]
        self.q[eng].append(lambda eng_, h=h, v=val: eng_.wait_ge(h, v))

    def cur(self, sem):
        return self.sem[sem][1]

    OWN = {"vector": "vE", "scalar": "sE", "tensor": "tE"}

    def selfw(self, eng):
        t = self.tok(eng, self.OWN[eng])
        self.wait(eng, self.OWN[eng], t)
        return t



def emit_ffn_sublayer(P, B, src_ap, dst_ap):
    src4 = src_ap.rearrange("(c tt p) d -> c p tt d", tt=NT1, p=128)
    dst4 = dst_ap.rearrange("(c tt p) d -> c p tt d", tt=NT1, p=128)

    xst, zC, sqst, znbf, znT = B["xst"], B["zC"], B["sqst"], B["znbf"], B["znT"]
    hsT, w1sb, w2sb = B["hsT"], B["w1sb"], B["w2sb"]
    mu4, v4, rstd4, rt, cC = B["mu4"], B["v4"], B["rstd4"], B["rt"], B["cC"]
    normw_b, wmt_sb, ident = B["normw_b"], B["wmt_sb"], B["ident"]
    ph, py, ptr_, pr = B["ph"], B["py"], B["ptr"], B["pr"]
    w1b4, w2b4 = B["w1b4"], B["w2b4"]
    st = B["state"]
    x3 = xst[:].rearrange("p (tt d) -> p tt d", tt=NT1)
    z3 = zC[:].rearrange("p (tt d) -> p tt d", tt=NT1)
    zn3 = znbf[:].rearrange("p (tt d) -> p tt d", tt=NT1)
    znsc = sqst

    def pump_w2(upto):
        while st["next_w2"] <= upto:
            ei = st["next_w2"]
            st["next_w2"] += 1
            if st["first_sync_wait"] is not None:
                P.wait("sync", "dW", st["first_sync_wait"])
                st["first_sync_wait"] = None
            P.wait("sync", "tE", st["y_done"].get(ei - 2))
            e = st["expert_of"](ei)
            dstb = w2sb[ei % 2][:].rearrange("p (ft d) -> p ft d", ft=NFT)
            P.emit("sync", lambda eng, s=w2b4[e], d=dstb: eng.dma_start(d, s))
            st["w2_tok"][ei] = P.tok("sync", "dW", 16)

    def pump_w1(upto):
        while st["next_w1"] <= upto:
            g = st["next_w1"]
            st["next_w1"] += 1
            if st["first_sync_wait"] is not None:
                P.wait("sync", "dW", st["first_sync_wait"])
                st["first_sync_wait"] = None
            P.wait("sync", "tE", st["h_done"].get(g - 2))
            e = st["expert_of"](g // 2)
            half = g % 2
            src = w1b4[e][:, :, half * 1024:(half + 1) * 1024]
            dstb = w1sb[g % 2][:].rearrange("p (dt f) -> p dt f", dt=DT)
            P.emit("sync", lambda eng, s=src, d=dstb: eng.dma_start(d, s))
            st["w1_tok"][g] = P.tok("sync", "dW", 16)

    def stats(eps):
        P.wait("scalar", "vE", st["sq_free"])
        P.wait("scalar", "sE", st["zC_sc_tok"])
        P.emit("scalar", lambda eng: eng.activation(sqst[:], zC[:], AF.Square))
        t = P.tok("scalar", "sE")
        P.wait("vector", "sE", t)
        P.emit("vector", lambda eng: eng.tensor_reduce(
            v4[:], sqst[:].rearrange("p (tt d) -> p tt d", tt=NT1), AX.X, ALU.add))
        P.selfw("vector")
        P.emit("vector", lambda eng, eps=eps: eng.tensor_scalar(
            v4[:], v4[:], 1.0 / D, eps, ALU.mult, ALU.add))
        st["sq_free"] = P.tok("vector", "vE")
        P.wait("scalar", "vE", st["sq_free"])
        P.emit("scalar", lambda eng: eng.activation(rstd4[:], v4[:], AF.Sqrt))
        t = P.tok("scalar", "sE")
        P.wait("vector", "sE", t)
        P.emit("vector", lambda eng: eng.reciprocal(rstd4[:], rstd4[:]))
        return P.tok("vector", "vE")

    for C in range(NCH):
        P.begin()
        pump_w2(st["e_seen"] + 1)
        pump_w1(2 * st["e_seen"] + 1)

        P.wait("gpsimd", "dG", st["xst_store_tok"])
        P.wait("gpsimd", "dW", st["xst_guard_dW"])
        st["xst_guard_dW"] = None
        P.wait("gpsimd", "vE", st["xst_free_tok"])
        xv = xst[:].rearrange("p (tt d) -> p tt d", tt=NT1)
        P.emit("gpsimd", lambda eng, s=src4[C], d=xv: eng.dma_start(d, s))
        x_tok = P.tok("gpsimd", "dG", 16)

        P.wait("vector", "dG", x_tok)
        P.emit("vector", lambda eng: eng.tensor_reduce(mu4[:], x3, AX.X, ALU.add))
        P.selfw("vector")
        P.emit("vector", lambda eng: eng.tensor_scalar(
            mu4[:], mu4[:], 1.0 / D, None, ALU.mult))
        P.selfw("vector")
        P.wait("vector", "sE", st["zC_sc_tok"])
        for tt in range(NT1):
            P.emit("vector", lambda eng, tt=tt: eng.tensor_scalar(
                z3[:, tt], x3[:, tt], mu4[:, tt:tt + 1], None, ALU.subtract))
        xc_tok = P.tok("vector", "vE")
        P.wait("scalar", "vE", xc_tok)
        rs_tok = stats(LN_EPS)
        P.wait("scalar", "vE", rs_tok)
        for tt in range(NT1):
            P.emit("scalar", lambda eng, tt=tt: eng.activation(
                z3[:, tt], z3[:, tt], AF.Copy, scale=rstd4[:, tt:tt + 1]))
        st["zC_sc_tok"] = P.tok("scalar", "sE")
        if C == 0 and getattr(P, "debug", False):
            P.wait("gpsimd", "sE", st["zC_sc_tok"])
            P.dbg_dump("zC_ln", zC[:], (128, NT1 * D), F32)

        for L in range(2):
            if L == 1:
                P.wait("scalar", "vE", st["z_upd_tok"])
            rs_tok = stats(RMS_EPS)
            P.wait("scalar", "vE", rs_tok)
            for tt in range(NT1):
                P.wait("scalar", "vE", st["sq_free"])
                P.emit("scalar", lambda eng, tt=tt: eng.activation(
                    znsc[:, 0:D], z3[:, tt], AF.Copy, scale=rstd4[:, tt:tt + 1]))
                sc_tok = P.tok("scalar", "sE")
                st["zC_sc_tok"] = sc_tok
                P.wait("vector", "sE", sc_tok)
                if tt == 0:
                    P.wait("vector", "tE", st["znbf_rd_tok"])
                P.emit("vector", lambda eng, tt=tt: eng.tensor_tensor(
                    zn3[:, tt], znsc[:, 0:D], normw_b[:], ALU.mult))
                st["sq_free"] = P.tok("vector", "vE")
            znbf_tok = st["sq_free"]

            P.wait("tensor", "vE", znbf_tok)
            for dt in range(DT):
                P.wait("tensor", "vE", st["ptr_ev_tok"])
                for tt in range(NT1):
                    P.emit("tensor", lambda eng, dt=dt, tt=tt: eng.transpose(
                        ptr_[:, tt * 128:(tt + 1) * 128],
                        znbf[:, tt * D + dt * 128: tt * D + dt * 128 + 128],
                        ident[:]))
                tr_tok = P.tok("tensor", "tE")
                if dt == DT - 1:
                    st["znbf_rd_tok"] = tr_tok
                P.wait("vector", "tE", tr_tok)
                if dt == 0:
                    P.wait("vector", "tE", st["znT_rd_tok"])
                P.emit("vector", lambda eng, dt=dt: eng.tensor_copy(
                    znT[:, dt * T1:(dt + 1) * T1], ptr_[:]))
                st["ptr_ev_tok"] = P.tok("vector", "vE")
            znT_tok = st["ptr_ev_tok"]
            if C == 0 and L == 0 and getattr(P, "debug", False):
                P.wait("gpsimd", "vE", znT_tok)
                P.dbg_dump("znT0", znT[:], (128, DT * T1), BF16)

            P.wait("tensor", "vE", znT_tok)
            P.wait("tensor", "vE", st["pr_ev_tok"])
            for tt in range(NT1):
                for dt in range(DT):
                    P.emit("tensor", lambda eng, tt=tt, dt=dt: eng.matmul(
                        pr[:, tt * 12:(tt + 1) * 12],
                        znT[:, dt * T1 + tt * 128: dt * T1 + (tt + 1) * 128],
                        wmt_sb[:, dt * 12:(dt + 1) * 12],
                        start=(dt == 0), stop=(dt == DT - 1),
                        skip_group_check=True))
            route_mm_tok = P.tok("tensor", "tE")
            P.wait("vector", "tE", route_mm_tok)
            P.emit("vector", lambda eng: eng.tensor_copy(rt[:], pr[:]))
            st["pr_ev_tok"] = P.tok("vector", "vE")
            route_tok = _emit_routing(P, B)
            st["route_tok"] = route_tok
            if C == 0 and L == 0 and getattr(P, "debug", False):
                P.wait("gpsimd", "vE", route_tok)
                P.dbg_dump("rt0", rt[:], (128, NT1 * 12), F32)
                P.dbg_dump("cC0", cC[:], (128, NT1 * 8), F32)

            for e in range(E):
                ei = st["e_seen"]
                st["e_seen"] += 1
                st["expert_map"][ei] = e
                hb = ei % 2
                pump_w2(ei + 1)
                for ft in range(NFT):
                    g = 2 * ei + ft // 8
                    pb = st["ph_i"] % 2
                    st["ph_i"] += 1
                    P.wait("tensor", "sE", st["silu_tok"].get(st["ph_i"] - 1 - 2))
                    if ft == 0 or ft == 8:
                        P.wait("tensor", "dW", st["w1_tok"][g])
                    for dt in range(DT):
                        P.emit("tensor",
                               lambda eng, pb=pb, gb=g % 2, ft=ft, dt=dt: eng.matmul(
                                   ph[pb][:],
                                   w1sb[gb][:, dt * 1024 + (ft % 8) * 128:
                                            dt * 1024 + (ft % 8) * 128 + 128],
                                   znT[:, dt * T1:(dt + 1) * T1],
                                   start=(dt == 0), stop=(dt == DT - 1),
                                   skip_group_check=True))
                    h_tok = P.tok("tensor", "tE")
                    if ft == 7 or ft == 15:
                        st["h_done"][g] = h_tok
                    if ft == NFT - 1:
                        st["znT_rd_tok"] = h_tok
                    P.wait("scalar", "tE", h_tok)
                    if ft == 0:
                        P.wait("scalar", "tE", st["y_done"].get(ei - 2))
                    P.emit("scalar", lambda eng, hb=hb, ft=ft, pb=pb: eng.activation(
                        hsT[hb][:, ft * T1:(ft + 1) * T1], ph[pb][:], AF.Silu))
                    st["silu_tok"][st["ph_i"] - 1] = P.tok("scalar", "sE")
                silu_done = st["silu_tok"][st["ph_i"] - 1]
                pump_w1(2 * (ei + 1) + 1)

                P.wait("tensor", "dW", st["w2_tok"][ei])
                base_ph = st["ph_i"] - NFT
                for tt in range(NT1):
                    pb = st["py_i"] % 2
                    st["py_i"] += 1
                    P.wait("tensor", "vE", st["stt_tok"].get(st["py_i"] - 1 - 2))
                    for ft in range(NFT):
                        if tt == 0:
                            P.wait("tensor", "sE", st["silu_tok"][base_ph + ft])
                        for h2 in range(2):
                            P.emit("tensor",
                                   lambda eng, pb=pb, hb=hb, tt=tt, ft=ft, h2=h2,
                                   eb=ei % 2: eng.matmul(
                                       py[pb][:, h2 * 512:(h2 + 1) * 512],
                                       hsT[hb][:, ft * T1 + tt * 128:
                                               ft * T1 + (tt + 1) * 128],
                                       w2sb[eb][:, ft * 1024 + h2 * 512:
                                                ft * 1024 + h2 * 512 + 512],
                                       start=(ft == 0), stop=(ft == NFT - 1),
                                       skip_group_check=True))
                    y_tok = P.tok("tensor", "tE")
                    if tt == NT1 - 1:
                        st["y_done"][ei] = y_tok
                    P.wait("vector", "tE", y_tok)
                    P.wait("vector", "vE",
                           st["stt_tok"].get(st["py_i"] - 1 - NT1))
                    if e == 0 and tt == 0:
                        P.wait("vector", "vE", st["route_tok"])
                    P.emit("vector",
                           lambda eng, tt=tt, pb=pb, e=e: eng.scalar_tensor_tensor(
                               z3[:, tt], py[pb][:],
                               cC[:, tt * 8 + e: tt * 8 + e + 1],
                               z3[:, tt], ALU.mult, ALU.add))
                    st["stt_tok"][st["py_i"] - 1] = P.tok("vector", "vE")
            st["z_upd_tok"] = st["stt_tok"][st["py_i"] - 1]
            if C == 0 and L == 0 and getattr(P, "debug", False):
                P.wait("gpsimd", "vE", st["z_upd_tok"])
                P.dbg_dump("zC_L0", zC[:], (128, NT1 * D), F32)
                P.dbg_dump("hsT_e7", hsT[(st["e_seen"] - 1) % 2][:],
                           (128, NFT * T1), BF16)

        P.wait("vector", "vE", st["z_upd_tok"])
        for tt in range(NT1):
            P.emit("vector", lambda eng, tt=tt: eng.scalar_tensor_tensor(
                x3[:, tt], z3[:, tt], 0.5, x3[:, tt], ALU.mult, ALU.add))
        out_tok = P.tok("vector", "vE")
        st["xst_free_tok"] = out_tok
        P.wait("gpsimd", "vE", out_tok)
        P.emit("gpsimd", lambda eng, d=dst4[C], s=xv: eng.dma_start(d, s))
        st["xst_store_tok"] = P.tok("gpsimd", "dG", 16)
        P.flush()


def _emit_routing(P, B):
    rt, cC = B["rt"], B["cC"]
    mm_, me, sc1 = B["r_max"], B["r_me"], B["r_sc1"]
    cnt, tmp, ev, od, lv = B["r_cnt"], B["r_tmp"], B["r_ev"], B["r_od"], B["r_lv"]
    xa, xb = B["r_xa"], B["r_xb"]
    rt3 = rt[:].rearrange("p (t c) -> p t c", c=12)
    me3 = me[:].rearrange("p (t c) -> p t c", c=4)
    ev3 = ev[:].rearrange("p (t c) -> p t c", c=4)
    od3 = od[:].rearrange("p (t c) -> p t c", c=4)
    cnt3 = cnt[:].rearrange("p (t c) -> p t c", c=4)
    tmp3 = tmp[:].rearrange("p (t c) -> p t c", c=4)
    lv3 = lv[:].rearrange("p (t c) -> p t c", c=4)
    cC3 = cC[:].rearrange("p (t c) -> p t c", c=8)

    def VS(fn):
        P.emit("vector", fn)
        P.selfw("vector")

    VS(lambda eng: eng.tensor_reduce(mm_[:], rt3[:, :, 0:4], AX.X, ALU.max))
    for tt in range(NT1):
        VS(lambda eng, tt=tt: eng.tensor_scalar(
            me3[:, tt], rt3[:, tt, 0:4], mm_[:, tt:tt + 1], None, ALU.subtract))
    t = P.cur("vE")
    P.wait("scalar", "vE", t)
    P.emit("scalar", lambda eng: eng.activation(xa[:], me[:], AF.Exp))
    t = P.tok("scalar", "sE")
    P.wait("vector", "sE", t)
    VS(lambda eng: eng.tensor_copy(me[:], xa[:]))
    VS(lambda eng: eng.tensor_reduce(sc1[:], me3, AX.X, ALU.add))
    VS(lambda eng: eng.reciprocal(sc1[:], sc1[:]))
    for tt in range(NT1):
        VS(lambda eng, tt=tt: eng.tensor_scalar(
            me3[:, tt], me3[:, tt], sc1[:, tt:tt + 1], None, ALU.mult))
    rank4 = B["r_rank"][:].rearrange("p (t g s) -> p t g s", g=4, s=8)
    P.emit("vector", lambda eng: eng.memset(B["r_rank"][:], 0.0))
    P.selfw("vector")
    for g in range(4):
        si = 0
        for gp in range(4):
            if gp == g:
                continue
            P.emit("vector", lambda eng, g=g, gp=gp, si=si: eng.tensor_tensor(
                rank4[:, :, g, si:si + 1], me3[:, :, gp:gp + 1],
                me3[:, :, g:g + 1], ALU.is_gt))
            si += 1
            if gp < g:
                P.emit("vector", lambda eng, g=g, gp=gp, si=si:
                       eng.tensor_tensor(
                           rank4[:, :, g, si:si + 1], me3[:, :, gp:gp + 1],
                           me3[:, :, g:g + 1], ALU.is_equal))
                si += 1
    P.selfw("vector")
    VS(lambda eng: eng.tensor_reduce(
        cnt[:], rank4.rearrange("p t g s -> p (t g) s"), AX.X, ALU.add))
    VS(lambda eng: eng.tensor_scalar(cnt[:], cnt[:], 2.0, None, ALU.is_lt))
    VS(lambda eng: eng.tensor_tensor(me[:], me[:], cnt[:], ALU.mult))
    VS(lambda eng: eng.tensor_reduce(sc1[:], me3, AX.X, ALU.add))
    VS(lambda eng: eng.tensor_scalar(sc1[:], sc1[:], EPS9, None, ALU.add))
    VS(lambda eng: eng.reciprocal(sc1[:], sc1[:]))
    for tt in range(NT1):
        VS(lambda eng, tt=tt: eng.tensor_scalar(
            me3[:, tt], me3[:, tt], sc1[:, tt:tt + 1], None, ALU.mult))
    if getattr(P, "debug", False) and not P.dbg_tensors.get("r_me_post"):
        P.dbg_dump("r_me_post", me[:], (128, NT1 * 4), F32)
        P.dbg_dump("r_cnt_post", cnt[:], (128, NT1 * 4), F32)
        P.dbg_dump("r_sc1_post", sc1[:], (128, NT1), F32)
    VS(lambda eng: eng.tensor_tensor(ev3, rt3[:, :, 4:12:2], rt3[:, :, 5:12:2],
                                     ALU.max))
    VS(lambda eng: eng.tensor_tensor(od3, rt3[:, :, 5:12:2], ev3, ALU.subtract))
    VS(lambda eng: eng.tensor_tensor(ev3, rt3[:, :, 4:12:2], ev3, ALU.subtract))
    t = P.cur("vE")
    P.wait("scalar", "vE", t)
    P.emit("scalar", lambda eng: eng.activation(xa[:], ev[:], AF.Exp))
    P.selfw("scalar")
    P.emit("scalar", lambda eng: eng.activation(xb[:], od[:], AF.Exp))
    t = P.tok("scalar", "sE")
    P.wait("vector", "sE", t)
    VS(lambda eng: eng.tensor_copy(ev[:], xa[:]))
    VS(lambda eng: eng.tensor_copy(od[:], xb[:]))
    VS(lambda eng: eng.tensor_tensor(tmp[:], ev[:], od[:], ALU.add))
    VS(lambda eng: eng.reciprocal(tmp[:], tmp[:]))
    VS(lambda eng: eng.tensor_tensor(ev[:], ev[:], tmp[:], ALU.mult))
    VS(lambda eng: eng.tensor_tensor(od[:], od[:], tmp[:], ALU.mult))
    VS(lambda eng: eng.tensor_tensor(lv[:], ev[:], od[:], ALU.max))
    VS(lambda eng: eng.tensor_tensor(tmp[:], ev[:], od[:], ALU.is_ge))
    VS(lambda eng: eng.tensor_scalar(ev[:], lv[:], EPS9, None, ALU.add))
    VS(lambda eng: eng.reciprocal(ev[:], ev[:]))
    VS(lambda eng: eng.tensor_tensor(lv[:], lv[:], ev[:], ALU.mult))
    if getattr(P, "debug", False) and not P.dbg_tensors.get("r_lv_post"):
        P.dbg_dump("r_lv_post", lv[:], (128, NT1 * 4), F32)
        P.dbg_dump("r_ind_post", tmp[:], (128, NT1 * 4), F32)
    VS(lambda eng: eng.tensor_tensor(lv[:], lv[:], me[:], ALU.mult))
    VS(lambda eng: eng.tensor_tensor(cC3[:, :, 0:8:2], lv3, tmp3, ALU.mult))
    VS(lambda eng: eng.tensor_scalar(tmp[:], tmp[:], -1.0, 1.0, ALU.mult, ALU.add))
    VS(lambda eng: eng.tensor_tensor(cC3[:, :, 1:8:2], lv3, tmp3, ALU.mult))
    return P.cur("vE")




TCA = 256
NTA = TCA // 128
CHA = T // TCA


def emit_attention(P, nc, A):
    x1v = A["x1s"].rearrange("(c tt p) d -> c p tt d", tt=NTA, p=128)
    x1t = A["x1s"].rearrange("(tt p) d -> tt p d", p=128)
    x2t = A["x2s"].rearrange("(tt p) d -> tt p d", p=128)
    qtr, ktr = A["qtr"], A["ktr"]
    ident, ptm, e2sb = A["ident"], A["ptm"], A["e2sb"]

    def barrier():
        for e_ in ENGS:
            for s_ in ("vE", "sE", "tE", "dG", "dW"):
                P.wait(e_, s_, P.cur(s_))

    with ExitStack() as SO:
        EO = SO.enter_context
        v65 = EO(nc.sbuf_tensor("v65", [128, 32 * 1040], BF16))

        with ExitStack() as SA:
            EA = SA.enter_context
            qn2 = EA(nc.sbuf_tensor("qn2", [16, T], F32))
            Bsb = EA(nc.sbuf_tensor("Bsb", [16, T], F32))
            ones16 = EA(nc.sbuf_tensor("ones16", [16, T], BF16))
            km2 = EA(nc.sbuf_tensor("km2", [2, 8], F32))
            kmx = EA(nc.sbuf_tensor("kmx", [16, 1], F32))
            kmx2 = EA(nc.sbuf_tensor("kmx2", [16, 1], F32))
            st2 = [EA(nc.sbuf_tensor("stg2_%d" % i, [2, 512], F32))
                   for i in range(2)]
            ast = EA(nc.sbuf_tensor("ast", [128, NTA * D], F32))
            asq = EA(nc.sbuf_tensor("asq", [128, NTA * D], F32))
            ybf = EA(nc.sbuf_tensor("ybf", [128, NTA * D], BF16))
            yT = EA(nc.sbuf_tensor("yT", [128, DT * TCA], BF16))
            amu = EA(nc.sbuf_tensor("amu", [128, NTA], F32))
            av4 = EA(nc.sbuf_tensor("av4", [128, NTA], F32))
            ars = EA(nc.sbuf_tensor("ars", [128, NTA], F32))
            lst = EA(nc.sbuf_tensor("lst", [128, LAT], F32))
            lsc = EA(nc.sbuf_tensor("lsc", [128, LAT], F32))
            lnbf = EA(nc.sbuf_tensor("lnbf", [128, LAT], BF16))
            lnT = EA(nc.sbuf_tensor("lnT", [128, 2 * TCA], BF16))
            lm1 = EA(nc.sbuf_tensor("lm1", [128, 1], F32))
            lv1 = EA(nc.sbuf_tensor("lv1", [128, 1], F32))
            lr1 = EA(nc.sbuf_tensor("lr1", [128, 1], F32))
            wq_sb = EA(nc.sbuf_tensor("wq_sb", [128, DT * D], BF16))
            wdkv_sb = EA(nc.sbuf_tensor("wdkv_sb", [128, DT * LAT], BF16))
            wuk_sb = EA(nc.sbuf_tensor("wuk_sb", [128, 2 * D], BF16))
            wuv_sb = EA(nc.sbuf_tensor("wuv_sb", [128, 2 * D], BF16))
            lng_b = EA(nc.sbuf_tensor("lng_b", [128, LAT], F32))
            lnb_b = EA(nc.sbuf_tensor("lnb_b", [128, LAT], F32))
            csq = EA(nc.sbuf_tensor("csq", [128, TCA], F32))
            snq = EA(nc.sbuf_tensor("snq", [128, TCA], F32))
            csk = EA(nc.sbuf_tensor("csk", [128, TCA], F32))
            snk = EA(nc.sbuf_tensor("snk", [128, TCA], F32))
            qtf = EA(nc.sbuf_tensor("qtf", [128, TCA], F32))
            qtb = EA(nc.sbuf_tensor("qtb", [128, TCA], BF16))
            rt1 = EA(nc.sbuf_tensor("rt1", [128, TCA], F32))
            rt2 = EA(nc.sbuf_tensor("rt2", [128, TCA], F32))
            qrf = EA(nc.sbuf_tensor("qrf", [128, TCA], F32))
            qrb = EA(nc.sbuf_tensor("qrb", [128, TCA], BF16))
            sqb = EA(nc.sbuf_tensor("sqb", [128, TCA], BF16))
            pT = EA(nc.psum_tensor("pTa", [128, 512], BF16))
            pL = EA(nc.psum_tensor("pLa", [128, LAT], F32))
            pQ = EA(nc.psum_tensor("pQa", [128, TCA], F32))
            pR = EA(nc.psum_tensor("pRa", [128, TCA], F32))
            pN = EA(nc.psum_tensor("pNa", [2, TCA], F32))
            pV = EA(nc.psum_tensor("pVa", [128, 1024], F32))

            P.begin()
            barrier()
            for dst_, srcm, nt_ in ((wq_sb, A["wq"], DT), (wdkv_sb, A["wdkv"], DT),
                                    (wuk_sb, A["wuk"], 2), (wuv_sb, A["wuv"], 2)):
                cc = dst_.shape[1] // nt_
                P.emit("gpsimd", lambda eng, d=dst_, s=srcm, nt=nt_, cc=cc:
                       eng.dma_start(
                           d[:].rearrange("p (t c) -> p t c", t=nt),
                           s.rearrange("(t p) c -> p t c", p=128)))
                P.tok("gpsimd", "dG", 16)
            P.emit("gpsimd", lambda eng: eng.dma_start(lng_b[:], A["lng"]))
            P.tok("gpsimd", "dG", 16)
            P.emit("gpsimd", lambda eng: eng.dma_start(lnb_b[:], A["lnb"]))
            wload_tok = P.tok("gpsimd", "dG", 16)
            P.emit("vector", lambda eng: eng.memset(v65[:], 1.0))
            v65_tok = P.tok("vector", "vE")
            P.emit("vector", lambda eng: eng.memset(km2[:], 0.0))
            P.selfw("vector")
            P.flush()

            tk = dict(ast_free=None, cs_rd=None, ln_done=None, ybf_rd=None,
                      yT_rd=None, lnT_rd=None, pT_ev=None, lat_ev=None,
                      lsc_rd=None, pQ_free=None, rot_rd=None, qtf_rd=None,
                      pN_ev=None, stg_ev=[None, None],
                      stg_free=[None, None], vout=v65_tok, sqb_rd=None)
            sti = [0]

            for C in range(CHA):
                P.begin()
                cix = slice(C * TCA, (C + 1) * TCA)
                P.wait("gpsimd", "sE", tk["ast_free"])
                P.emit("gpsimd", lambda eng, s=x1v[C],
                       d=ast[:].rearrange("p (tt d) -> p tt d", tt=NTA):
                       eng.dma_start(d, s))
                P.tok("gpsimd", "dG", 16)
                P.wait("gpsimd", "vE", tk["cs_rd"])
                for dst_, src_ in ((csq, A["cosq"]), (snq, A["sinq"]),
                                   (csk, A["cosk"]), (snk, A["sink"])):
                    P.emit("gpsimd", lambda eng, d=dst_, s=src_[:, cix]:
                           eng.dma_start(d[:], s))
                    P.tok("gpsimd", "dG", 16)
                ld_tok = P.cur("dG")

                x3 = ast[:].rearrange("p (tt d) -> p tt d", tt=NTA)
                a3 = asq[:].rearrange("p (tt d) -> p tt d", tt=NTA)
                P.wait("vector", "dG", ld_tok)
                P.emit("vector", lambda eng: eng.tensor_reduce(
                    amu[:], x3, AX.X, ALU.add))
                P.selfw("vector")
                P.emit("vector", lambda eng: eng.tensor_scalar(
                    amu[:], amu[:], 1.0 / D, None, ALU.mult))
                P.selfw("vector")
                P.wait("vector", "sE", tk["ln_done"])
                for tt in range(NTA):
                    P.emit("vector", lambda eng, tt=tt: eng.tensor_scalar(
                        a3[:, tt], x3[:, tt], amu[:, tt:tt + 1], None,
                        ALU.subtract))
                xc_tok = P.tok("vector", "vE")
                P.wait("scalar", "vE", xc_tok)
                P.emit("scalar", lambda eng: eng.activation(ast[:], asq[:],
                                                            AF.Square))
                sq_tok = P.tok("scalar", "sE")
                tk["ast_free"] = sq_tok
                P.wait("vector", "sE", sq_tok)
                P.emit("vector", lambda eng: eng.tensor_reduce(
                    av4[:], x3, AX.X, ALU.add))
                P.selfw("vector")
                P.emit("vector", lambda eng: eng.tensor_scalar(
                    av4[:], av4[:], 1.0 / D, LN_EPS, ALU.mult, ALU.add))
                t = P.tok("vector", "vE")
                P.wait("scalar", "vE", t)
                P.emit("scalar", lambda eng: eng.activation(ars[:], av4[:],
                                                            AF.Sqrt))
                t = P.tok("scalar", "sE")
                P.wait("vector", "sE", t)
                P.emit("vector", lambda eng: eng.reciprocal(ars[:], ars[:]))
                rs_tok = P.tok("vector", "vE")
                P.wait("scalar", "vE", rs_tok)
                P.wait("scalar", "tE", tk["ybf_rd"])
                for tt in range(NTA):
                    P.emit("scalar", lambda eng, tt=tt: eng.activation(
                        ybf[:, tt * D:(tt + 1) * D], a3[:, tt], AF.Copy,
                        scale=ars[:, tt:tt + 1]))
                ybf_tok = P.tok("scalar", "sE")
                tk["ln_done"] = ybf_tok

                P.wait("tensor", "sE", ybf_tok)
                for dt in range(DT):
                    P.wait("tensor", "vE", tk["pT_ev"])
                    for tt in range(NTA):
                        P.emit("tensor", lambda eng, dt=dt, tt=tt: eng.transpose(
                            pT[:, tt * 128:(tt + 1) * 128],
                            ybf[:, tt * D + dt * 128: tt * D + dt * 128 + 128],
                            ident[:]))
                    tr_tok = P.tok("tensor", "tE")
                    if dt == DT - 1:
                        tk["ybf_rd"] = tr_tok
                    P.wait("vector", "tE", tr_tok)
                    if dt == 0:
                        P.wait("vector", "tE", tk["yT_rd"])
                    P.emit("vector", lambda eng, dt=dt: eng.tensor_copy(
                        yT[:, dt * TCA:(dt + 1) * TCA], pT[:, 0:NTA * 128]))
                    tk["pT_ev"] = P.tok("vector", "vE")
                yT_tok = tk["pT_ev"]

                P.wait("tensor", "vE", yT_tok)
                P.wait("tensor", "dG", wload_tok)
                for tt in range(NTA):
                    P.wait("tensor", "vE", tk["lat_ev"])
                    for dt in range(DT):
                        P.emit("tensor", lambda eng, tt=tt, dt=dt: eng.matmul(
                            pL[:],
                            yT[:, dt * TCA + tt * 128: dt * TCA + (tt + 1) * 128],
                            wdkv_sb[:, dt * LAT:(dt + 1) * LAT],
                            start=(dt == 0), stop=(dt == DT - 1),
                            skip_group_check=True))
                    lm_tok = P.tok("tensor", "tE")
                    P.wait("vector", "tE", lm_tok)
                    P.wait("vector", "sE", tk["lsc_rd"])
                    P.emit("vector", lambda eng: eng.tensor_copy(lst[:], pL[:]))
                    tk["lat_ev"] = P.tok("vector", "vE")
                    P.wait("vector", "vE", tk["lat_ev"])
                    P.emit("vector", lambda eng: eng.tensor_reduce(
                        lm1[:], lst[:].rearrange("p (o l) -> p o l", o=1),
                        AX.X, ALU.add))
                    P.selfw("vector")
                    P.emit("vector", lambda eng: eng.tensor_scalar(
                        lm1[:], lm1[:], 1.0 / LAT, None, ALU.mult))
                    P.selfw("vector")
                    P.emit("vector", lambda eng: eng.tensor_scalar(
                        lsc[:], lst[:], lm1[:, 0:1], None, ALU.subtract))
                    t = P.tok("vector", "vE")
                    P.wait("scalar", "vE", t)
                    P.emit("scalar", lambda eng: eng.activation(lst[:], lsc[:],
                                                                AF.Square))
                    t = P.tok("scalar", "sE")
                    P.wait("vector", "sE", t)
                    P.emit("vector", lambda eng: eng.tensor_reduce(
                        lv1[:], lst[:].rearrange("p (o l) -> p o l", o=1),
                        AX.X, ALU.add))
                    P.selfw("vector")
                    P.emit("vector", lambda eng: eng.tensor_scalar(
                        lv1[:], lv1[:], 1.0 / LAT, LN_EPS, ALU.mult, ALU.add))
                    t = P.tok("vector", "vE")
                    P.wait("scalar", "vE", t)
                    P.emit("scalar", lambda eng: eng.activation(lr1[:], lv1[:],
                                                                AF.Sqrt))
                    t = P.tok("scalar", "sE")
                    P.wait("vector", "sE", t)
                    P.emit("vector", lambda eng: eng.reciprocal(lr1[:], lr1[:]))
                    t = P.tok("vector", "vE")
                    P.wait("scalar", "vE", t)
                    P.emit("scalar", lambda eng: eng.activation(
                        lst[:], lsc[:], AF.Copy, scale=lr1[:, 0:1]))
                    tk["lsc_rd"] = P.tok("scalar", "sE")
                    P.wait("vector", "sE", tk["lsc_rd"])
                    P.emit("vector", lambda eng: eng.tensor_tensor(
                        lsc[:], lst[:], lng_b[:], ALU.mult))
                    P.selfw("vector")
                    if tt == 0:
                        P.wait("vector", "tE", tk["lnT_rd"])
                    P.emit("vector", lambda eng: eng.tensor_tensor(
                        lnbf[:], lsc[:], lnb_b[:], ALU.add))
                    lnbf_tok = P.tok("vector", "vE")
                    P.wait("tensor", "vE", lnbf_tok)
                    P.wait("tensor", "vE", tk["pT_ev"])
                    for lt in range(2):
                        P.emit("tensor", lambda eng, lt=lt: eng.transpose(
                            pT[:, lt * 128:(lt + 1) * 128],
                            lnbf[:, lt * 128:(lt + 1) * 128], ident[:]))
                    t = P.tok("tensor", "tE")
                    P.wait("vector", "tE", t)
                    for lt in range(2):
                        P.emit("vector", lambda eng, lt=lt, tt=tt: eng.tensor_copy(
                            lnT[:, lt * TCA + tt * 128: lt * TCA + (tt + 1) * 128],
                            pT[:, lt * 128:(lt + 1) * 128]))
                    tk["pT_ev"] = P.tok("vector", "vE")
                lnT_tok = tk["pT_ev"]
                P.wait("tensor", "vE", lnT_tok)

                def qk_side(wsb, nacc, rhsT, cosb, sinb, scr, is_q, C=C):
                    for ot in range(8):
                        P.wait("tensor", "sE", tk["pQ_free"])
                        for at in range(nacc):
                            P.emit("tensor", lambda eng, ot=ot, at=at, w=wsb,
                                   r=rhsT: eng.matmul(
                                       pQ[:],
                                       w[:, at * D + ot * 128:
                                         at * D + ot * 128 + 128],
                                       r[:, at * TCA:(at + 1) * TCA],
                                       start=(at == 0), stop=(at == nacc - 1),
                                       skip_group_check=True))
                        mm_tok = P.tok("tensor", "tE")
                        P.wait("scalar", "tE", mm_tok)
                        P.wait("scalar", "vE", tk["qtf_rd"])
                        P.emit("scalar", lambda eng: eng.copy(qtf[:], pQ[:]))
                        ev_tok = P.tok("scalar", "sE")
                        tk["pQ_free"] = ev_tok
                        P.wait("vector", "sE", ev_tok)
                        P.emit("vector", lambda eng: eng.tensor_copy(qtb[:],
                                                                     qtf[:]))
                        cb_tok = P.tok("vector", "vE")
                        P.wait("tensor", "vE", cb_tok)
                        P.wait("tensor", "vE", tk["rot_rd"])
                        P.emit("tensor", lambda eng: eng.matmul(
                            pR[:], ptm[:], qtb[:],
                            start=True, stop=True, skip_group_check=True))
                        rot_tok = P.tok("tensor", "tE")
                        P.wait("vector", "tE", rot_tok)
                        P.emit("vector", lambda eng, cb=cosb: eng.tensor_tensor(
                            rt1[:], qtf[:], cb[:], ALU.mult))
                        P.selfw("vector")
                        P.emit("vector", lambda eng, sb=sinb: eng.tensor_tensor(
                            rt2[:], pR[:], sb[:], ALU.mult))
                        tk["rot_rd"] = P.tok("vector", "vE")
                        tk["qtf_rd"] = tk["rot_rd"]
                        P.wait("vector", "vE", tk["rot_rd"])
                        P.wait("vector", "sE", tk["sqb_rd"])
                        P.emit("vector", lambda eng: eng.tensor_tensor(
                            qrf[:], rt1[:], rt2[:], ALU.add))
                        P.selfw("vector")
                        P.emit("vector", lambda eng: eng.tensor_copy(qrb[:],
                                                                     qrf[:]))
                        qr_tok = P.tok("vector", "vE")
                        P.wait("scalar", "vE", qr_tok)
                        P.wait("scalar", "tE", tk.get("pN_mm"))
                        P.emit("scalar", lambda eng: eng.activation(
                            sqb[:], qrf[:], AF.Square))
                        sq2_tok = P.tok("scalar", "sE")
                        tk["sqb_rd"] = sq2_tok
                        P.wait("tensor", "sE", sq2_tok)
                        P.wait("tensor", "vE", tk["pN_ev"])
                        P.emit("tensor", lambda eng: eng.matmul(
                            pN[:], e2sb[:], sqb[:],
                            start=True, stop=True, skip_group_check=True))
                        n_tok = P.tok("tensor", "tE")
                        tk["pN_mm"] = n_tok
                        sb_ = sti[0] % 2
                        sti[0] += 1
                        P.wait("vector", "tE", n_tok)
                        if tk["stg_free"][sb_] is not None:
                            P.wait("vector", *tk["stg_free"][sb_])
                        P.emit("vector", lambda eng, sb=sb_: eng.tensor_copy(
                            st2[sb][:, 0:TCA], pN[:]))
                        tk["stg_ev"][sb_] = P.tok("vector", "vE")
                        tk["pN_ev"] = tk["stg_ev"][sb_]
                        P.wait("sync", "vE", qr_tok)
                        for hh in range(2):
                            P.emit("sync", lambda eng, ot=ot, s=scr, hh=hh:
                                   eng.dma_start(
                                       s[2 * ot + hh, 0:64,
                                         C * TCA:(C + 1) * TCA],
                                       qrb[hh * 64:(hh + 1) * 64, :]))
                            P.tok("sync", "dW", 16)
                        if is_q:
                            P.wait("gpsimd", "vE", tk["stg_ev"][sb_])
                            P.emit("gpsimd", lambda eng, sb=sb_, ot=ot:
                                   eng.dma_start(
                                       qn2[2 * ot:2 * ot + 2,
                                           C * TCA:(C + 1) * TCA],
                                       st2[sb][:, 0:TCA]))
                            tk["stg_free"][sb_] = ("dG",
                                                   P.tok("gpsimd", "dG", 16))
                        else:
                            P.wait("vector", "vE", tk["stg_ev"][sb_])
                            P.emit("vector", lambda eng, sb=sb_:
                                   eng.tensor_reduce(
                                       st2[sb][:, 300:301], st2[sb][:, 0:TCA],
                                       AX.X, ALU.max))
                            P.selfw("vector")
                            P.emit("vector", lambda eng, sb=sb_, ot=ot:
                                   eng.tensor_tensor(
                                       km2[:, ot:ot + 1], km2[:, ot:ot + 1],
                                       st2[sb][:, 300:301], ALU.max))
                            P.selfw("vector")
                            tk["stg_free"][sb_] = ("vE", P.cur("vE"))

                qk_side(wq_sb, DT, yT, csq, snq, qtr, True)
                qk_side(wuk_sb, 2, lnT, csk, snk, ktr, False)

                for tt in range(NTA):
                    P.wait("tensor", "vE", tk["vout"])
                    for h2 in range(2):
                        for lt in range(2):
                            P.emit("tensor", lambda eng, tt=tt, h2=h2, lt=lt:
                                   eng.matmul(
                                       pV[:, h2 * 512:(h2 + 1) * 512],
                                       lnT[:, lt * TCA + tt * 128:
                                           lt * TCA + (tt + 1) * 128],
                                       wuv_sb[:, lt * D + h2 * 512:
                                              lt * D + h2 * 512 + 512],
                                       start=(lt == 0), stop=(lt == 1),
                                       skip_group_check=True))
                    v_tok = P.tok("tensor", "tE")
                    P.wait("vector", "tE", v_tok)
                    tg = C * NTA + tt
                    for h in range(NH):
                        P.emit("vector", lambda eng, h=h, tg=tg: eng.tensor_copy(
                            v65[:, tg * 1040 + 65 * h: tg * 1040 + 65 * h + 64],
                            pV[:, h * 64:(h + 1) * 64]))
                    tk["vout"] = P.tok("vector", "vE")
                tk["yT_rd"] = P.cur("tE")
                tk["lnT_rd"] = P.cur("tE")
                tk["cs_rd"] = P.cur("vE")
                P.flush()

            P.begin()
            barrier()
            P.emit("gpsimd", lambda eng: eng.dma_start(A["kmxs"], km2[:]))
            t = P.tok("gpsimd", "dG", 16)
            P.wait("gpsimd", "dG", t)
            P.emit("gpsimd", lambda eng: eng.dma_start(
                kmx[:, 0:1], A["kmxs"].rearrange("c p -> (c p)")[:, None]))
            t = P.tok("gpsimd", "dG", 16)
            P.wait("scalar", "dG", t)
            P.emit("scalar", lambda eng: eng.activation(kmx2[:], kmx[:], AF.Sqrt))
            P.selfw("scalar")
            P.emit("scalar", lambda eng: eng.activation(Bsb[:], qn2[:], AF.Sqrt))
            b_tok = P.tok("scalar", "sE")
            P.wait("vector", "sE", b_tok)
            P.emit("vector", lambda eng: eng.tensor_scalar(
                Bsb[:], Bsb[:], kmx2[:, 0:1], -1.0, ALU.mult, ALU.mult))
            t = P.tok("vector", "vE")
            P.wait("gpsimd", "vE", t)
            P.emit("gpsimd", lambda eng: eng.dma_start(qtr[:, 64, :], Bsb[:]))
            P.tok("gpsimd", "dG", 16)
            P.emit("vector", lambda eng: eng.memset(ones16[:], 1.0))
            t = P.tok("vector", "vE")
            P.wait("sync", "vE", t)
            P.emit("sync", lambda eng: eng.dma_start(ktr[:, 64, :], ones16[:]))
            P.tok("sync", "dW", 16)
            P.flush()

        avT = EO(nc.sbuf_tensor("avT", [128, DT * T], BF16))
        with ExitStack() as SB:
            EB = SB.enter_context
            ktb = [EB(nc.sbuf_tensor("ktb%d" % i, [65, T], BF16))
                   for i in range(2)]
            qhb = [EB(nc.sbuf_tensor("qhb%d" % i, [65, T], BF16))
                   for i in range(2)]
            expt = [EB(nc.sbuf_tensor("expt%d" % i, [128, 512], BF16))
                    for i in range(4)]
            tms = EB(nc.sbuf_tensor("tms", [128, 4 * 512], BF16))
            recb = EB(nc.sbuf_tensor("recb", [64, 512], F32))
            rec1 = EB(nc.sbuf_tensor("rec1", [1, 512], F32))
            on64 = EB(nc.sbuf_tensor("on64", [1, 64], F32))
            pS = [EB(nc.psum_tensor("pS%d" % i, [128, 512], F32))
                  for i in range(3)]
            pAV = [EB(nc.psum_tensor("pAV%d" % i, [65, 512], F32))
                   for i in range(2)]
            pBC = EB(nc.psum_tensor("pBC", [64, 512], F32))

            P.begin()
            barrier()
            P.emit("gpsimd", lambda eng: eng.dma_start(
                tms[:].rearrange("p (m c) -> p m c", m=4),
                A["trimask"].rearrange("m p c -> p m c")))
            P.tok("gpsimd", "dG", 16)
            P.emit("gpsimd", lambda eng: eng.dma_start(on64[:], A["ones64"]))
            msk_tok = P.tok("gpsimd", "dG", 16)
            P.wait("vector", "dG", msk_tok)
            P.wait("tensor", "dG", msk_tok)
            P.flush()

            bt = dict(hload=[None, None], hdone=[None, None],
                      exp={}, mm2={}, expdone={}, pav_ev=[None, None],
                      bc_done=None, recb_ev=None, pend_epi=None)
            jg = [0]

            def flush_epi():
                if bt["pend_epi"] is None:
                    return
                h_, tqc_, pv_, av_tok = bt["pend_epi"]
                bt["pend_epi"] = None
                P.wait("vector", "tE", av_tok)
                P.wait("vector", "tE", bt["bc_done"])
                P.emit("vector", lambda eng, pv=pv_: eng.reciprocal(
                    rec1[:], pAV[pv][64:65, :]))
                r_tok = P.tok("vector", "vE")
                P.wait("tensor", "vE", r_tok)
                P.emit("tensor", lambda eng: eng.matmul(
                    pBC[:], on64[:], rec1[:],
                    start=True, stop=True, skip_group_check=True))
                bc_tok = P.tok("tensor", "tE")
                bt["bc_done"] = bc_tok
                P.wait("vector", "tE", bc_tok)
                P.wait("vector", "vE", bt["recb_ev"])
                P.emit("vector", lambda eng: eng.tensor_copy(recb[:], pBC[:]))
                P.selfw("vector")
                P.emit("vector", lambda eng, h=h_, tqc=tqc_, pv=pv_:
                       eng.tensor_tensor(
                           avT[(h % 2) * 64:(h % 2) * 64 + 64,
                               (h // 2) * T + tqc * 512:
                               (h // 2) * T + (tqc + 1) * 512],
                           pAV[pv][0:64, :], recb[:], ALU.mult))
                bt["pav_ev"][pv_] = P.tok("vector", "vE")
                bt["recb_ev"] = bt["pav_ev"][pv_]

            for h in range(NH):
                P.begin()
                hb = h % 2
                P.wait("sync", "tE", bt["hdone"][hb])
                P.emit("sync", lambda eng, h=h, hb=hb: eng.dma_start(
                    ktb[hb][:], ktr[h]))
                P.tok("sync", "dW", 16)
                P.emit("sync", lambda eng, h=h, hb=hb: eng.dma_start(
                    qhb[hb][:], qtr[h]))
                bt["hload"][hb] = P.tok("sync", "dW", 16)
                P.wait("tensor", "dW", bt["hload"][hb])
                for tqc in range(8):
                    pv = tqc % 2
                    P.wait("tensor", "vE", bt["pav_ev"][pv])
                    njt = 4 * tqc + 4

                    def mm2_emit(j, g, eb, pv=pv, h=h, njt=njt):
                        sem, val = bt["expdone"][g]
                        P.wait("tensor", sem, val)
                        P.emit("tensor", lambda eng, j=j, eb=eb, pv=pv, h=h,
                               njt=njt: eng.matmul(
                                   pAV[pv][0:65, :],
                                   v65[:, j * 1040 + 65 * h:
                                       j * 1040 + 65 * h + 65],
                                   expt[eb][:],
                                   start=(j == 0), stop=(j == njt - 1),
                                   skip_group_check=True))
                        bt["mm2"][g] = P.tok("tensor", "tE")

                    pend = None
                    for j in range(njt):
                        g = jg[0]
                        jg[0] += 1
                        ps = g % 3
                        eb = g % 4
                        ed = bt["expdone"].get(g - 3)
                        if ed is not None:
                            P.wait("tensor", *ed)
                        P.emit("tensor", lambda eng, hb=hb, j=j, tqc=tqc, ps=ps:
                               eng.matmul(
                                   pS[ps][:],
                                   ktb[hb][:, j * 128:(j + 1) * 128],
                                   qhb[hb][:, tqc * 512:(tqc + 1) * 512],
                                   start=True, stop=True,
                                   skip_group_check=True))
                        s_tok = P.tok("tensor", "tE")
                        P.wait("scalar", "tE", s_tok)
                        P.wait("scalar", "tE", bt["mm2"].get(g - 4))
                        P.emit("scalar", lambda eng, ps=ps, eb=eb:
                               eng.activation(expt[eb][:], pS[ps][:], AF.Exp))
                        e_tok = P.tok("scalar", "sE")
                        m = j - 4 * tqc
                        if m >= 0:
                            P.wait("vector", "sE", e_tok)
                            P.emit("vector", lambda eng, eb=eb, m=m:
                                   eng.tensor_tensor(
                                       expt[eb][:], expt[eb][:],
                                       tms[:, m * 512:(m + 1) * 512], ALU.mult))
                            bt["expdone"][g] = ("vE", P.tok("vector", "vE"))
                        else:
                            bt["expdone"][g] = ("sE", e_tok)
                        if j == 2:
                            flush_epi()
                        if pend is not None:
                            mm2_emit(*pend)
                        pend = (j, g, eb)
                    mm2_emit(*pend)
                    av_tok = bt["mm2"][jg[0] - 1]
                    bt["pend_epi"] = (h, tqc, pv, av_tok)
                bt["hdone"][hb] = P.cur("tE")
                P.flush()
            P.begin()
            flush_epi()
            P.flush()

        with ExitStack() as SC:
            ECp = SC.enter_context
            wo_sb = ECp(nc.sbuf_tensor("wo_sb", [128, DT * D], BF16))
            x1b = [ECp(nc.sbuf_tensor("x1b%d" % i, [128, D], F32))
                   for i in range(2)]
            x2st = [ECp(nc.sbuf_tensor("x2st%d" % i, [128, D], F32))
                    for i in range(2)]
            pF = [ECp(nc.psum_tensor("pF%d" % i, [128, 1024], F32))
                  for i in range(2)]
            P.begin()
            barrier()
            P.emit("gpsimd", lambda eng: eng.dma_start(
                wo_sb[:].rearrange("p (dt o) -> p dt o", dt=DT),
                A["wo"].rearrange("(dt p) o -> p dt o", p=128)))
            wo_tok = P.tok("gpsimd", "dG", 16)
            P.wait("tensor", "dG", wo_tok)
            ct = dict(out=[None, None], store=[None, None], xload=[None, None])
            for tt in range(32):
                b = tt % 2
                P.wait("gpsimd", "vE", ct["out"][b])
                P.emit("gpsimd", lambda eng, tt=tt, b=b: eng.dma_start(
                    x1b[b][:], x1t[tt]))
                ct["xload"][b] = P.tok("gpsimd", "dG", 16)
                P.wait("tensor", "vE", ct["out"][b])
                for dt in range(DT):
                    for h2 in range(2):
                        P.emit("tensor", lambda eng, tt=tt, dt=dt, h2=h2, b=b:
                               eng.matmul(
                                   pF[b][:, h2 * 512:(h2 + 1) * 512],
                                   avT[:, dt * T + tt * 128:
                                       dt * T + (tt + 1) * 128],
                                   wo_sb[:, dt * D + h2 * 512:
                                         dt * D + h2 * 512 + 512],
                                   start=(dt == 0), stop=(dt == DT - 1),
                                   skip_group_check=True))
                f_tok = P.tok("tensor", "tE")
                P.wait("vector", "tE", f_tok)
                P.wait("vector", "dG", ct["xload"][b])
                P.wait("vector", "dG", ct["store"][b])
                P.emit("vector", lambda eng, b=b: eng.tensor_tensor(
                    x2st[b][:], x1b[b][:], pF[b][:], ALU.add))
                ct["out"][b] = P.tok("vector", "vE")
                P.wait("gpsimd", "vE", ct["out"][b])
                P.emit("gpsimd", lambda eng, tt=tt, b=b: eng.dma_start(
                    x2t[tt], x2st[b][:]))
                ct["store"][b] = P.tok("gpsimd", "dG", 16)
            P.flush()



def build_program(upto="full", debug=False):
    nc = bass.Bass("TRN2", target_bir_lowering=False)
    dt_ = nc.dram_tensor
    x = dt_("x", [T, D], F32, kind="ExternalInput").ap()
    w1t = dt_("w1t", [E, D, F], F32, kind="ExternalInput").ap()
    w2t = dt_("w2t", [E, F, D], F32, kind="ExternalInput").ap()
    wq = dt_("wq", [D, D], F32, kind="ExternalInput").ap()
    wdkv = dt_("wdkv", [D, LAT], F32, kind="ExternalInput").ap()
    wuk = dt_("wuk", [LAT, D], F32, kind="ExternalInput").ap()
    wuv = dt_("wuv", [LAT, D], F32, kind="ExternalInput").ap()
    wo = dt_("wo", [D, D], F32, kind="ExternalInput").ap()
    lng = dt_("lng", [128, LAT], F32, kind="ExternalInput").ap()
    lnb = dt_("lnb", [128, LAT], F32, kind="ExternalInput").ap()
    normw = dt_("normw", [128, D], F32, kind="ExternalInput").ap()
    wmt = dt_("wmt", [D, 12], F32, kind="ExternalInput").ap()
    identity = dt_("identity", [128, 128], BF16, kind="ExternalInput").ap()
    ptmat = dt_("ptmat", [128, 128], BF16, kind="ExternalInput").ap()
    cosq = dt_("cosq", [128, T], F32, kind="ExternalInput").ap()
    sinq = dt_("sinq", [128, T], F32, kind="ExternalInput").ap()
    cosk = dt_("cosk", [128, T], F32, kind="ExternalInput").ap()
    sink = dt_("sink", [128, T], F32, kind="ExternalInput").ap()
    trimask = dt_("trimask", [4, 128, 512], BF16, kind="ExternalInput").ap()
    e2mat = dt_("e2mat", [128, 2], BF16, kind="ExternalInput").ap()
    ones64 = dt_("ones64", [1, 64], F32, kind="ExternalInput").ap()
    y = dt_("y", [T, D], F32, kind="ExternalOutput").ap()

    kmxs = dt_("kmxs", [8, 2], F32).ap()
    w1b = dt_("w1b", [E, D, F], BF16).ap()
    w2b = dt_("w2b", [E, F, D], BF16).ap()
    x1s = dt_("x1s", [T, D], F32).ap()
    x2s = dt_("x2s", [T, D], F32).ap()
    qtr = dt_("qtr", [NH, 65, T], BF16).ap()
    ktr = dt_("ktr", [NH, 65, T], BF16).ap()
    _ = (wq, wdkv, wuk, wuv, wo, lng, lnb, cosq, sinq, cosk, sink, trimask,
         ptmat, x2s, qtr, ktr)

    with ExitStack() as stack:
        EC = stack.enter_context
        P = Prog(nc, stack)
        P.debug = debug
        P.dbg_tensors = {}

        def dbg_dump(name, sb_ap, shape, dtype):
            if not P.debug or name in P.dbg_tensors:
                return
            t = nc.dram_tensor("dbg_" + name, list(shape), dtype,
                               kind="ExternalOutput").ap()
            P.dbg_tensors[name] = t
            for s_ in ("vE", "sE", "tE"):
                P.wait("gpsimd", s_, P.cur(s_))
            P.emit("gpsimd", lambda eng, d=t, s=sb_ap: eng.dma_start(d, s))
            tk = P.tok("gpsimd", "dG", 16)
            for e_ in ("vector", "scalar", "tensor", "sync"):
                P.wait(e_, "dG", tk)
        P.dbg_dump = dbg_dump

        ident = EC(nc.sbuf_tensor("identsb", [128, 128], BF16))
        ptm = EC(nc.sbuf_tensor("ptmsb", [128, 128], BF16))
        e2sb = EC(nc.sbuf_tensor("e2sb", [128, 2], BF16))
        normw_b = EC(nc.sbuf_tensor("normw_b", [128, D], F32))
        wmt_sb = EC(nc.sbuf_tensor("wmt_sb", [128, DT * 12], BF16))

        P.begin()
        P.emit("gpsimd", lambda eng: eng.dma_start(ident[:], identity))
        P.tok("gpsimd", "dG", 16)
        P.emit("gpsimd", lambda eng: eng.dma_start(ptm[:], ptmat))
        P.tok("gpsimd", "dG", 16)
        P.emit("gpsimd", lambda eng: eng.dma_start(e2sb[:], e2mat))
        P.tok("gpsimd", "dG", 16)
        P.emit("gpsimd", lambda eng: eng.dma_start(normw_b[:], normw))
        P.tok("gpsimd", "dG", 16)
        P.emit("gpsimd", lambda eng: eng.dma_start(
            wmt_sb[:].rearrange("p (dt c) -> p dt c", c=12),
            wmt.rearrange("(dt p) c -> p dt c", p=128)))
        P.tok("gpsimd", "dG", 16)
        P.flush()

        def alloc_ffn(EC, sfx=""):
            nm = lambda s: s + sfx
            return dict(
            xst=EC(nc.sbuf_tensor(nm("xst"), [128, NT1 * D], F32)),
            zC=EC(nc.sbuf_tensor(nm("zC"), [128, NT1 * D], F32)),
            sqst=EC(nc.sbuf_tensor(nm("sqst"), [128, NT1 * D], F32)),
            znbf=EC(nc.sbuf_tensor(nm("znbf"), [128, NT1 * D], BF16)),
            znT=EC(nc.sbuf_tensor(nm("znT"), [128, DT * T1], BF16)),
            hsT=[EC(nc.sbuf_tensor(nm("hsT0"), [128, NFT * T1], BF16)),
                 EC(nc.sbuf_tensor(nm("hsT1"), [128, NFT * T1], BF16))],
            w1sb=[EC(nc.sbuf_tensor(nm("w1sb0"), [128, DT * 1024], BF16)),
                  EC(nc.sbuf_tensor(nm("w1sb1"), [128, DT * 1024], BF16))],
            w2sb=[EC(nc.sbuf_tensor(nm("w2sb0"), [128, NFT * 1024], BF16)),
                  EC(nc.sbuf_tensor(nm("w2sb1"), [128, NFT * 1024], BF16))],
            mu4=EC(nc.sbuf_tensor(nm("mu4"), [128, NT1], F32)),
            v4=EC(nc.sbuf_tensor(nm("v4"), [128, NT1], F32)),
            rstd4=EC(nc.sbuf_tensor(nm("rstd4"), [128, NT1], F32)),
            rt=EC(nc.sbuf_tensor(nm("rt"), [128, NT1 * 12], F32)),
            cC=EC(nc.sbuf_tensor(nm("cCb"), [128, NT1 * 8], F32)),
            r_max=EC(nc.sbuf_tensor(nm("r_max"), [128, NT1], F32)),
            r_me=EC(nc.sbuf_tensor(nm("r_me"), [128, NT1 * 4], F32)),
            r_sc1=EC(nc.sbuf_tensor(nm("r_sc1"), [128, NT1], F32)),
            r_cnt=EC(nc.sbuf_tensor(nm("r_cnt"), [128, NT1 * 4], F32)),
            r_tmp=EC(nc.sbuf_tensor(nm("r_tmp"), [128, NT1 * 4], F32)),
            r_ev=EC(nc.sbuf_tensor(nm("r_ev"), [128, NT1 * 4], F32)),
            r_od=EC(nc.sbuf_tensor(nm("r_od"), [128, NT1 * 4], F32)),
            r_lv=EC(nc.sbuf_tensor(nm("r_lv"), [128, NT1 * 4], F32)),
            r_xa=EC(nc.sbuf_tensor(nm("r_xa"), [128, NT1 * 4], F32)),
            r_xb=EC(nc.sbuf_tensor(nm("r_xb"), [128, NT1 * 4], F32)),
            r_rank=EC(nc.sbuf_tensor(nm("r_rank"), [128, NT1 * 32], F32)),
            ph=[EC(nc.psum_tensor(nm("ph0"), [128, T1], F32)),
                EC(nc.psum_tensor(nm("ph1"), [128, T1], F32))],
            py=[EC(nc.psum_tensor(nm("py0"), [128, 1024], F32)),
                EC(nc.psum_tensor(nm("py1"), [128, 1024], F32))],
            ptr=EC(nc.psum_tensor(nm("ptrb"), [128, NT1 * 128], BF16)),
            pr=EC(nc.psum_tensor(nm("prb"), [128, NT1 * 12], F32)),
            normw_b=normw_b, wmt_sb=wmt_sb, ident=ident,
            w1b4=[w1b[e].rearrange("(dt p) f -> p dt f", p=128) for e in range(E)],
            w2b4=[w2b[e].rearrange("(ft p) d -> p ft d", p=128) for e in range(E)],
            state=dict(
                g_w1=0, g_w2=0, e_seen=0, ph_i=0, py_i=0,
                next_w1=0, next_w2=0,
                w1_tok={}, w2_tok={}, h_done={}, y_done={},
                silu_tok={}, stt_tok={}, expert_map={},
                znbf_rd_tok=None, znT_rd_tok=None,
                ptr_ev_tok=None, pr_ev_tok=None,
                zC_sc_tok=None, z_upd_tok=None, sq_free=None, route_tok=None,
                xst_store_tok=None, xst_free_tok=None,
                first_sync_wait=None, xst_guard_dW=None,
            ),
        )

        ffn1_stack = ExitStack()
        ffnbufs = alloc_ffn(ffn1_stack.enter_context)
        st = ffnbufs["state"]
        st["expert_of"] = lambda ei: ei % E

        P.begin()
        cbuf = ffnbufs["w2sb"]
        i = 0
        store_tok = {}
        for (srcm, dstm, R, Cc) in ((w1t, w1b, D, F), (w2t, w2b, F, D)):
            s4 = srcm.rearrange("e (rt p) c -> e p rt c", p=128)
            d4 = dstm.rearrange("e (rt p) c -> e p rt c", p=128)
            nrt = R // 128
            for e in range(E):
                b = i % 2
                P.wait("gpsimd", "dW", store_tok.get(i - 2))
                sb3 = cbuf[b][:, 0:nrt * Cc].rearrange("p (rt c) -> p rt c", c=Cc)
                P.emit("gpsimd", lambda eng, s=s4[e], d=sb3: eng.dma_start(d, s))
                ld = P.tok("gpsimd", "dG", 16)
                P.wait("sync", "dG", ld)
                P.emit("sync", lambda eng, s=sb3, d=d4[e]: eng.dma_start(d, s))
                store_tok[i] = P.tok("sync", "dW", 16)
                i += 1
        cast_done_dW = P.cur("dW")
        P.flush()
        st["first_sync_wait"] = cast_done_dW
        st["xst_guard_dW"] = cast_done_dW


        dst1 = y if upto == "ffn1" else x1s
        emit_ffn_sublayer(P, ffnbufs, x, dst1)
        ffn1_stack.close()
        if upto != "ffn1":
            x2dst = y if upto == "attn" else x2s
            A = dict(x1s=x1s, x2s=x2dst, qtr=qtr, ktr=ktr, kmxs=kmxs,
                     wq=wq, wdkv=wdkv, wuk=wuk, wuv=wuv, wo=wo,
                     lng=lng, lnb=lnb, cosq=cosq, sinq=sinq, cosk=cosk,
                     sink=sink, trimask=trimask, ones64=ones64,
                     ident=ident, ptm=ptm, e2sb=e2sb)
            emit_attention(P, nc, A)
            if upto == "attn":
                P.begin()
                P.wait("gpsimd", "dG", P.cur("dG"))
                P.wait("sync", "dW", P.cur("dW"))
                P.flush()
                return nc
            P.begin()
            for e_ in ENGS:
                for s_ in ("vE", "sE", "tE", "dG", "dW"):
                    P.wait(e_, s_, P.cur(s_))
            P.flush()
            ffn2_stack = ExitStack()
            fb2 = alloc_ffn(ffn2_stack.enter_context, sfx="_f2")
            stf2 = fb2["state"]
            stf2["expert_of"] = lambda ei: ei % E
            emit_ffn_sublayer(P, fb2, x2s, y)
            ffn2_stack.close()

        P.begin()
        P.wait("gpsimd", "dG", P.cur("dG"))
        P.wait("sync", "dW", P.cur("dW"))
        P.flush()
    return nc



def _host_constants():
    c = {}
    c["identity"] = np.eye(128, dtype=ml_dtypes.bfloat16)
    Pm = np.zeros((128, 128), dtype=np.float32)
    for b in (0, 64):
        for i in range(32):
            Pm[b + i, b + 32 + i] = -1.0
            Pm[b + 32 + i, b + i] = 1.0
    c["ptmat"] = Pm.T.astype(ml_dtypes.bfloat16)
    pos = np.arange(T, dtype=np.float64)
    inv = 1.0 / (10000.0 ** (np.arange(32, dtype=np.float64) / 32.0))
    ang = pos[None, :] * inv[:, None]
    cos128 = np.tile(np.cos(ang), (4, 1)).astype(np.float32)
    sin128 = np.tile(np.sin(ang), (4, 1)).astype(np.float32)
    c["cosk"], c["sink"] = cos128, sin128
    c["cosq"] = (cos128 * 0.125).astype(np.float32)
    c["sinq"] = (sin128 * 0.125).astype(np.float32)
    e2 = np.zeros((128, 2), dtype=np.float32)
    e2[0:64, 0] = 1.0
    e2[64:128, 1] = 1.0
    c["e2mat"] = e2.astype(ml_dtypes.bfloat16)
    c["ones64"] = np.ones((1, 64), dtype=np.float32)
    tm = np.zeros((4, 128, 512), dtype=np.float32)
    r = np.arange(128)[:, None]
    cidx = np.arange(512)[None, :]
    for m in range(4):
        tm[m] = np.where(128 * m + r <= cidx, 1.0, 0.0)
    c["trimask"] = tm.astype(ml_dtypes.bfloat16)
    return c


def kernel(L0_attn_x, L0_attn_W_q, L0_attn_W_dkv, L0_attn_W_uk, L0_attn_W_uv,
           L0_attn_W_o, L0_attn_ln_gamma, L0_attn_ln_beta, L0_ffn_norm_w,
           L0_ffn_w_macro, L0_ffn_w_micro, L0_ffn_w1, L0_ffn_w2, _upto="full",
           _debug=False):
    xs = np.ascontiguousarray(np.asarray(L0_attn_x, dtype=np.float32))
    w1t = np.ascontiguousarray(np.asarray(L0_ffn_w1, np.float32).transpose(0, 2, 1))
    w2t = np.ascontiguousarray(np.asarray(L0_ffn_w2, np.float32).transpose(0, 2, 1))
    wmt = np.ascontiguousarray(np.concatenate(
        [np.asarray(L0_ffn_w_macro, np.float32).T,
         np.asarray(L0_ffn_w_micro, np.float32).reshape(8, D).T], axis=1))
    consts = _host_constants()
    shared = dict(
        w1t=w1t, w2t=w2t,
        wq=np.ascontiguousarray(np.asarray(L0_attn_W_q, np.float32)),
        wdkv=np.ascontiguousarray(np.asarray(L0_attn_W_dkv, np.float32)),
        wuk=np.ascontiguousarray(np.asarray(L0_attn_W_uk, np.float32)),
        wuv=np.ascontiguousarray(np.asarray(L0_attn_W_uv, np.float32)),
        wo=np.ascontiguousarray(np.asarray(L0_attn_W_o, np.float32)),
        lng=np.tile(np.asarray(L0_attn_ln_gamma, np.float32).reshape(1, LAT),
                    (128, 1)),
        lnb=np.tile(np.asarray(L0_attn_ln_beta, np.float32).reshape(1, LAT),
                    (128, 1)),
        normw=np.tile(np.asarray(L0_ffn_norm_w, np.float32).reshape(1, D),
                      (128, 1)),
        wmt=wmt, **consts)
    nc = build_program(upto=_upto, debug=_debug)
    in_maps = [dict(shared, x=np.ascontiguousarray(xs[i])) for i in range(8)]
    res = run_bass_kernel_spmd(nc, in_maps, core_ids=list(range(8)))
    out = np.stack([np.asarray(r["y"], np.float32) for r in res.results], axis=0)
    if _debug:
        return out, res.results
    return out
